# version 24
# baseline (speedup 1.0000x reference)
import numpy as np
import ml_dtypes
from contextlib import ExitStack

import concourse.bass as bass
import concourse.mybir as mybir
from concourse.bass_utils import run_bass_kernel_spmd

F32 = mybir.dt.float32
BF16 = mybir.dt.bfloat16
FP8 = mybir.dt.float8e4
AX = mybir.AxisListType
OP = mybir.AluOpType
ACT = mybir.ActivationFunctionType

D = 1024
H = 16
BS = 64
TILE = 256
SCALE = 0.125
EPS = 1e-5
CLIP_HALF = 224.0
ENGINES = ("sync", "gpsimd", "tensor", "vector", "scalar")


class Bld:
    def __init__(self):
        self.ops = {e: [] for e in ENGINES}
        self.cnt = {}
        self.est = {e: 0.0 for e in ENGINES}
        self.G = {}
        self.accs = []
        self.sem_log = {}
        self.waits_of = {}

    def check_banks(self):
        order = {e: i for i, e in enumerate(ENGINES)}
        n = {e: len(self.ops[e]) for e in ENGINES}
        clocks = {e: [None] * n[e] for e in ENGINES}
        sat = {}
        def provider(s, v):
            log = self.sem_log.get(s, [])
            lo, hi = 0, len(log)
            while lo < hi:
                mid = (lo + hi) // 2
                if log[mid][0] >= v:
                    hi = mid
                else:
                    lo = mid + 1
            if lo == len(log):
                raise RuntimeError(f"wait {s}>={v} never satisfied")
            return log[lo][1], log[lo][2]
        ptr = {e: 0 for e in ENGINES}
        cur = {e: [0] * len(ENGINES) for e in ENGINES}
        total = sum(n.values())
        done = 0
        while done < total:
            progressed = False
            for e in ENGINES:
                while ptr[e] < n[e]:
                    i = ptr[e]
                    fn, ws, inc = self.ops[e][i]
                    ok = True
                    c = list(cur[e])
                    for (s, v) in ws:
                        pe, pi = provider(s, v)
                        if clocks[pe][pi] is None or (
                                pe == e and pi >= i):
                            ok = False
                            break
                        pc = clocks[pe][pi]
                        for k in range(len(ENGINES)):
                            c[k] = max(c[k], pc[k])
                        c[order[pe]] = max(c[order[pe]], pi + 1)
                    if not ok:
                        break
                    clocks[e][i] = tuple(c)
                    c[order[e]] = i + 1
                    cur[e] = c
                    ptr[e] += 1
                    done += 1
                    progressed = True
            if not progressed:
                stuck = {e: (ptr[e], self.ops[e][ptr[e]][1]
                             if ptr[e] < n[e] else None) for e in ENGINES}
                raise RuntimeError(f"deadlock in program graph: {stuck}")
        bad = []
        by_bank = {}
        for (e, i, bank, rw) in self.accs:
            by_bank.setdefault(bank, []).append((e, i, rw))
        def unordered(e1, i1, e2, i2):
            if e1 == e2:
                return False
            c1 = clocks[e1][i1]
            c2 = clocks[e2][i2]
            return not (i1 < c2[order[e1]] or i2 < c1[order[e2]])

        for bank, lst in by_bank.items():
            tw = [(e, i) for (e, i, rw) in lst if e == "tensor" and rw == "w"]
            vs = [(e, i) for (e, i, rw) in lst if e in ("vector", "scalar")]
            for (te, ti) in tw:
                for (ve, vi) in vs:
                    if unordered(te, ti, ve, vi):
                        bad.append(("T-vs-VS", bank, ti, ve, vi))
            vv = [(e, i) for (e, i, rw) in lst if e == "vector"]
            ss = [(e, i) for (e, i, rw) in lst if e == "scalar"]
            for (e1, i1) in vv:
                for (e2, i2) in ss:
                    if unordered(e1, i1, e2, i2):
                        bad.append(("V-vs-S", bank, i1, e2, i2))
        return bad

    def sem(self, name):
        self.cnt.setdefault(name, 0)
        return name

    def op(self, eng, fn, wait=(), inc=None, est=0.0, acc=()):
        ws = []
        for w in wait:
            if w is None:
                continue
            if isinstance(w, list):
                ws.extend(p for p in w if p is not None and p[1] > 0)
            elif w[0] is not None and w[1] > 0:
                ws.append(w)
        self.ops[eng].append((fn, tuple(ws), inc))
        self.est[eng] += est
        idx = len(self.ops[eng]) - 1
        for a in acc:
            self.accs.append((eng, idx, a[0], a[1]))
        if inc is not None:
            self.cnt[inc[0]] += inc[1]
            self.sem_log.setdefault(inc[0], []).append(
                (self.cnt[inc[0]], eng, idx))
            return self.cnt[inc[0]]
        return None

    def v(self, s):
        return self.cnt[s]

    def rec(self, key, *pairs):
        self.G[key] = [p for p in pairs if p is not None]

    def g(self, key):
        return self.G.get(key, [])


def build_nc(NTOK, gi0, gi1):
    NB = NTOK // BS
    NT = NTOK // TILE
    assert NTOK % TILE == 0 and 0 <= gi0 < NB and 0 <= gi1 < NB

    nc = bass.Bass("TRN2", target_bir_lowering=False, debug=False)
    b = Bld()
    for s in ("dS", "dG", "wl0", "wl1", "xl0", "xl1", "st0", "st1", "T", "V", "S", "P", "tA0", "tA1"):
        b.sem(s)

    x_d = nc.dram_tensor("x", [NTOK, D], F32, kind="ExternalInput").ap()
    w_d = {w: nc.dram_tensor(w, [D, D], F32, kind="ExternalInput").ap()
           for w in ("wq", "wk", "wv", "wo", "wf")}
    b_d = {n: nc.dram_tensor(n, [1, D], F32, kind="ExternalInput").ap()
           for n in ("bq", "bk", "bv", "bo", "bf")}
    sin_d = nc.dram_tensor("sin", [1, 1], F32, kind="ExternalInput").ap()
    sw_d = nc.dram_tensor("sw", [1, 1], F32, kind="ExternalInput").ap()
    out_d = nc.dram_tensor("out", [NTOK, D], F32, kind="ExternalOutput").ap()

    ident_dram = nc.inline_tensor(np.eye(128, dtype=ml_dtypes.bfloat16), "ident_c")
    identf_dram = nc.inline_tensor(np.eye(128, dtype=np.float32), "identf_c")
    consts_dram = nc.inline_tensor(
        np.tile(np.array([[-0.5, SCALE, 0.5, 1e-5]], np.float32), (128, 1)),
        "consts_c")
    _sel = np.zeros((4, 2, 128), dtype=ml_dtypes.bfloat16)
    _sel[0, 0, :] = 1
    _sel[1, 1, :] = 1
    _sel[2, :, 0:64] = 1
    _sel[3, :, 64:128] = 1
    sel_dram = nc.inline_tensor(_sel, "sel_c")
    _cbig = np.zeros((2, 2, 3, 128), dtype=ml_dtypes.bfloat16)
    _cbig[0, :, 0, 64:128] = -30000.0
    _cbig[1, :, 1, 0:64] = -30000.0
    cbig_dram = nc.inline_tensor(_cbig, "cbig_c")

    with ExitStack() as ctx:
        E = ctx.enter_context
        sb = lambda n, sh, dt=F32: E(nc.sbuf_tensor(n, sh, dt))

        Wf = {w: sb(f"W_{w}", [128, 8, D], BF16) for w in ("wq", "wk", "wv", "wo")}
        Wf8 = sb("W_f8", [128, 8, D], FP8)
        bqf = sb("bqf", [128, 8])
        bkf = sb("bkf", [128, 8])
        bq_row = sb("bq_row", [1, D])
        bk_row = sb("bk_row", [1, D])
        bvrow = sb("bvrow", [1, D], BF16)
        borow = sb("borow", [1, D], BF16)
        bfrow_f = sb("bfrow_f", [1, D])
        bfrow = sb("bfrow", [1, D], BF16)
        ones_b = sb("ones_b", [1, 128], BF16)
        ones_f = sb("ones_f", [1, 128])
        ident = sb("ident", [128, 128], BF16)
        identf = sb("identf", [128, 128])
        CONSTS = sb("CONSTS", [128, 4])
        nc.const_aps.register(F32, -0.5, CONSTS[:, 0:1])
        nc.const_aps.register(F32, SCALE, CONSTS[:, 1:2])
        nc.const_aps.register(F32, 0.5, CONSTS[:, 2:3])
        nc.const_aps.register(F32, EPS, CONSTS[:, 3:4])
        s11 = sb("s11", [1, 4])
        sc_bc = sb("sc_bc", [128, 3])

        Kr = [sb(f"Kr{j}", [128, 8, 384], BF16) for j in range(3)]
        Vr = [sb(f"Vr{j}", [128, 2, H, 65], BF16) for j in range(3)]
        VEU = [sb(f"VEU{j}", [128, H, 65], BF16) for j in range(3)]
        K0f = sb("K0f", [128, 8, 64], BF16)
        K127f = sb("K127f", [128, 8, 64], BF16)
        V0x = sb("V0x", [128, H, 65], BF16)
        V127x = sb("V127x", [128, H, 65], BF16)
        gkf = sb("gkf", [128, 8, 128], BF16)
        gvx = sb("gvx", [128, H, 65], BF16)
        qgf = sb("qgf", [128, 8, 128], BF16)
        racc = sb("racc", [128, H, 65])
        mrun = sb("mrun", [128, 1], BF16)
        mnb = sb("mnb", [128, 1], BF16)
        RS8 = sb("RS8", [128, 16])

        xbuf = [sb(f"xbuf{j}", [128, 2, D]) for j in range(2)]
        qf = [sb(f"qf{j}", [128, 8, TILE], BF16) for j in range(2)]
        xfm_t = sb("xfm", [128, 8, TILE], BF16)
        xfm = [xfm_t, xfm_t]
        ln1o2 = sb("ln1o2", [128, 2, D], BF16)
        SMA = sb("SMA", [128, 8])
        SMW = sb("SMW", [128, 4])
        nfm2 = sb("nfm2", [4, 2, 3, 128], BF16)
        SEL = sb("SEL", [4, 2, 128], BF16)
        RLD = sb("RLD", [128, 16, 2])
        CTXlc = sb("CTXlc", [128, 16, 2, 65], BF16)
        PT = sb("PT", [128, 2, 2, 3, 128], BF16)
        CTXt_t = sb("CTXt", [128, H, 64], BF16)
        CTXt = [CTXt_t, CTXt_t]
        ctxf_t = sb("ctxf", [128, 8, 128], BF16)
        ctxf = [ctxf_t, ctxf_t]
        xp_t = sb("xp", [128, D])
        xp = [xp_t, xp_t]
        lnat_t = sb("lnat", [128, D])
        lnat = [lnat_t, lnat_t]
        ln2s_t = sb("ln2s", [128, D])
        ln2s = [ln2s_t, ln2s_t]
        xq8_t = sb("xq8", [128, 8, 128], FP8)
        xq8 = [xq8_t, xq8_t]
        SQ_t = sb("SQ", [128, D])
        SQ = [SQ_t, SQ_t]
        SM = [sb(f"SM{j}", [128, 8]) for j in range(2)]
        SM2 = [sb(f"SM2{j}", [128, 8]) for j in range(2)]
        RS = sb("RS", [128, 6])
        RL = sb("RL", [128, 4])
        RL16 = sb("RL16", [128, 16])

        P_big = E(nc.psum_tensor("P_big", [128, 2, 512], F32))
        P_tr = E(nc.psum_tensor("P_tr", [128, 8, 128], BF16))
        P_aux = E(nc.psum_tensor("P_aux", [128, 4, 128], F32))
        PS = [E(nc.psum_tensor(f"PS{j}", [128, 2, 4, 128], F32))
              for j in range(2)]

        g, rec = b.g, b.rec
        _pbase = {"P_big": 0, "P_tr": 2, "P_aux": 3, "PS0": 4, "PS1": 6}

        def pb(ap):
            try:
                t = ap.tensor
            except Exception:
                return []
            if t.name not in _pbase:
                return []
            es = mybir.dt.size(ap.dtype)
            start = ap.offset * es
            span = 0
            for (st, cnt) in ap.ap[1:]:
                span += st * es * (cnt - 1)
            span += es
            b0 = _pbase[t.name] + start // 2048
            b1 = _pbase[t.name] + (start + span - 1) // 2048
            return list(range(b0, b1 + 1))

        def accr(*aps):
            out = []
            for a in aps:
                out.extend((bb_, "r") for bb_ in pb(a))
            return out

        def accw(*aps):
            out = []
            for a in aps:
                out.extend((bb_, "w") for bb_ in pb(a))
            return out

        def vop(fn, wait=(), inc=None, est=250.0, acc=()):
            return b.op("vector", fn, wait=wait, inc=inc, est=est, acc=acc)

        def sop(fn, wait=(), inc=None, est=250.0, acc=()):
            return b.op("scalar", fn, wait=wait, inc=inc, est=est, acc=acc)

        def top(fn, wait=(), inc=None, est=100.0, acc=()):
            return b.op("tensor", fn, wait=wait, inc=inc, est=est, acc=acc)

        def vdrain(inc=None):
            return vop(lambda e: e.drain(), inc=inc, est=30.0)

        def sdrain(inc=None):
            return sop(lambda e: e.drain(), inc=inc, est=30.0)

        def tmark():
            return top(lambda e: e.nop(), inc=("T", 1), est=5.0)

        def dma(q, sem, out_ap, in_ap, wait=()):
            return b.op(q, lambda e, o=out_ap, i=in_ap: e.dma_start(out=o, in_=i),
                        wait=wait, inc=(sem, 16))

        def evac(fnv, fns, wait=(), est=500.0, acc=()):
            if fnv is None:
                eng = "scalar"
            elif fns is None:
                eng = "vector"
            else:
                eng = "vector" if b.est["vector"] <= b.est["scalar"] else "scalar"
            if eng == "vector":
                vop(fnv, wait=wait, est=est, acc=acc)
                return ("V", vdrain(inc=("V", 1)))
            sop(fns, wait=wait, est=est, acc=acc)
            return ("S", sdrain(inc=("S", 1)))

        def evac_copy(out_ap, in_ap, wait=(), est=500.0):
            return evac(lambda e, o=out_ap, i=in_ap: e.tensor_copy(o, i),
                        lambda e, o=out_ap, i=in_ap: e.copy(o, i),
                        wait=wait, est=est, acc=accr(in_ap))

        def evac_bias(out_ap, in_ap, bias_ap, wait=(), est=500.0):
            return evac(lambda e, o=out_ap, i=in_ap, bb=bias_ap:
                        e.tensor_scalar(o, i, bb, None, op0=OP.add),
                        lambda e, o=out_ap, i=in_ap, bb=bias_ap:
                        e.activation(o, i, ACT.Identity, bias=bb),
                        wait=wait, est=est, acc=accr(in_ap))

        def transpose(out_ap, in_ap, idt, wait=(), inc=None):
            return top(lambda e, o=out_ap, i=in_ap, d=idt: e.transpose(o, i, d),
                       wait=wait, inc=inc, est=120, acc=accw(out_ap))

        def mm(out_ap, lhsT_ap, rhs_ap, start, stop, wait=(), inc=None, est=120,
               acc=None, skip=False):
            return top(lambda e, o=out_ap, l=lhsT_ap, r=rhs_ap, st=start,
                       sp=stop, sk=skip:
                       e.matmul(o, l, r, start=st, stop=sp,
                                skip_group_check=sk),
                       wait=wait, inc=inc, est=est,
                       acc=(accw(out_ap) if acc is None else acc))

        dma("sync", "dS", ident[:], ident_dram.ap())
        dma("sync", "dS", identf[:], identf_dram.ap())
        dma("sync", "dS", CONSTS[:], consts_dram.ap())
        dma("sync", "dS", s11[0:1, 0:1], sin_d)
        dma("sync", "dS", s11[0:1, 1:2], sw_d)
        dma("sync", "dS", bq_row[:], b_d["bq"])
        dma("sync", "dS", bk_row[:], b_d["bk"])
        dma("sync", "dS", bfrow_f[:], b_d["bf"])
        dma("sync", "dS", SEL[:], sel_dram.ap())
        dma("sync", "dS", nfm2[2:4, :, :, :], cbig_dram.ap())
        xg = xbuf[NT % 2][:, 0, :]
        gath = xbuf[0]
        for j, blk in enumerate((0, NB - 1, gi0, gi1)):
            dma("sync", "dS", gath[(j % 2) * 64:(j % 2) * 64 + 64, j // 2, :],
                x_d[blk * 64:(blk + 1) * 64, :])
        pro_dS = b.v("dS")
        dma("gpsimd", "dG", bvrow[:], b_d["bv"])
        dma("gpsimd", "dG", borow[:], b_d["bo"])
        pro_dG = b.v("dG")

        vop(lambda e: e.memset(ones_b[:], 1.0))

        vop(lambda e: e.memset(ones_f[:], 1.0))
        vop(lambda e: e.memset(PT[:], 0.0))
        vop(lambda e: e.memset(racc[:], 0.0))
        vop(lambda e: e.memset(mrun[:], -1e30))
        for j in range(3):
            vop(lambda e, jj=j: e.memset(Vr[jj][:, :, :, 64:65], 1.0))
        for t in (V0x, V127x, gvx):
            vop(lambda e, tt=t: e.memset(tt[:, :, 64:65], 1.0))
        vop(lambda e: e.tensor_scalar(s11[0:1, 2:3], s11[0:1, 0:1], s11[0:1, 1:2],
                                      4.0, op0=OP.mult, op1=OP.mult),
            wait=[("dS", pro_dS)])
        vdrain()
        vop(lambda e: e.reciprocal(s11[0:1, 3:4], s11[0:1, 2:3]))
        vop(lambda e: e.reciprocal(s11[0:1, 0:1], s11[0:1, 0:1]))
        vdrain()
        vop(lambda e: e.tensor_scalar(bfrow[:], bfrow_f[:], s11[0:1, 3:4], None,
                                      op0=OP.mult), wait=[("dG", pro_dG)])
        v_pro = vdrain(inc=("V", 1))
        mm(P_aux[:, 0, 0:1], ones_f[:], s11[0:1, 2:3], True, True,
           wait=[("V", v_pro)])
        t_bc = mm(P_aux[:, 1, 0:1], ones_f[:], s11[0:1, 0:1], True, True,
                  inc=("T", 1))
        vop(lambda e: e.tensor_copy(sc_bc[:, 0:1], P_aux[:, 0, 0:1]),
            wait=[("T", t_bc)], acc=accr(P_aux[:, 0, 0:1]))
        vop(lambda e: e.tensor_scalar(sc_bc[:, 2:3], P_aux[:, 1, 0:1], 0.5, None,
                                      op0=OP.mult), acc=accr(P_aux[:, 1, 0:1]))
        vdrain()
        vop(lambda e: e.tensor_tensor(sc_bc[:, 1:2], sc_bc[:, 0:1], sc_bc[:, 0:1],
                                      op=OP.mult))
        v_scbc = vdrain(inc=("V", 1))
        rec("p_aux", ("V", v_scbc))
        for row, dstf in ((bq_row, bqf), (bk_row, bkf)):
            t_bb = None
            for j in range(8):
                t_bb = transpose(P_aux[:, 2, j:j + 1],
                                 row[0:1, j * 128:(j + 1) * 128],
                                 ones_f[0:1, 0:1],
                                 wait=([("dS", pro_dS), ("V", v_scbc),
                                        g("p_aux")] if j == 0 else []),
                                 inc=(("T", 1) if j == 7 else None))
            ev_bb = evac_copy(dstf[:], P_aux[:, 2, 0:8], wait=[("T", t_bb)],
                              est=300)
            rec("p_aux", ev_bb)

        for w in ("wq", "wk", "wv", "wo", "wf"):
            for oc in range(8):
                buf = SQ_t if oc % 2 == 0 else lnat_t
                dv = dma("sync", f"wl{oc % 2}", buf[:],
                         w_d[w][oc * 128:(oc + 1) * 128, :],
                         wait=[g(("wbuf", oc % 2))])
                for half in range(2):
                    tv = None
                    for kc in range(4 * half, 4 * half + 4):
                        tv = transpose(
                            P_big[:, half, (kc % 4) * 128:(kc % 4) * 128 + 128],
                            buf[:, kc * 128:(kc + 1) * 128], identf[:],
                            wait=([(f"wl{oc % 2}", dv), g(("wps", half))]
                                  if kc % 4 == 0 else []),
                            inc=(("T", 1) if kc % 4 == 3 else None))
                    src = P_big[:, half, :].rearrange("p (a c) -> p a c", a=4)
                    if w == "wf":
                        ev = evac(None, lambda e, hh=half, occ=oc, s=src:
                                  e.activation(
                                      Wf8[:, 4 * hh:4 * hh + 4,
                                          occ * 128:(occ + 1) * 128],
                                      s, ACT.Identity, scale=0.5),
                                  wait=[("T", tv)], est=600, acc=accr(src))
                    else:
                        ev = evac(lambda e, ww=w, hh=half, occ=oc, s=src:
                                  e.tensor_copy(
                                      Wf[ww][:, 4 * hh:4 * hh + 4,
                                             occ * 128:(occ + 1) * 128], s),
                                  lambda e, ww=w, hh=half, occ=oc, s=src:
                                  e.activation(
                                      Wf[ww][:, 4 * hh:4 * hh + 4,
                                             occ * 128:(occ + 1) * 128],
                                      s, ACT.Identity),
                                  wait=[("T", tv)], est=600, acc=accr(src))
                    rec(("wps", half), ev)
                rec(("wbuf", oc % 2), ("T", b.v("T")))
        rec(("pbig", 0), g(("wps", 0))[0])
        rec(("pbig", 1), g(("wps", 1))[0])

        def ln_full(x_ap, sm, out_ap, sq_scr, wait_pre=(), scale_extra=None,
                    sfin_fold=False):
            wp = list(wait_pre)
            parts = x_ap if isinstance(x_ap, list) else None
            if parts is None:
                vop(lambda e, xa=x_ap, s=sm: e.reduce_sum(
                    s[:, 0:1], xa, axis=AX.X, negate=True),
                    wait=wp, est=(140 + D) / 0.96)
                vdrain()
                vop(lambda e, s=sm: e.tensor_scalar(
                    s[:, 0:1], s[:, 0:1], 1.0 / D, None, op0=OP.mult), est=140)
                v_mean = vdrain(inc=("V", 1))
                sop(lambda e, xa=x_ap, s=sm, q=sq_scr: e.activation(
                    q[:], xa, ACT.Square, accum_out=s[:, 1:2]),
                    wait=wp, est=(200 + D) / 1.2)
                s_sq = sdrain(inc=("S", 1))
            else:
                vop(lambda e, xa=parts[0][0], s=sm: e.reduce_sum(
                    s[:, 0:1], xa, axis=AX.X, negate=True),
                    wait=wp, est=(140 + D / 2) / 0.96, acc=accr(parts[0][0]))
                v_r0 = vdrain(inc=("V", 1))
                sop(lambda e, xa=parts[1][0], s=sm, q=sq_scr: e.activation(
                    q[:, D // 2:D], xa, ACT.Square, accum_out=s[:, 7:8]),
                    wait=wp, est=(200 + D / 2) / 1.2, acc=accr(parts[1][0]))
                s_sq2 = sdrain(inc=("S", 1))
                vop(lambda e, xa=parts[1][0], s=sm: e.reduce_sum(
                    s[:, 6:7], xa, axis=AX.X, negate=True),
                    wait=[("S", s_sq2)], est=(140 + D / 2) / 0.96,
                    acc=accr(parts[1][0]))
                vdrain()
                vop(lambda e, s=sm: e.tensor_tensor(
                    s[:, 0:1], s[:, 0:1], s[:, 6:7], op=OP.add), est=140)
                vdrain()
                vop(lambda e, s=sm: e.tensor_scalar(
                    s[:, 0:1], s[:, 0:1], 1.0 / D, None, op0=OP.mult), est=140)
                v_mean = vdrain(inc=("V", 1))
                sop(lambda e, xa=parts[0][0], s=sm, q=sq_scr: e.activation(
                    q[:, 0:D // 2], xa, ACT.Square, accum_out=s[:, 1:2]),
                    wait=[("V", v_r0)], est=(200 + D / 2) / 1.2,
                    acc=accr(parts[0][0]))
                s_sq1 = sdrain(inc=("S", 1))
                vop(lambda e, s=sm: e.tensor_tensor(
                    s[:, 1:2], s[:, 1:2], s[:, 7:8], op=OP.add),
                    wait=[("S", s_sq1)], est=140)
                vdrain()
                s_sq = sdrain(inc=("S", 1))
            vop(lambda e, s=sm: e.tensor_tensor(s[:, 4:5], s[:, 0:1], s[:, 0:1],
                                                op=OP.mult),
                wait=[("V", v_mean), ("S", s_sq)], est=140)
            vdrain()
            vdrain()
            vop(lambda e, s=sm: e.scalar_tensor_tensor(
                s[:, 5:6], s[:, 1:2], 1.0 / D, s[:, 4:5],
                op0=OP.mult, op1=OP.subtract), est=140)
            vdrain()
            if sfin_fold:
                vop(lambda e, s=sm: e.tensor_tensor(s[:, 5:6], s[:, 5:6],
                                                    sc_bc[:, 1:2], op=OP.mult),
                    est=140)
                vdrain()
            vop(lambda e, s=sm: e.tensor_scalar(s[:, 5:6], s[:, 5:6], EPS, None,
                                                op0=OP.add), est=140)
            v_var = vdrain(inc=("V", 1))
            sop(lambda e, s=sm: e.activation(s[:, 4:5], s[:, 5:6], ACT.Ln),
                wait=[("V", v_var)], est=350)
            sdrain()
            sop(lambda e, s=sm: e.activation(s[:, 2:3], s[:, 4:5], ACT.Exp,
                                             scale=-0.5), est=350)
            s_r = sdrain(inc=("S", 1))
            v_w = [("S", s_r)]
            if scale_extra is not None:
                vop(lambda e, s=sm, x2=scale_extra: e.tensor_tensor(
                    s[:, 2:3], s[:, 2:3], x2, op=OP.mult), wait=v_w, est=140)
                vdrain()
                v_w = []
            if sfin_fold:
                vop(lambda e, s=sm: e.tensor_tensor(s[:, 2:3], s[:, 2:3],
                                                    sc_bc[:, 0:1], op=OP.mult),
                    wait=v_w, est=140)
                vdrain()
                v_w = []
            vop(lambda e, s=sm: e.tensor_tensor(s[:, 3:4], s[:, 0:1], s[:, 2:3],
                                                op=OP.mult), wait=v_w, est=140)
            v_n = vdrain(inc=("V", 1))
            if parts is None:
                sop(lambda e, xa=x_ap, s=sm, o=out_ap: e.activation(
                    o, xa, ACT.Identity, scale=s[:, 2:3], bias=s[:, 3:4]),
                    wait=[("V", v_n)], est=(200 + D) / 1.2)
            else:
                sop(lambda e, xa=parts[0][0], s=sm, o=parts[0][1]: e.activation(
                    o, xa, ACT.Identity, scale=s[:, 2:3], bias=s[:, 3:4]),
                    wait=[("V", v_n)], est=(200 + D / 2) / 1.2,
                    acc=accr(parts[0][0]))
                sop(lambda e, xa=parts[1][0], s=sm, o=parts[1][1]: e.activation(
                    o, xa, ACT.Identity, scale=s[:, 2:3], bias=s[:, 3:4]),
                    est=(200 + D / 2) / 1.2, acc=accr(parts[1][0]))
            s_a = sdrain(inc=("S", 1))
            return ("S", s_a)

        def ln_psum(sm, out_ap, wait_pre=(), sfin_fold=False):
            wp = list(wait_pre)
            v1 = vop(lambda e, s=sm: e.reduce_sum(
                s[:, 0:2], P_big[:, 0:2, :], axis=AX.X, negate=True),
                wait=wp, inc=("V", 1), est=(140 + D) / 0.96,
                acc=accr(P_big[:, 0, :], P_big[:, 1, :]))
            sop(lambda e, s=sm: e.activation(
                SQ_t[:, 0:512], P_big[:, 0, :], ACT.Square,
                accum_out=s[:, 2:3]),
                wait=[("V", v1)], est=(200 + 512) / 1.2,
                acc=accr(P_big[:, 0, :]))
            sdrain()
            s_sq = sop(lambda e, s=sm: e.activation(
                SQ_t[:, 512:1024], P_big[:, 1, :], ACT.Square,
                accum_out=s[:, 3:4]),
                inc=("S", 1), est=(200 + 512) / 1.2,
                acc=accr(P_big[:, 1, :]))
            vdrain()
            vop(lambda e, s=sm: e.tensor_tensor(s[:, 0:1], s[:, 0:1],
                                                s[:, 1:2], op=OP.add), est=140)
            vdrain()
            vop(lambda e, s=sm: e.tensor_scalar(s[:, 0:1], s[:, 0:1], 1.0 / D,
                                                None, op0=OP.mult), est=140)
            vdrain()
            vop(lambda e, s=sm: e.tensor_tensor(s[:, 4:5], s[:, 0:1],
                                                s[:, 0:1], op=OP.mult), est=140)
            vop(lambda e, s=sm: e.tensor_tensor(s[:, 2:3], s[:, 2:3],
                                                s[:, 3:4], op=OP.add),
                wait=[("S", s_sq)], est=140)
            vdrain()
            vop(lambda e, s=sm: e.scalar_tensor_tensor(
                s[:, 5:6], s[:, 2:3], 1.0 / D, s[:, 4:5],
                op0=OP.mult, op1=OP.subtract), est=140)
            if sfin_fold:
                vdrain()
                vop(lambda e, s=sm: e.tensor_tensor(s[:, 5:6], s[:, 5:6],
                                                    sc_bc[:, 1:2], op=OP.mult),
                    est=140)
            v_var = vdrain(inc=("V", 1))
            sop(lambda e, s=sm: e.activation(s[:, 4:5], s[:, 5:6], ACT.Ln,
                                             bias=EPS),
                wait=[("V", v_var)], est=350)
            sdrain()
            s_r = sop(lambda e, s=sm: e.activation(s[:, 6:7], s[:, 4:5],
                                                   ACT.Exp, scale=-0.5),
                      inc=("S", 1), est=350)
            if sfin_fold:
                vop(lambda e, s=sm: e.tensor_tensor(s[:, 6:7], s[:, 6:7],
                                                    sc_bc[:, 0:1], op=OP.mult),
                    wait=[("S", s_r)], est=140)
                vdrain()
            v_b = vop(lambda e, s=sm: e.tensor_tensor(s[:, 0:1], s[:, 0:1],
                                                      s[:, 6:7], op=OP.mult),
                      wait=([] if sfin_fold else [("S", s_r)]),
                      inc=("V", 1), est=140)
            s_a = sop(lambda e, s=sm, o=out_ap: e.activation(
                o.rearrange("p (a c) -> p a c", a=2), P_big[:, 0:2, :],
                ACT.Identity, scale=s[:, 6:7], bias=s[:, 0:1]),
                wait=[("V", v_b)], inc=("S", 1), est=(200 + D) / 1.2,
                acc=accr(P_big[:, 0, :], P_big[:, 1, :]))
            return ("S", s_a)

        def ln_sbuf(x_ap, sm, out_ap, wait_pre=(), scale_extra=None,
                    have_sum=False):
            wp = list(wait_pre)
            if not have_sum:
                vop(lambda e, xa=x_ap, s=sm: e.reduce_sum(
                    s[:, 0:1], xa, axis=AX.X, negate=True),
                    wait=wp, est=(140 + D) / 0.96)
            s_sq = sop(lambda e, xa=x_ap, s=sm: e.activation(
                SQ_t[:], xa, ACT.Square, accum_out=s[:, 2:3]),
                wait=wp, inc=("S", 1), est=(200 + D) / 1.2)
            if not have_sum:
                vdrain()
            sgn = -1.0 if have_sum else 1.0
            vop(lambda e, s=sm, sg=sgn: e.tensor_scalar(
                s[:, 0:1], s[:, 0:1], sg / D, None, op0=OP.mult),
                wait=(wp if have_sum else []), est=140)
            vdrain()
            vop(lambda e, s=sm: e.tensor_tensor(s[:, 4:5], s[:, 0:1],
                                                s[:, 0:1], op=OP.mult), est=140)
            vdrain()
            vop(lambda e, s=sm: e.scalar_tensor_tensor(
                s[:, 5:6], s[:, 2:3], 1.0 / D, s[:, 4:5],
                op0=OP.mult, op1=OP.subtract),
                wait=[("S", s_sq)], est=140)
            v_var = vdrain(inc=("V", 1))
            sop(lambda e, s=sm: e.activation(s[:, 4:5], s[:, 5:6], ACT.Ln,
                                             bias=EPS),
                wait=[("V", v_var)], est=350)
            sdrain()
            s_r = sop(lambda e, s=sm: e.activation(s[:, 6:7], s[:, 4:5],
                                                   ACT.Exp, scale=-0.5),
                      inc=("S", 1), est=350)
            v_w = [("S", s_r)]
            if scale_extra is not None:
                vop(lambda e, s=sm, x2=scale_extra: e.tensor_tensor(
                    s[:, 6:7], s[:, 6:7], x2, op=OP.mult), wait=v_w, est=140)
                vdrain()
                v_w = []
            v_b = vop(lambda e, s=sm: e.tensor_tensor(s[:, 0:1], s[:, 0:1],
                                                      s[:, 6:7], op=OP.mult),
                      wait=v_w, inc=("V", 1), est=140)
            s_a = sop(lambda e, xa=x_ap, s=sm, o=out_ap: e.activation(
                o, xa, ACT.Identity, scale=s[:, 6:7], bias=s[:, 0:1]),
                wait=[("V", v_b)], inc=("S", 1), est=(200 + D) / 1.2)
            return ("S", s_a)

        def proj_fm(w, out_region, oc, rhs_fn, wl):
            t = None
            for kc in range(8):
                t = mm(out_region, Wf[w][:, kc, oc * 128:(oc + 1) * 128],
                       rhs_fn(kc), kc == 0, kc == 7,
                       wait=(wl if kc == 0 else []),
                       inc=(("T", 1) if kc == 7 else None), est=115)
            return t

        pro_fm = xfm[0]
        for gc in range(2):
            s_a = ln_full(gath[:, gc, :], SM[gc], ln1o2[:, gc, :], SQ[gc],
                          wait_pre=[("dS", pro_dS)])
            t_t = None
            for j in range(8):
                t_t = transpose(P_tr[:, j, :], ln1o2[:, gc, j * 128:(j + 1) * 128],
                                ident[:], wait=([s_a, g("ptr")] if j == 0 else []),
                                inc=(("T", 1) if j == 7 else None))
            ev = evac_copy(pro_fm[:, :, gc * 128:(gc + 1) * 128], P_tr[:],
                           wait=[("T", t_t)], est=680)
            rec("ptr", ev)
            rec(("profm", gc), ev)
        for oc in range(8):
            bank = oc % 2
            wl = [g(("pbig", bank)), g(("profm", 0)), g(("profm", 1))]
            t_mm = proj_fm("wk", P_big[:, bank, 0:256], oc,
                           lambda kc, f=pro_fm: f[:, kc, 0:256], wl)
            ev1 = evac_bias(K0f[:, oc, :], P_big[:, bank, 0:64],
                            bkf[:, oc:oc + 1], wait=[("T", t_mm)], est=300)
            ev2 = evac_bias(K127f[:, oc, :], P_big[:, bank, 64:128],
                            bkf[:, oc:oc + 1], wait=[("T", t_mm)], est=300)
            ev3 = evac_bias(gkf[:, oc, :], P_big[:, bank, 128:256],
                            bkf[:, oc:oc + 1], wait=[("T", t_mm)], est=300)
            rec(("pbig", bank), ev1, ev2, ev3)
        for oc in range(8):
            bank = oc % 2
            t_mm = proj_fm("wq", P_big[:, bank, 0:128], oc,
                           lambda kc, f=pro_fm: f[:, kc, 128:256],
                           [g(("pbig", bank))])
            ev = evac_bias(qgf[:, oc, :], P_big[:, bank, 0:128],
                           bqf[:, oc:oc + 1], wait=[("T", t_mm)], est=300)
            rec(("pbig", bank), ev)
        for gc in range(2):
            for half in range(2):
                bank = half
                wl = [g(("pbig", bank))]
                for kc in range(8):
                    mm(P_big[:, bank, :], pro_fm[:, kc, gc * 128:(gc + 1) * 128],
                       Wf["wv"][:, kc, half * 512:(half + 1) * 512],
                       kc == 0, False, wait=(wl if kc == 0 else []), est=230)
                t_mm = mm(P_big[:, bank, :], ones_b[:],
                          bvrow[:, half * 512:(half + 1) * 512], False, True,
                          inc=("T", 1), est=230)
                src = P_big[:, bank, :].rearrange("p (h v) -> p h v", v=64)
                if gc == 0:
                    ev1 = evac_copy(V0x[0:64, half * 8:(half + 1) * 8, 0:64],
                                    src[0:64], wait=[("T", t_mm)], est=300)
                    ev2 = evac_copy(V127x[64:128, half * 8:(half + 1) * 8, 0:64],
                                    src[64:128], wait=[("T", t_mm)], est=300)
                    rec(("pbig", bank), ev1, ev2)
                else:
                    ev = evac_copy(gvx[:, half * 8:(half + 1) * 8, 0:64], src[:],
                                   wait=[("T", t_mm)], est=600)
                    rec(("pbig", bank), ev)
        pro_done = [("T", b.v("T")), ("V", b.v("V")), ("S", b.v("S"))]
        rec(("pro",), *pro_done)
        rec(("pt", 0), ("V", b.v("V")))
        rec(("pt", 1), ("V", b.v("V")))
        rec(("xfree_pro",), ("T", b.v("T")), ("S", b.v("S")), ("V", b.v("V")))
        rec(("ln1o_free",), ("T", b.v("T")))
        rec(("paux_free",), *pro_done)
        rec(("xfm_free",), ("T", b.v("T")))
        for ch in range(2):
            rec(("sq_free", ch), ("S", b.v("S")), ("T", b.v("T")))
            rec(("lnat_free", ch), ("T", b.v("T")))
            rec(("xp_free", ch), ("T", b.v("T")), ("S", b.v("S")))

        def load_x(i):
            if i >= NT:
                return
            slot = i % 2
            w = []
            if i < 2:
                if slot == 0:
                    w.append(g(("xfree_pro",)))
            else:
                w.append(g(("xfree", slot)))
            dv = dma("sync", f"xl{slot}", xbuf[slot][:],
                     x_d[i * TILE:(i + 1) * TILE, :].rearrange(
                         "(c p) d -> p c d", p=128), wait=w)
            rec(("xload", i), (f"xl{slot}", dv))

        def stage_A_gen(i):
            par, slot = i % 2, i % 3
            wl = [g(("xload", i)), g(("sma_free",))]
            if i < 2:
                wl.append(g(("pro",)))
            vop(lambda e: e.reduce_sum(SMA[:, 0:2], xbuf[par][:, 0:2, :],
                                       axis=AX.X, negate=True),
                wait=wl, est=(140 + 2 * D) / 0.96)
            sop(lambda e: e.activation(SQ_t[:], xbuf[par][:, 0, :], ACT.Square,
                                       accum_out=SMA[:, 2:3]),
                wait=wl, est=(200 + D) / 1.2)
            sdrain()
            s_sq = sop(lambda e: e.activation(SQ_t[:], xbuf[par][:, 1, :],
                                              ACT.Square, accum_out=SMA[:, 3:4]),
                       inc=("S", 1), est=(200 + D) / 1.2)
            vdrain()
            vop(lambda e: e.tensor_scalar(SMA[:, 0:2], SMA[:, 0:2], 1.0 / D,
                                          None, op0=OP.mult), est=140)
            vdrain()
            vop(lambda e: e.tensor_tensor(SMA[:, 4:6], SMA[:, 0:2], SMA[:, 0:2],
                                          op=OP.mult), est=140)
            vdrain()
            v_var = vop(lambda e: e.scalar_tensor_tensor(
                SMA[:, 6:8], SMA[:, 2:4], 1.0 / D, SMA[:, 4:6],
                op0=OP.mult, op1=OP.subtract),
                wait=[("S", s_sq)], inc=("V", 1), est=140)
            sop(lambda e: e.activation(SMA[:, 4:6], SMA[:, 6:8], ACT.Ln,
                                       bias=EPS), wait=[("V", v_var)], est=350)
            sdrain()
            s_r = sop(lambda e: e.activation(SMA[:, 6:8], SMA[:, 4:6], ACT.Exp,
                                             scale=-0.5), inc=("S", 1), est=350)
            v_b = vop(lambda e: e.tensor_tensor(SMA[:, 0:2], SMA[:, 0:2],
                                                SMA[:, 6:8], op=OP.mult),
                      wait=[("S", s_r)], inc=("V", 1), est=140)
            s_ap = []
            for ch in range(2):
                s_ap.append(sop(
                    lambda e, cc=ch: e.activation(
                        ln1o2[:, cc, :], xbuf[par][:, cc, :], ACT.Identity,
                        scale=SMA[:, 6 + cc:7 + cc], bias=SMA[:, cc:cc + 1]),
                    wait=([("V", v_b), g(("ln1o_free",))] if ch == 0 else []),
                    inc=("S", 1), est=(200 + D) / 1.2))
            rec(("sma_free",), ("S", s_ap[1]), ("V", v_b))
            yield
            for ch in range(2):
                t_t = None
                for j in range(8):
                    t_t = transpose(P_tr[:, j, :],
                                    ln1o2[:, ch, j * 128:(j + 1) * 128],
                                    ident[:],
                                    wait=([("S", s_ap[ch]), g("ptr")]
                                          if j == 0 else []),
                                    inc=(("T", 1) if j == 7 else None))
                ev = evac_copy(xfm[par][:, :, ch * 128:(ch + 1) * 128], P_tr[:],
                               wait=[("T", t_t)], est=680)
                rec("ptr", ev)
                rec(("xfm_ready", par, ch), ev)
                yield
            rec(("ln1o_free",), ("T", b.v("T")))
            for (w, kind) in (("wq", "q"), ("wk", "k")):
                for oc in range(8):
                    bank = oc % 2
                    wl = [g(("pbig", bank)), g(("xfm_ready", par, 0)),
                          g(("xfm_ready", par, 1))]
                    t_mm = proj_fm(w, P_big[:, bank, 0:TILE], oc,
                                   lambda kc, f=xfm[par]: f[:, kc, :], wl)
                    if kind == "q":
                        ev = evac_bias(qf[par][:, oc, :], P_big[:, bank, 0:TILE],
                                       bqf[:, oc:oc + 1], wait=[("T", t_mm)],
                                       est=(200 + TILE) / 1.2)
                    else:
                        ev = evac_bias(Kr[slot][:, oc, 64:320],
                                       P_big[:, bank, 0:TILE],
                                       bkf[:, oc:oc + 1], wait=[("T", t_mm)],
                                       est=(200 + TILE) / 1.2)
                    rec(("pbig", bank), ev)
                    if oc % 2 == 1:
                        yield
                rec(("qready", i) if kind == "q" else ("kready", i),
                    ("S", b.v("S")), ("V", b.v("V")))
            src_c1 = (K127f[:, :, :] if i == 0
                      else Kr[(i - 1) % 3][:, :, 256:320])
            cw = [g(("kr_a_free", slot))]
            cw.append(g(("pro",)) if i == 0 else g(("kready", i - 1)))
            c1 = vop(lambda e, s=src_c1: e.tensor_copy(Kr[slot][:, :, 0:64], s),
                     wait=cw, inc=("V", 1), est=250)
            rec(("kr_c1", i), ("V", c1))
            for ch in range(2):
                for half in range(2):
                    bank = half
                    wl = [g(("pbig", bank)), g(("xfm_ready", par, ch))]
                    for kc in range(8):
                        mm(P_big[:, bank, :],
                           xfm[par][:, kc, ch * 128:(ch + 1) * 128],
                           Wf["wv"][:, kc, half * 512:(half + 1) * 512],
                           kc == 0, False, wait=(wl if kc == 0 else []), est=230)
                    t_mm = mm(P_big[:, bank, :], ones_b[:],
                              bvrow[:, half * 512:(half + 1) * 512], False, True,
                              inc=("T", 1), est=230)
                    src = P_big[:, bank, :].rearrange("p (h v) -> p h v", v=64)
                    ev = evac_copy(Vr[slot][:, ch, half * 8:(half + 1) * 8, 0:64],
                                   src[:], wait=[("T", t_mm)], est=680)
                    rec(("pbig", bank), ev)
                    yield
            rec(("xfm_free",), ("T", b.v("T")))
            rec(("vready", i), ("S", b.v("S")), ("V", b.v("V")))

        def vblk(b0):
            if b0 < 0:
                return V127x[64:128, :, :]
            if b0 >= NB:
                return V0x[0:64, :, :]
            tb, vp = (b0 // 4) % 3, (b0 % 2) * 64
            return Vr[tb][vp:vp + 64, (b0 % 4) // 2, :, :]

        def build_U(j, wl):
            s = j % 3
            vop(lambda e, src=vblk(2 * j - 1): e.tensor_copy(
                VEU[s][0:64, :, :], src),
                wait=list(wl) + [g(("veu_free", s))], est=330)
            v2 = vop(lambda e, src=vblk(2 * j): e.tensor_copy(
                VEU[s][64:128, :, :], src), inc=("V", 1), est=330)
            rec(("veu", j), ("V", v2))

        def waves_ch(i, ch):
            par, slot = i % 2, i % 3
            cg, off = 2 * i + ch, 128 * ch
            VEa, VEb = VEU[cg % 3], VEU[(cg + 1) % 3]
            NWV = H // 2
            n_extra = 4 if ch == 1 else 0
            se_rec, evn_rec = {}, {}
            for w in range(NWV + 2 + n_extra):
                if ch == 1 and NWV + 2 <= w:
                    for sp in (2 * (w - NWV - 2), 2 * (w - NWV - 2) + 1):
                        s_r = sp % 2
                        Pr = PS[s_r]
                        t_m = None
                        for p in range(2):
                            h = 2 * sp + p
                            pt = (h % 2) * 64
                            wl = [g(("psc_free", s_r))] if p == 0 else []
                            t_m = mm(Pr[:, p, 0:2, :].rearrange(
                                "p a q -> p (a q)"),
                                qgf[pt:pt + 64, h // 2, :],
                                Kr[slot][pt:pt + 64, h // 2, 64:320],
                                True, True, wait=wl,
                                inc=(("T", 1) if p == 1 else None),
                                est=200, acc=accw(Pr[:, p, 0:2, :]), skip=True)
                        v_red = vop(lambda e, P_=Pr, s_=sp: e.reduce_max(
                            RS8[:, 2 * s_:2 * s_ + 2],
                            P_[:, 0:2, 0:2, :].rearrange("p a b q -> p a (b q)"),
                            axis=AX.X),
                            wait=[("T", t_m)], inc=("V", 1),
                            est=(140 + 512) / 0.96,
                            acc=accr(Pr[:, 0:2, 0:2, :]))
                        rec(("psc_free", s_r), ("V", v_red))
                    yield
                    continue
                if w >= 2:
                    wb = w - 2
                    sb_ = wb % 2
                    t_pv = None
                    for p in range(2):
                        h = 2 * wb + p
                        wl = [("S", se_rec[wb][p])]
                        if p == 0:
                            wl.append(g(("paux_free",)))
                            if wb == 0:
                                wl += [g(("veu", cg)), g(("veu", cg + 1))]
                        mm(P_aux[:, 2 + p, 0:65], PT[:, sb_, p, 2, :],
                           gvx[:, h, :], True, True, wait=wl, est=120,
                           acc=accw(P_aux[:, 2 + p, 0:65]), skip=True)
                        mm(P_aux[:, p, 0:65], PT[:, sb_, p, 0, :],
                           VEa[:, h, :], True, False, est=120,
                           acc=accw(P_aux[:, p, 0:65]), skip=True)
                        t_pv = mm(P_aux[:, p, 0:65], PT[:, sb_, p, 1, :],
                                  VEb[:, h, :], False, True,
                                  inc=(("T", 1) if p == 1 else None), est=120,
                                  acc=accw(P_aux[:, p, 0:65]), skip=True)
                    wl = [("T", t_pv)]
                    if wb == 0:
                        wl.append(g(("ctxlc_free",)))
                    vop(lambda e, w_=wb: e.tensor_copy(
                        CTXlc[:, 2 * w_:2 * w_ + 2, 0, :],
                        P_aux[:, 0:2, 0:65]), wait=wl, est=280,
                        acc=accr(P_aux[:, 0:2, 0:65]))
                    v_ev = vop(lambda e, w_=wb: e.tensor_copy(
                        CTXlc[:, 2 * w_:2 * w_ + 2, 1, :],
                        P_aux[:, 2:4, 0:65]), inc=("V", 1), est=280,
                        acc=accr(P_aux[:, 2:4, 0:65]))
                    rec(("paux_free",), ("V", v_ev))
                if 1 <= w and w - 1 < NWV:
                    wm = w - 1
                    s = wm % 2
                    Pw = PS[s]
                    t_n = transpose(Pw[0:2, 0, 3, :], SMW[:, 2 * s:2 * s + 2],
                                    identf[:], wait=[g(("vmx", s))],
                                    inc=("T", 1))
                    rec(("smw_free", s), ("T", t_n))
                    evn_rec[wm] = vop(lambda e, s_=s, P_=Pw: e.tensor_copy(
                        nfm2[0:2, s_, :, :],
                        P_[0:2, 0, 3:4, :].broadcast_to([2, 3, 128])),
                        wait=[("T", t_n), g(("nfm_free", s))], inc=("V", 1),
                        est=450, acc=accr(Pw[0:2, 0, 3, :]))
                if w < NWV:
                    s = w % 2
                    Pw = PS[s]
                    t_qm = None
                    for p in range(2):
                        h = 2 * w + p
                        part, hc = (h % 2) * 64, h // 2
                        q_h = qf[par][part:part + 64, hc, off:off + 128]
                        wl = []
                        if p == 0:
                            wl.append(g(("psc_free", s)))
                            if w == 0:
                                wl += [g(("qready", i)), g(("pro",))]
                                if ch == 0:
                                    wl += [g(("kready", i)), g(("kr_c1", i))]
                                else:
                                    wl += [g(("kr_c2", i))]
                        mm(Pw[:, p, 0:2, :].rearrange("p a q -> p (a q)"), q_h,
                           Kr[slot][part:part + 64, hc, off:off + 256],
                           True, True, wait=wl, est=180,
                           acc=accw(Pw[:, p, 0:2, :]), skip=True)
                        t_qm = mm(Pw[:, p, 2, :], q_h,
                                  gkf[part:part + 64, hc, :], True, True,
                                  inc=(("T", 1) if p == 1 else None), est=120,
                                  acc=accw(Pw[:, p, 2, :]), skip=True)
                    v_mx = vop(lambda e, s_=s, P_=Pw: e.reduce_max(
                        SMW[:, 2 * s_:2 * s_ + 2],
                        P_[:, 0:2, 0:3, :].rearrange("p a b q -> p a (b q)"),
                        axis=AX.X, negate=True),
                        wait=[("T", t_qm), g(("smw_free", s))], inc=("V", 1),
                        est=(140 + 768) / 0.96, acc=accr(Pw[:, 0:2, 0:3, :]))
                    rec(("vmx", s), ("V", v_mx))
                if 1 <= w and w - 1 < NWV:
                    wm = w - 1
                    s = wm % 2
                    Pw = PS[s]
                    t_r1 = {}
                    for p in range(2):
                        h = 2 * wm + p
                        part, hc = (h % 2) * 64, h // 2
                        q_mv = qf[par][part:part + 64, hc, off:off + 128]
                        mm(Pw[:, p, 0:3, :].rearrange("p a q -> p (a q)"),
                           SEL[:, p, :],
                           nfm2[0:4, s, :, :].rearrange("a b q -> a (b q)"),
                           True, False,
                           wait=([("V", evn_rec[wm]), g(("vmx", s))]
                                 if p == 0 else []),
                           est=230, acc=accw(Pw[:, p, 0:3, :]), skip=True)
                        mm(Pw[:, p, 0, :],
                           Kr[slot][part:part + 64, hc, off:off + 128], q_mv,
                           False, False, est=120,
                           acc=accw(Pw[:, p, 0, :]), skip=True)
                        mm(Pw[:, p, 1, :],
                           Kr[slot][part:part + 64, hc, off + 128:off + 256],
                           q_mv, False, False, est=120,
                           acc=accw(Pw[:, p, 1, :]), skip=True)
                        t_r1[p] = mm(Pw[:, p, 2, :], gkf[part:part + 64, hc, :],
                                     q_mv, False, True, inc=("T", 1), est=120,
                                     acc=accw(Pw[:, p, 2, :]), skip=True)
                    rec(("nfm_free", s), ("T", t_r1[1]))
                    se_p = {}
                    for p in range(2):
                        se_p[p] = sop(lambda e, s_=s, P_=Pw, pp=p: e.activation(
                            PT[:, s_, pp, 0:3, :], P_[:, pp, 0:3, :],
                            ACT.Exp, scale=SCALE),
                            wait=[("T", t_r1[p])], inc=("S", 1), est=780,
                            acc=accr(Pw[:, p, 0:3, :]))
                    se_rec[wm] = se_p
                    rec(("psc_free", s), ("S", se_p[1]))
                yield
            vdrain()
            vop(lambda e: e.reciprocal(RLD[:, :, :],
                                       CTXlc[:, :, :, 64:65]), est=250)
            vdrain()
            vop(lambda e: e.tensor_tensor(
                CTXlc[:, :, 0, 0:64], CTXlc[:, :, 0, 0:64],
                RLD[:, :, 0:1].broadcast_to([128, 16, 64]), op=OP.mult),
                est=1100)
            vop(lambda e: e.tensor_tensor(
                CTXlc[:, :, 1, 0:64], CTXlc[:, :, 1, 0:64],
                RLD[:, :, 1:2].broadcast_to([128, 16, 64]), op=OP.mult),
                est=1100)
            v_dv = vdrain(inc=("V", 1))
            p_c = b.op("gpsimd", lambda e: e.tensor_add(
                CTXt_t[:], CTXlc[:, :, 0, 0:64], CTXlc[:, :, 1, 0:64]),
                wait=[("V", v_dv), g(("ctxT_done", 0)), g(("ctxT_done", 1))],
                inc=("P", 1), est=2100)
            rec(("ctxlc_free",), ("P", p_c))
            rec(("ctx_ready", ch), ("P", p_c))
            if ch == 0:
                rec(("kr_a_free", slot), ("T", b.v("T")))
            else:
                rec(("kr_b_free", slot), ("T", b.v("T")))
            rec(("veu_free", cg % 3), ("T", b.v("T")))

        def emit_c2_u(i):
            slot = i % 3
            src_c2 = (K0f[:, :, :] if i + 1 >= NT
                      else Kr[(i + 1) % 3][:, :, 64:128])
            cw = [g(("kr_b_free", slot))]
            cw.append(g(("pro",)) if i + 1 >= NT else g(("kready", i + 1)))
            c2 = vop(lambda e, s=src_c2: e.tensor_copy(
                Kr[slot][:, :, 320:384], s), wait=cw, inc=("V", 1), est=250)
            rec(("kr_c2", i), ("V", c2))
            build_U(2 * i + 2,
                    [g(("vready", i)),
                     g(("pro",)) if i + 1 >= NT else g(("vready", i + 1))])

        def rows_gen(i):
            slot = i % 3
            vdrain()
            vop(lambda e: e.reduce_max(RS[:, 1:2], RS8[:, 0:16], axis=AX.X),
                est=140)
            vdrain()
            vop(lambda e: e.tensor_tensor(mnb[:], RS[:, 1:2], mrun[:],
                                          op=OP.max),
                wait=[g(("rs_s",))], est=140)
            vdrain()
            vop(lambda e: e.tensor_scalar(RS[:, 3:4], mnb[:], -1.0,
                                          None, op0=OP.mult),
                wait=[g(("rs_rd",))], est=140)
            v_nm = vdrain(inc=("V", 1))
            vop(lambda e: e.tensor_tensor(RS[:, 4:5], mrun[:], mnb[:],
                                          op=OP.subtract), est=140)
            v_al = vdrain(inc=("V", 1))
            vop(lambda e: e.tensor_copy(mrun[:], mnb[:]), est=140)
            s_al = sop(lambda e: e.activation(RS[:, 2:3], RS[:, 4:5], ACT.Exp,
                                              scale=SCALE),
                       wait=[("V", v_al)], inc=("S", 1), est=350)
            rec(("rs_s",), ("S", s_al))
            v_rs = vop(lambda e: e.tensor_scalar(racc[:], racc[:], RS[:, 2:3],
                                                 None, op0=OP.mult),
                       wait=[("S", s_al)], inc=("V", 1), est=1200)
            t_nr = transpose(PS[0][0:1, 0, 3, :], RS[:, 3:4], identf[:],
                             wait=[("V", v_nm), g(("psc_free", 0))],
                             inc=("T", 1))
            rec(("rs_rd",), ("T", t_nr))
            ev_n = sop(lambda e: e.copy(
                nfm2[0:1, 0, 0:2, :],
                PS[0][0:1, 0, 3:4, :].broadcast_to([1, 2, 128])),
                wait=[("T", t_nr), g(("nfm_free", 0))], inc=("S", 1), est=400,
                acc=accr(PS[0][0:1, 0, 3, :]))
            yield
            se_rec = {}
            v_a = None
            for w in range(H // 2 + 1):
                if w < H // 2:
                    s = w % 2
                    Pw = PS[s]
                    t_r1 = None
                    for p in range(2):
                        h = 2 * w + p
                        pt = (h % 2) * 64
                        wl = [("S", ev_n)]
                        if p == 0:
                            wl.append(g(("psc_free", s)))
                        mm(Pw[:, p, 0:2, :].rearrange("p a q -> p (a q)"),
                           ones_b[:],
                           nfm2[0:1, 0, 0:2, :].rearrange("a b q -> a (b q)"),
                           True, False, wait=wl, est=180,
                           acc=accw(Pw[:, p, 0:2, :]), skip=True)
                        mm(Pw[:, p, 0, :],
                           Kr[slot][pt:pt + 64, h // 2, 64:192],
                           qgf[pt:pt + 64, h // 2, :], False, False,
                           est=120, acc=accw(Pw[:, p, 0, :]), skip=True)
                        t_r1 = mm(Pw[:, p, 1, :],
                                  Kr[slot][pt:pt + 64, h // 2, 192:320],
                                  qgf[pt:pt + 64, h // 2, :], False, True,
                                  inc=(("T", 1) if p == 1 else None),
                                  est=120, acc=accw(Pw[:, p, 1, :]), skip=True)
                    if w >= H // 2 - 2:
                        rec(("nfm_free", 0), ("T", t_r1))
                    s_e = sop(lambda e, s_=s, P_=Pw: e.activation(
                        PT[:, s_, 0:2, 0:2, :], P_[:, 0:2, 0:2, :], ACT.Exp,
                        scale=SCALE), wait=[("T", t_r1)], inc=("S", 1),
                        est=600, acc=accr(Pw[:, 0:2, 0:2, :]))
                    se_rec[w] = s_e
                    rec(("psc_free", s), ("S", s_e))
                if w >= 1:
                    wb = w - 1
                    sb_ = wb % 2
                    t_pv = None
                    for p in range(2):
                        h = 2 * wb + p
                        wl = []
                        if p == 0:
                            wl += [("S", se_rec[wb]), g(("paux_free",))]
                        mm(P_aux[:, p, 0:65], PT[:, sb_, p, 0, :],
                           Vr[slot][:, 0, h, :], True, False, wait=wl,
                           est=120, acc=accw(P_aux[:, p, 0:65]), skip=True)
                        t_pv = mm(P_aux[:, p, 0:65], PT[:, sb_, p, 1, :],
                                  Vr[slot][:, 1, h, :], False, True,
                                  inc=(("T", 1) if p == 1 else None), est=120,
                                  acc=accw(P_aux[:, p, 0:65]), skip=True)
                    v_a = vop(lambda e, w_=wb: e.tensor_tensor(
                        racc[:, 2 * w_:2 * w_ + 2, :],
                        racc[:, 2 * w_:2 * w_ + 2, :],
                        P_aux[:, 0:2, 0:65], op=OP.add),
                        wait=[("T", t_pv)] +
                             ([("V", v_rs)] if wb == 0 else []),
                        inc=("V", 1), est=400, acc=accr(P_aux[:, 0:2, 0:65]))
                    rec(("paux_free",), ("V", v_a))
                yield
            rec("racc_rd", ("V", v_a))

        def stage_C_gen(i, ch, epilogue=False, out=None):
            par = 0 if epilogue else i % 2
            ctx_flat = CTXt[ch][:].rearrange("p h v -> p (h v)")
            t_t = None
            for j in range(8):
                t_t = transpose(P_tr[:, j, :], ctx_flat[:, j * 128:(j + 1) * 128],
                                ident[:],
                                wait=([g(("ctx_ready", ch)), g("ptr")]
                                      if j == 0 else []),
                                inc=(("T", 1) if j == 7 else None))
            ev = evac_copy(ctxf[ch][:], P_tr[:], wait=[("T", t_t)], est=680)
            rec("ptr", ev)
            rec(("ctxT_done", ch), ("T", b.v("T")))
            yield
            for half in range(2):
                bank = half
                wl = [ev, g(("pbig", bank))]
                for kc in range(8):
                    mm(P_big[:, bank, :], ctxf[ch][:, kc, :],
                       Wf["wo"][:, kc, half * 512:(half + 1) * 512],
                       kc == 0, False, wait=(wl if kc == 0 else []), est=230)
                t_wo = mm(P_big[:, bank, :], ones_b[:],
                          borow[:, half * 512:(half + 1) * 512], False, True,
                          inc=(("T", 1) if half == 1 else None), est=230)
            yield
            ev_o = sop(lambda e, cc=ch: e.activation(
                ln2s[cc][:].rearrange("p (a c) -> p a c", a=2),
                P_big[:, 0:2, :], ACT.Identity, accum_out=SM2[ch][:, 0:1]),
                wait=[("T", t_wo), g(("ln2s_free", 0)), g(("ln2s_free", 1)),
                      g(("sm2", ch))],
                inc=("S", 1), est=(200 + D) / 1.2,
                acc=accr(P_big[:, 0, :], P_big[:, 1, :]))
            rec(("pbig", 0), ("S", ev_o))
            rec(("pbig", 1), ("S", ev_o))
            s_a = ln_sbuf(ln2s[ch][:], SM2[ch], lnat[ch][:],
                          wait_pre=[("S", ev_o),
                                    g(("sq_free", ch)), g(("lnat_free", ch))],
                          have_sum=True)
            rec(("sq_free", ch), ("S", b.v("S")))
            yield
            xin = xg if epilogue else xbuf[par][:, ch, :]
            vop(lambda e, cc=ch, xi=xin: e.tensor_add(xp[cc][:], xi, lnat[cc][:]),
                wait=[s_a, g(("xp_free", 0)), g(("xp_free", 1)),
                      g(("sm2", ch))] +
                     ([g(("xg_ld",))] if epilogue else [g(("xload", i))]),
                est=(140 + D) / 0.96)
            v_xp = vdrain(inc=("V", 1))
            s_2 = ln_sbuf(xp[ch][:], SM2[ch], ln2s[ch][:],
                          wait_pre=[("V", v_xp), g(("ln2s_free", ch))],
                          scale_extra=sc_bc[:, 2:3])
            rec(("pbig", 0), s_a, s_2, ("V", v_xp))
            rec(("pbig", 1), s_a, s_2, ("V", v_xp))
            rec(("sq_free", ch), ("S", b.v("S")))
            rec(("lnat_free", ch), ("V", v_xp))
            yield
            pb1 = P_big[:, 1, :].rearrange("p (a c) -> p a c", a=4)
            t_a = None
            for j in range(4):
                t_a = transpose(pb1[:, j, :],
                                ln2s[ch][:, j * 128:(j + 1) * 128],
                                identf[:],
                                wait=([s_2, g(("pbig", 1))] if j == 0 else []),
                                inc=(("T", 1) if j == 3 else None))
            ev_a = evac(lambda e, cc=ch, s=pb1: e.tensor_scalar(
                xq8[cc][:, 0:4, :], s[:], CLIP_HALF, -CLIP_HALF,
                op0=OP.min, op1=OP.max), None, wait=[("T", t_a)], est=700,
                acc=accr(pb1[:]))
            rec(("pbig", 1), ev_a)
            pb0 = P_big[:, 0, :].rearrange("p (a c) -> p a c", a=4)
            t_b = None
            for j in range(4):
                t_b = transpose(pb0[:, j, :],
                                ln2s[ch][:, (4 + j) * 128:(5 + j) * 128],
                                identf[:],
                                wait=([g(("pbig", 0))] if j == 0 else []),
                                inc=(("T", 1) if j == 3 else None))
            ev_b = evac(lambda e, cc=ch, s=pb0: e.tensor_scalar(
                xq8[cc][:, 4:8, :], s[:], CLIP_HALF, -CLIP_HALF,
                op0=OP.min, op1=OP.max), None, wait=[("T", t_b)], est=700,
                acc=accr(pb0[:]))
            rec(("pbig", 0), ev_b)
            rec(("ln2s_free", ch), ("T", b.v("T")))
            yield
            for half in range(2):
                bank = half
                wl = [ev_a, ev_b, g(("pbig", bank))]
                for kp in range(4):
                    b.op("tensor", lambda e, cc=ch, k_=kp, hh=half, bb=bank:
                         e.matmul(P_big[:, bb, :], xq8[cc][:, 2 * k_:2 * k_ + 2, :],
                                  Wf8[:, 2 * k_:2 * k_ + 2,
                                      hh * 512:(hh + 1) * 512],
                                  start=(k_ == 0), stop=False,
                                  perf_mode=mybir.MatmulPerfMode.DoubleRow),
                         wait=(wl if kp == 0 else []), est=300,
                         acc=accw(P_big[:, bank, :]))
                t_ff = mm(P_big[:, bank, :], ones_b[:],
                          bfrow[:, half * 512:(half + 1) * 512], False, True,
                          inc=(("T", 1) if half == 1 else None), est=230)
            yield
            ev_f = sop(lambda e, cc=ch: e.activation(
                ln2s[cc][:].rearrange("p (a c) -> p a c", a=2),
                P_big[:, 0:2, :], ACT.Identity, scale=sc_bc[:, 0:1],
                accum_out=SM2[ch][:, 0:1]),
                wait=[("T", t_ff), g(("ln2s_free", ch)), g(("sm2", ch))],
                inc=("S", 1), est=(200 + D) / 1.2,
                acc=accr(P_big[:, 0, :], P_big[:, 1, :]))
            rec(("pbig", 0), ("S", ev_f))
            rec(("pbig", 1), ("S", ev_f))
            s_f = ln_sbuf(ln2s[ch][:], SM2[ch], lnat[ch][:],
                          wait_pre=[("S", ev_f), g(("lnat_free", ch)),
                                    g(("sq_free", ch))], have_sum=True)
            rec(("sq_free", ch), ("S", b.v("S")))
            rec(("sm2", ch), ("S", b.v("S")), ("V", b.v("V")))
            if epilogue:
                vop(lambda e, cc=ch: e.tensor_add(xg, xp[cc][:], lnat[cc][:]),
                    wait=[s_f], est=(140 + D) / 0.96)
                v_o = vdrain(inc=("V", 1))
                rec(("lnat_free", ch), ("V", v_o))
                rec(("xp_free", ch), ("V", v_o))
                rec(("pbig", 0), s_f, ("V", b.v("V")))
                rec(("pbig", 1), s_f, ("V", b.v("V")))
                if out is not None:
                    out["v_o"] = ("V", v_o)
                return
            vop(lambda e, cc=ch, pp=par: e.tensor_add(
                xbuf[pp][:, cc, :], xp[cc][:], lnat[cc][:]),
                wait=[s_f], est=(140 + D) / 0.96)
            v_o = vdrain(inc=("V", 1))
            rec(("lnat_free", ch), ("V", v_o))
            rec(("xp_free", ch), ("V", v_o))
            rec(("pbig", 0), s_f, ("V", b.v("V")))
            rec(("pbig", 1), s_f, ("V", b.v("V")))
            cg = 2 * i + ch
            dv = dma("gpsimd", f"st{par}", out_d[cg * 128:(cg + 1) * 128, :],
                     xbuf[par][:, ch, :], wait=[("V", v_o)])
            if ch == 1:
                rec(("xfree", i % 2), (f"st{par}", dv))

        def drain(gen):
            for _ in gen:
                pass

        def zip2(ga, gb):
            live = [x for x in (ga, gb) if x is not None]
            while live:
                for g_ in list(live):
                    try:
                        next(g_)
                    except StopIteration:
                        live.remove(g_)

        load_x(0)
        load_x(1)
        drain(stage_A_gen(0))
        for i in range(NT):
            if i == 0:
                build_U(0, [g(("vready", 0)), g(("pro",))])
            build_U(2 * i + 1, [g(("vready", i))])
            zip2(waves_ch(i, 0),
                 stage_A_gen(i + 1) if i + 1 < NT else None)
            emit_c2_u(i)
            zip2(waves_ch(i, 1), stage_C_gen(i, 0))
            zip2(rows_gen(i), stage_C_gen(i, 1))
            load_x(i + 2)

        dma("sync", "dS", xg[0:64, :], x_d[gi0 * 64:(gi0 + 1) * 64, :],
            wait=[g(("xfree", NT % 2))])
        dxg = dma("sync", "dS", xg[64:128, :], x_d[gi1 * 64:(gi1 + 1) * 64, :])
        rec(("xg_ld",), ("dS", dxg))
        vdrain()
        vop(lambda e: e.reciprocal(RL16[:], racc[:, :, 64]),
            wait=[g("racc_rd"), g(("ctxT_done", 0))], est=200)
        vdrain()
        vop(lambda e: e.tensor_tensor(
            CTXt[0][:], racc[:, :, 0:64],
            RL16[:].rearrange("p (h o) -> p h o", o=1).broadcast_to([128, H, 64]),
            op=OP.mult), est=1200)
        v_d = vdrain(inc=("V", 1))
        rec(("ctx_ready", 0), ("V", v_d))
        hold = {}
        drain(stage_C_gen(None, 0, epilogue=True, out=hold))
        v_o = hold["v_o"]
        st_f0, st_f1 = b.v("st0"), b.v("st1")
        dma("gpsimd", "st0", out_d[gi0 * 64:(gi0 + 1) * 64, :], xg[0:64, :],
            wait=[v_o, ("st0", st_f0), ("st1", st_f1)])
        dma("gpsimd", "st1", out_d[gi1 * 64:(gi1 + 1) * 64, :], xg[64:128, :])
        b.op("gpsimd", lambda e: e.nop(),
             wait=[("st0", b.v("st0")), ("st1", b.v("st1"))])

        import os as _osc
        if _osc.environ.get("KCHECK", "1") == "1":
            bad = b.check_banks()
            if bad:
                for x in bad[:20]:
                    print("PSUM COLLISION:", x)
                raise RuntimeError(f"{len(bad)} psum bank collisions")

        with ExitStack() as sctx:
            semh = {name: sctx.enter_context(nc.semaphore(name))
                    for name in b.cnt}
            with nc.Block() as block:
                def mk(ename):
                    def run(eng):
                        for fn, waits, inc in b.ops[ename]:
                            for (s, vv) in waits:
                                eng.wait_ge(semh[s], vv)
                            ins = fn(eng)
                            if inc is not None:
                                ins.then_inc(semh[inc[0]], inc[1])
                    return run
                block.sync(mk("sync"))
                block.gpsimd(mk("gpsimd"))
                block.tensor(mk("tensor"))
                block.vector(mk("vector"))
                block.scalar(mk("scalar"))
    return nc


def kernel(L0_attn_x, L0_attn_Wq, L0_attn_bq, L0_attn_Wk, L0_attn_bk,
           L0_attn_Wv, L0_attn_bv, L0_attn_Wo, L0_attn_bo,
           L0_attn_global_indices, L0_ffn_weight_fp8, L0_ffn_bias,
           L0_ffn_input_scale, L0_ffn_weight_scale):
    x = np.ascontiguousarray(np.asarray(L0_attn_x), dtype=np.float32)
    Bn, NTOK, Dm = x.shape
    gi = np.asarray(L0_attn_global_indices).astype(np.int64)
    nc = build_nc(NTOK, int(gi[0]), int(gi[1]))
    base = {
        "wq": np.ascontiguousarray(L0_attn_Wq, dtype=np.float32),
        "wk": np.ascontiguousarray(L0_attn_Wk, dtype=np.float32),
        "wv": np.ascontiguousarray(L0_attn_Wv, dtype=np.float32),
        "wo": np.ascontiguousarray(L0_attn_Wo, dtype=np.float32),
        "wf": np.ascontiguousarray(L0_ffn_weight_fp8, dtype=np.float32),
        "bq": np.ascontiguousarray(L0_attn_bq, dtype=np.float32).reshape(1, Dm),
        "bk": np.ascontiguousarray(L0_attn_bk, dtype=np.float32).reshape(1, Dm),
        "bv": np.ascontiguousarray(L0_attn_bv, dtype=np.float32).reshape(1, Dm),
        "bo": np.ascontiguousarray(L0_attn_bo, dtype=np.float32).reshape(1, Dm),
        "bf": np.ascontiguousarray(L0_ffn_bias, dtype=np.float32).reshape(1, Dm),
        "sin": np.ascontiguousarray(L0_ffn_input_scale,
                                    dtype=np.float32).reshape(1, 1),
        "sw": np.ascontiguousarray(L0_ffn_weight_scale,
                                   dtype=np.float32).reshape(1, 1),
    }
    in_maps = [dict(base, x=np.ascontiguousarray(x[c])) for c in range(Bn)]
    res = run_bass_kernel_spmd(nc, in_maps, core_ids=list(range(Bn)))
    return np.stack([np.asarray(r["out"]) for r in res.results], axis=0)


# version 25
# speedup vs baseline: 1.0093x; 1.0093x over previous
import numpy as np
import ml_dtypes
from contextlib import ExitStack

import concourse.bass as bass
import concourse.mybir as mybir
from concourse.bass_utils import run_bass_kernel_spmd

F32 = mybir.dt.float32
BF16 = mybir.dt.bfloat16
FP8 = mybir.dt.float8e4
AX = mybir.AxisListType
OP = mybir.AluOpType
ACT = mybir.ActivationFunctionType

D = 1024
H = 16
BS = 64
TILE = 256
SCALE = 0.125
EPS = 1e-5
CLIP_HALF = 224.0
ENGINES = ("sync", "gpsimd", "tensor", "vector", "scalar")


class Bld:
    def __init__(self):
        self.ops = {e: [] for e in ENGINES}
        self.cnt = {}
        self.est = {e: 0.0 for e in ENGINES}
        self.G = {}
        self.accs = []
        self.sem_log = {}
        self.waits_of = {}

    def check_banks(self):
        order = {e: i for i, e in enumerate(ENGINES)}
        n = {e: len(self.ops[e]) for e in ENGINES}
        clocks = {e: [None] * n[e] for e in ENGINES}
        sat = {}
        def provider(s, v):
            log = self.sem_log.get(s, [])
            lo, hi = 0, len(log)
            while lo < hi:
                mid = (lo + hi) // 2
                if log[mid][0] >= v:
                    hi = mid
                else:
                    lo = mid + 1
            if lo == len(log):
                raise RuntimeError(f"wait {s}>={v} never satisfied")
            return log[lo][1], log[lo][2]
        ptr = {e: 0 for e in ENGINES}
        cur = {e: [0] * len(ENGINES) for e in ENGINES}
        total = sum(n.values())
        done = 0
        while done < total:
            progressed = False
            for e in ENGINES:
                while ptr[e] < n[e]:
                    i = ptr[e]
                    fn, ws, inc = self.ops[e][i]
                    ok = True
                    c = list(cur[e])
                    for (s, v) in ws:
                        pe, pi = provider(s, v)
                        if clocks[pe][pi] is None or (
                                pe == e and pi >= i):
                            ok = False
                            break
                        pc = clocks[pe][pi]
                        for k in range(len(ENGINES)):
                            c[k] = max(c[k], pc[k])
                        c[order[pe]] = max(c[order[pe]], pi + 1)
                    if not ok:
                        break
                    clocks[e][i] = tuple(c)
                    c[order[e]] = i + 1
                    cur[e] = c
                    ptr[e] += 1
                    done += 1
                    progressed = True
            if not progressed:
                stuck = {e: (ptr[e], self.ops[e][ptr[e]][1]
                             if ptr[e] < n[e] else None) for e in ENGINES}
                raise RuntimeError(f"deadlock in program graph: {stuck}")
        bad = []
        by_bank = {}
        for (e, i, bank, rw) in self.accs:
            by_bank.setdefault(bank, []).append((e, i, rw))
        def unordered(e1, i1, e2, i2):
            if e1 == e2:
                return False
            c1 = clocks[e1][i1]
            c2 = clocks[e2][i2]
            return not (i1 < c2[order[e1]] or i2 < c1[order[e2]])

        for bank, lst in by_bank.items():
            tw = [(e, i) for (e, i, rw) in lst if e == "tensor" and rw == "w"]
            vs = [(e, i) for (e, i, rw) in lst if e in ("vector", "scalar")]
            for (te, ti) in tw:
                for (ve, vi) in vs:
                    if unordered(te, ti, ve, vi):
                        bad.append(("T-vs-VS", bank, ti, ve, vi))
            vv = [(e, i) for (e, i, rw) in lst if e == "vector"]
            ss = [(e, i) for (e, i, rw) in lst if e == "scalar"]
            for (e1, i1) in vv:
                for (e2, i2) in ss:
                    if unordered(e1, i1, e2, i2):
                        bad.append(("V-vs-S", bank, i1, e2, i2))
        return bad

    def sem(self, name):
        self.cnt.setdefault(name, 0)
        return name

    def op(self, eng, fn, wait=(), inc=None, est=0.0, acc=()):
        ws = []
        for w in wait:
            if w is None:
                continue
            if isinstance(w, list):
                ws.extend(p for p in w if p is not None and p[1] > 0)
            elif w[0] is not None and w[1] > 0:
                ws.append(w)
        self.ops[eng].append((fn, tuple(ws), inc))
        self.est[eng] += est
        idx = len(self.ops[eng]) - 1
        for a in acc:
            self.accs.append((eng, idx, a[0], a[1]))
        if inc is not None:
            self.cnt[inc[0]] += inc[1]
            self.sem_log.setdefault(inc[0], []).append(
                (self.cnt[inc[0]], eng, idx))
            return self.cnt[inc[0]]
        return None

    def v(self, s):
        return self.cnt[s]

    def rec(self, key, *pairs):
        self.G[key] = [p for p in pairs if p is not None]

    def g(self, key):
        return self.G.get(key, [])


def build_nc(NTOK, gi0, gi1):
    NB = NTOK // BS
    NT = NTOK // TILE
    assert NTOK % TILE == 0 and 0 <= gi0 < NB and 0 <= gi1 < NB

    nc = bass.Bass("TRN2", target_bir_lowering=False, debug=False)
    b = Bld()
    for s in ("dS", "dG", "wl0", "wl1", "xl0", "xl1", "st0", "st1", "T", "V", "S", "P", "tA0", "tA1"):
        b.sem(s)

    x_d = nc.dram_tensor("x", [NTOK, D], F32, kind="ExternalInput").ap()
    w_d = {w: nc.dram_tensor(w, [D, D], F32, kind="ExternalInput").ap()
           for w in ("wq", "wk", "wv", "wo", "wf")}
    b_d = {n: nc.dram_tensor(n, [1, D], F32, kind="ExternalInput").ap()
           for n in ("bq", "bk", "bv", "bo", "bf")}
    sin_d = nc.dram_tensor("sin", [1, 1], F32, kind="ExternalInput").ap()
    sw_d = nc.dram_tensor("sw", [1, 1], F32, kind="ExternalInput").ap()
    out_d = nc.dram_tensor("out", [NTOK, D], F32, kind="ExternalOutput").ap()

    ident_dram = nc.inline_tensor(np.eye(128, dtype=ml_dtypes.bfloat16), "ident_c")
    identf_dram = nc.inline_tensor(np.eye(128, dtype=np.float32), "identf_c")
    consts_dram = nc.inline_tensor(
        np.tile(np.array([[-0.5, SCALE, 0.5, 1e-5]], np.float32), (128, 1)),
        "consts_c")
    _sel = np.zeros((4, 2, 128), dtype=ml_dtypes.bfloat16)
    _sel[0, 0, :] = 1
    _sel[1, 1, :] = 1
    _sel[2, :, 0:64] = 1
    _sel[3, :, 64:128] = 1
    sel_dram = nc.inline_tensor(_sel, "sel_c")
    _cbig = np.zeros((2, 2, 3, 128), dtype=ml_dtypes.bfloat16)
    _cbig[0, :, 0, 64:128] = -30000.0
    _cbig[1, :, 1, 0:64] = -30000.0
    cbig_dram = nc.inline_tensor(_cbig, "cbig_c")

    with ExitStack() as ctx:
        E = ctx.enter_context
        sb = lambda n, sh, dt=F32: E(nc.sbuf_tensor(n, sh, dt))

        Wf = {w: sb(f"W_{w}", [128, 8, D], BF16) for w in ("wq", "wk", "wv", "wo")}
        Wf8 = sb("W_f8", [128, 8, D], FP8)
        bqf = sb("bqf", [128, 8])
        bkf = sb("bkf", [128, 8])
        bq_row = sb("bq_row", [1, D])
        bk_row = sb("bk_row", [1, D])
        bvrow = sb("bvrow", [1, D], BF16)
        borow = sb("borow", [1, D], BF16)
        bfrow_f = sb("bfrow_f", [1, D])
        bfrow = sb("bfrow", [1, D], BF16)
        ones_b = sb("ones_b", [1, 128], BF16)
        ones_f = sb("ones_f", [1, 128])
        ident = sb("ident", [128, 128], BF16)
        identf = sb("identf", [128, 128])
        CONSTS = sb("CONSTS", [128, 4])
        nc.const_aps.register(F32, -0.5, CONSTS[:, 0:1])
        nc.const_aps.register(F32, SCALE, CONSTS[:, 1:2])
        nc.const_aps.register(F32, 0.5, CONSTS[:, 2:3])
        nc.const_aps.register(F32, EPS, CONSTS[:, 3:4])
        s11 = sb("s11", [1, 4])
        sc_bc = sb("sc_bc", [128, 3])

        Kr = [sb(f"Kr{j}", [128, 8, 384], BF16) for j in range(3)]
        Vr = [sb(f"Vr{j}", [128, 2, H, 65], BF16) for j in range(3)]
        VEU = [sb(f"VEU{j}", [128, H, 65], BF16) for j in range(3)]
        K0f = sb("K0f", [128, 8, 64], BF16)
        K127f = sb("K127f", [128, 8, 64], BF16)
        V0x = sb("V0x", [128, H, 65], BF16)
        V127x = sb("V127x", [128, H, 65], BF16)
        gkf = sb("gkf", [128, 8, 128], BF16)
        gvx = sb("gvx", [128, H, 65], BF16)
        qgf = sb("qgf", [128, 8, 128], BF16)
        racc = sb("racc", [128, H, 65])
        mrun = sb("mrun", [128, 1], BF16)
        mnb = sb("mnb", [128, 1], BF16)
        RS8 = sb("RS8", [128, 16])

        xbuf = [sb(f"xbuf{j}", [128, 2, D]) for j in range(2)]
        qf = [sb(f"qf{j}", [128, 8, TILE], BF16) for j in range(2)]
        xfm_t = sb("xfm", [128, 8, TILE], BF16)
        xfm = [xfm_t, xfm_t]
        ln1o2 = sb("ln1o2", [128, 2, D], BF16)
        SMA = sb("SMA", [128, 8])
        SMW = sb("SMW", [128, 4])
        nfm2 = sb("nfm2", [4, 2, 3, 128], BF16)
        SEL = sb("SEL", [4, 2, 128], BF16)
        RLD = sb("RLD", [128, 16, 2])
        CTXlc = sb("CTXlc", [128, 16, 2, 65], BF16)
        PT = sb("PT", [128, 2, 2, 3, 128], BF16)
        CTXt_t = sb("CTXt", [128, H, 64], BF16)
        CTXt = [CTXt_t, CTXt_t]
        ctxf_t = sb("ctxf", [128, 8, 128], BF16)
        ctxf = [ctxf_t, ctxf_t]
        xp_t = sb("xp", [128, D])
        xp = [xp_t, xp_t]
        lnat_t = sb("lnat", [128, D])
        lnat = [lnat_t, lnat_t]
        ln2s_t = sb("ln2s", [128, D])
        ln2s = [ln2s_t, ln2s_t]
        xq8_t = sb("xq8", [128, 8, 128], FP8)
        xq8 = [xq8_t, xq8_t]
        SQ_t = sb("SQ", [128, D])
        SQ = [SQ_t, SQ_t]
        SM = [sb(f"SM{j}", [128, 8]) for j in range(2)]
        SM2 = [sb(f"SM2{j}", [128, 8]) for j in range(2)]
        RS = sb("RS", [128, 6])
        RL = sb("RL", [128, 4])
        RL16 = sb("RL16", [128, 16])

        P_big = E(nc.psum_tensor("P_big", [128, 2, 512], F32))
        P_tr = E(nc.psum_tensor("P_tr", [128, 8, 128], BF16))
        P_aux = E(nc.psum_tensor("P_aux", [128, 4, 128], F32))
        PS = [E(nc.psum_tensor(f"PS{j}", [128, 2, 4, 128], F32))
              for j in range(2)]

        g, rec = b.g, b.rec
        _pbase = {"P_big": 0, "P_tr": 2, "P_aux": 3, "PS0": 4, "PS1": 6}

        def pb(ap):
            try:
                t = ap.tensor
            except Exception:
                return []
            if t.name not in _pbase:
                return []
            es = mybir.dt.size(ap.dtype)
            start = ap.offset * es
            span = 0
            for (st, cnt) in ap.ap[1:]:
                span += st * es * (cnt - 1)
            span += es
            b0 = _pbase[t.name] + start // 2048
            b1 = _pbase[t.name] + (start + span - 1) // 2048
            return list(range(b0, b1 + 1))

        def accr(*aps):
            out = []
            for a in aps:
                out.extend((bb_, "r") for bb_ in pb(a))
            return out

        def accw(*aps):
            out = []
            for a in aps:
                out.extend((bb_, "w") for bb_ in pb(a))
            return out

        def vop(fn, wait=(), inc=None, est=250.0, acc=()):
            return b.op("vector", fn, wait=wait, inc=inc, est=est, acc=acc)

        def sop(fn, wait=(), inc=None, est=250.0, acc=()):
            return b.op("scalar", fn, wait=wait, inc=inc, est=est, acc=acc)

        def top(fn, wait=(), inc=None, est=100.0, acc=()):
            return b.op("tensor", fn, wait=wait, inc=inc, est=est, acc=acc)

        def vdrain(inc=None):
            return vop(lambda e: e.drain(), inc=inc, est=30.0)

        def sdrain(inc=None):
            return sop(lambda e: e.drain(), inc=inc, est=30.0)

        def tmark():
            return top(lambda e: e.nop(), inc=("T", 1), est=5.0)

        def dma(q, sem, out_ap, in_ap, wait=()):
            return b.op(q, lambda e, o=out_ap, i=in_ap: e.dma_start(out=o, in_=i),
                        wait=wait, inc=(sem, 16))

        def evac(fnv, fns, wait=(), est=500.0, acc=()):
            if fnv is None:
                eng = "scalar"
            elif fns is None:
                eng = "vector"
            else:
                eng = "vector" if b.est["vector"] <= b.est["scalar"] else "scalar"
            if eng == "vector":
                vop(fnv, wait=wait, est=est, acc=acc)
                return ("V", vdrain(inc=("V", 1)))
            sop(fns, wait=wait, est=est, acc=acc)
            return ("S", sdrain(inc=("S", 1)))

        def evac_copy(out_ap, in_ap, wait=(), est=500.0):
            return evac(lambda e, o=out_ap, i=in_ap: e.tensor_copy(o, i),
                        lambda e, o=out_ap, i=in_ap: e.copy(o, i),
                        wait=wait, est=est, acc=accr(in_ap))

        def evac_bias(out_ap, in_ap, bias_ap, wait=(), est=500.0):
            return evac(lambda e, o=out_ap, i=in_ap, bb=bias_ap:
                        e.tensor_scalar(o, i, bb, None, op0=OP.add),
                        lambda e, o=out_ap, i=in_ap, bb=bias_ap:
                        e.activation(o, i, ACT.Identity, bias=bb),
                        wait=wait, est=est, acc=accr(in_ap))

        def transpose(out_ap, in_ap, idt, wait=(), inc=None):
            return top(lambda e, o=out_ap, i=in_ap, d=idt: e.transpose(o, i, d),
                       wait=wait, inc=inc, est=120, acc=accw(out_ap))

        def mm(out_ap, lhsT_ap, rhs_ap, start, stop, wait=(), inc=None, est=120,
               acc=None, skip=False):
            return top(lambda e, o=out_ap, l=lhsT_ap, r=rhs_ap, st=start,
                       sp=stop, sk=skip:
                       e.matmul(o, l, r, start=st, stop=sp,
                                skip_group_check=sk),
                       wait=wait, inc=inc, est=est,
                       acc=(accw(out_ap) if acc is None else acc))

        dma("sync", "dS", ident[:], ident_dram.ap())
        dma("sync", "dS", identf[:], identf_dram.ap())
        dma("sync", "dS", CONSTS[:], consts_dram.ap())
        dma("sync", "dS", s11[0:1, 0:1], sin_d)
        dma("sync", "dS", s11[0:1, 1:2], sw_d)
        dma("sync", "dS", bq_row[:], b_d["bq"])
        dma("sync", "dS", bk_row[:], b_d["bk"])
        dma("sync", "dS", bfrow_f[:], b_d["bf"])
        dma("sync", "dS", SEL[:], sel_dram.ap())
        dma("sync", "dS", nfm2[2:4, :, :, :], cbig_dram.ap())
        xg = xbuf[NT % 2][:, 0, :]
        gath = xbuf[0]
        for j, blk in enumerate((0, NB - 1, gi0, gi1)):
            dma("sync", "dS", gath[(j % 2) * 64:(j % 2) * 64 + 64, j // 2, :],
                x_d[blk * 64:(blk + 1) * 64, :])
        pro_dS = b.v("dS")
        dma("gpsimd", "dG", bvrow[:], b_d["bv"])
        dma("gpsimd", "dG", borow[:], b_d["bo"])
        pro_dG = b.v("dG")

        vop(lambda e: e.memset(ones_b[:], 1.0))

        vop(lambda e: e.memset(ones_f[:], 1.0))
        vop(lambda e: e.memset(PT[:], 0.0))
        vop(lambda e: e.memset(racc[:], 0.0))
        vop(lambda e: e.memset(mrun[:], -1e30))
        for j in range(3):
            vop(lambda e, jj=j: e.memset(Vr[jj][:, :, :, 64:65], 1.0))
        for t in (V0x, V127x, gvx):
            vop(lambda e, tt=t: e.memset(tt[:, :, 64:65], 1.0))
        vop(lambda e: e.tensor_scalar(s11[0:1, 2:3], s11[0:1, 0:1], s11[0:1, 1:2],
                                      4.0, op0=OP.mult, op1=OP.mult),
            wait=[("dS", pro_dS)])
        vdrain()
        vop(lambda e: e.reciprocal(s11[0:1, 3:4], s11[0:1, 2:3]))
        vop(lambda e: e.reciprocal(s11[0:1, 0:1], s11[0:1, 0:1]))
        vdrain()
        vop(lambda e: e.tensor_scalar(bfrow[:], bfrow_f[:], s11[0:1, 3:4], None,
                                      op0=OP.mult), wait=[("dG", pro_dG)])
        v_pro = vdrain(inc=("V", 1))
        mm(P_aux[:, 0, 0:1], ones_f[:], s11[0:1, 2:3], True, True,
           wait=[("V", v_pro)])
        t_bc = mm(P_aux[:, 1, 0:1], ones_f[:], s11[0:1, 0:1], True, True,
                  inc=("T", 1))
        vop(lambda e: e.tensor_copy(sc_bc[:, 0:1], P_aux[:, 0, 0:1]),
            wait=[("T", t_bc)], acc=accr(P_aux[:, 0, 0:1]))
        vop(lambda e: e.tensor_scalar(sc_bc[:, 2:3], P_aux[:, 1, 0:1], 0.5, None,
                                      op0=OP.mult), acc=accr(P_aux[:, 1, 0:1]))
        vdrain()
        vop(lambda e: e.tensor_tensor(sc_bc[:, 1:2], sc_bc[:, 0:1], sc_bc[:, 0:1],
                                      op=OP.mult))
        v_scbc = vdrain(inc=("V", 1))
        rec("p_aux", ("V", v_scbc))
        for row, dstf in ((bq_row, bqf), (bk_row, bkf)):
            t_bb = None
            for j in range(8):
                t_bb = transpose(P_aux[:, 2, j:j + 1],
                                 row[0:1, j * 128:(j + 1) * 128],
                                 ones_f[0:1, 0:1],
                                 wait=([("dS", pro_dS), ("V", v_scbc),
                                        g("p_aux")] if j == 0 else []),
                                 inc=(("T", 1) if j == 7 else None))
            ev_bb = evac_copy(dstf[:], P_aux[:, 2, 0:8], wait=[("T", t_bb)],
                              est=300)
            rec("p_aux", ev_bb)

        for w in ("wq", "wk", "wv", "wo", "wf"):
            for oc in range(8):
                buf = SQ_t if oc % 2 == 0 else lnat_t
                dv = dma("sync", f"wl{oc % 2}", buf[:],
                         w_d[w][oc * 128:(oc + 1) * 128, :],
                         wait=[g(("wbuf", oc % 2))])
                for half in range(2):
                    tv = None
                    for kc in range(4 * half, 4 * half + 4):
                        tv = transpose(
                            P_big[:, half, (kc % 4) * 128:(kc % 4) * 128 + 128],
                            buf[:, kc * 128:(kc + 1) * 128], identf[:],
                            wait=([(f"wl{oc % 2}", dv), g(("wps", half))]
                                  if kc % 4 == 0 else []),
                            inc=(("T", 1) if kc % 4 == 3 else None))
                    src = P_big[:, half, :].rearrange("p (a c) -> p a c", a=4)
                    if w == "wf":
                        ev = evac(None, lambda e, hh=half, occ=oc, s=src:
                                  e.activation(
                                      Wf8[:, 4 * hh:4 * hh + 4,
                                          occ * 128:(occ + 1) * 128],
                                      s, ACT.Identity, scale=0.5),
                                  wait=[("T", tv)], est=600, acc=accr(src))
                    else:
                        ev = evac(lambda e, ww=w, hh=half, occ=oc, s=src:
                                  e.tensor_copy(
                                      Wf[ww][:, 4 * hh:4 * hh + 4,
                                             occ * 128:(occ + 1) * 128], s),
                                  lambda e, ww=w, hh=half, occ=oc, s=src:
                                  e.activation(
                                      Wf[ww][:, 4 * hh:4 * hh + 4,
                                             occ * 128:(occ + 1) * 128],
                                      s, ACT.Identity),
                                  wait=[("T", tv)], est=600, acc=accr(src))
                    rec(("wps", half), ev)
                rec(("wbuf", oc % 2), ("T", b.v("T")))
        rec(("pbig", 0), g(("wps", 0))[0])
        rec(("pbig", 1), g(("wps", 1))[0])

        def ln_full(x_ap, sm, out_ap, sq_scr, wait_pre=(), scale_extra=None,
                    sfin_fold=False):
            wp = list(wait_pre)
            parts = x_ap if isinstance(x_ap, list) else None
            if parts is None:
                vop(lambda e, xa=x_ap, s=sm: e.reduce_sum(
                    s[:, 0:1], xa, axis=AX.X, negate=True),
                    wait=wp, est=(140 + D) / 0.96)
                vdrain()
                vop(lambda e, s=sm: e.tensor_scalar(
                    s[:, 0:1], s[:, 0:1], 1.0 / D, None, op0=OP.mult), est=140)
                v_mean = vdrain(inc=("V", 1))
                sop(lambda e, xa=x_ap, s=sm, q=sq_scr: e.activation(
                    q[:], xa, ACT.Square, accum_out=s[:, 1:2]),
                    wait=wp, est=(200 + D) / 1.2)
                s_sq = sdrain(inc=("S", 1))
            else:
                vop(lambda e, xa=parts[0][0], s=sm: e.reduce_sum(
                    s[:, 0:1], xa, axis=AX.X, negate=True),
                    wait=wp, est=(140 + D / 2) / 0.96, acc=accr(parts[0][0]))
                v_r0 = vdrain(inc=("V", 1))
                sop(lambda e, xa=parts[1][0], s=sm, q=sq_scr: e.activation(
                    q[:, D // 2:D], xa, ACT.Square, accum_out=s[:, 7:8]),
                    wait=wp, est=(200 + D / 2) / 1.2, acc=accr(parts[1][0]))
                s_sq2 = sdrain(inc=("S", 1))
                vop(lambda e, xa=parts[1][0], s=sm: e.reduce_sum(
                    s[:, 6:7], xa, axis=AX.X, negate=True),
                    wait=[("S", s_sq2)], est=(140 + D / 2) / 0.96,
                    acc=accr(parts[1][0]))
                vdrain()
                vop(lambda e, s=sm: e.tensor_tensor(
                    s[:, 0:1], s[:, 0:1], s[:, 6:7], op=OP.add), est=140)
                vdrain()
                vop(lambda e, s=sm: e.tensor_scalar(
                    s[:, 0:1], s[:, 0:1], 1.0 / D, None, op0=OP.mult), est=140)
                v_mean = vdrain(inc=("V", 1))
                sop(lambda e, xa=parts[0][0], s=sm, q=sq_scr: e.activation(
                    q[:, 0:D // 2], xa, ACT.Square, accum_out=s[:, 1:2]),
                    wait=[("V", v_r0)], est=(200 + D / 2) / 1.2,
                    acc=accr(parts[0][0]))
                s_sq1 = sdrain(inc=("S", 1))
                vop(lambda e, s=sm: e.tensor_tensor(
                    s[:, 1:2], s[:, 1:2], s[:, 7:8], op=OP.add),
                    wait=[("S", s_sq1)], est=140)
                vdrain()
                s_sq = sdrain(inc=("S", 1))
            vop(lambda e, s=sm: e.tensor_tensor(s[:, 4:5], s[:, 0:1], s[:, 0:1],
                                                op=OP.mult),
                wait=[("V", v_mean), ("S", s_sq)], est=140)
            vdrain()
            vdrain()
            vop(lambda e, s=sm: e.scalar_tensor_tensor(
                s[:, 5:6], s[:, 1:2], 1.0 / D, s[:, 4:5],
                op0=OP.mult, op1=OP.subtract), est=140)
            vdrain()
            if sfin_fold:
                vop(lambda e, s=sm: e.tensor_tensor(s[:, 5:6], s[:, 5:6],
                                                    sc_bc[:, 1:2], op=OP.mult),
                    est=140)
                vdrain()
            vop(lambda e, s=sm: e.tensor_scalar(s[:, 5:6], s[:, 5:6], EPS, None,
                                                op0=OP.add), est=140)
            v_var = vdrain(inc=("V", 1))
            sop(lambda e, s=sm: e.activation(s[:, 4:5], s[:, 5:6], ACT.Ln),
                wait=[("V", v_var)], est=350)
            sdrain()
            sop(lambda e, s=sm: e.activation(s[:, 2:3], s[:, 4:5], ACT.Exp,
                                             scale=-0.5), est=350)
            s_r = sdrain(inc=("S", 1))
            v_w = [("S", s_r)]
            if scale_extra is not None:
                vop(lambda e, s=sm, x2=scale_extra: e.tensor_tensor(
                    s[:, 2:3], s[:, 2:3], x2, op=OP.mult), wait=v_w, est=140)
                vdrain()
                v_w = []
            if sfin_fold:
                vop(lambda e, s=sm: e.tensor_tensor(s[:, 2:3], s[:, 2:3],
                                                    sc_bc[:, 0:1], op=OP.mult),
                    wait=v_w, est=140)
                vdrain()
                v_w = []
            vop(lambda e, s=sm: e.tensor_tensor(s[:, 3:4], s[:, 0:1], s[:, 2:3],
                                                op=OP.mult), wait=v_w, est=140)
            v_n = vdrain(inc=("V", 1))
            if parts is None:
                sop(lambda e, xa=x_ap, s=sm, o=out_ap: e.activation(
                    o, xa, ACT.Identity, scale=s[:, 2:3], bias=s[:, 3:4]),
                    wait=[("V", v_n)], est=(200 + D) / 1.2)
            else:
                sop(lambda e, xa=parts[0][0], s=sm, o=parts[0][1]: e.activation(
                    o, xa, ACT.Identity, scale=s[:, 2:3], bias=s[:, 3:4]),
                    wait=[("V", v_n)], est=(200 + D / 2) / 1.2,
                    acc=accr(parts[0][0]))
                sop(lambda e, xa=parts[1][0], s=sm, o=parts[1][1]: e.activation(
                    o, xa, ACT.Identity, scale=s[:, 2:3], bias=s[:, 3:4]),
                    est=(200 + D / 2) / 1.2, acc=accr(parts[1][0]))
            s_a = sdrain(inc=("S", 1))
            return ("S", s_a)

        def ln_psum(sm, out_ap, wait_pre=(), sfin_fold=False):
            wp = list(wait_pre)
            v1 = vop(lambda e, s=sm: e.reduce_sum(
                s[:, 0:2], P_big[:, 0:2, :], axis=AX.X, negate=True),
                wait=wp, inc=("V", 1), est=(140 + D) / 0.96,
                acc=accr(P_big[:, 0, :], P_big[:, 1, :]))
            sop(lambda e, s=sm: e.activation(
                SQ_t[:, 0:512], P_big[:, 0, :], ACT.Square,
                accum_out=s[:, 2:3]),
                wait=[("V", v1)], est=(200 + 512) / 1.2,
                acc=accr(P_big[:, 0, :]))
            sdrain()
            s_sq = sop(lambda e, s=sm: e.activation(
                SQ_t[:, 512:1024], P_big[:, 1, :], ACT.Square,
                accum_out=s[:, 3:4]),
                inc=("S", 1), est=(200 + 512) / 1.2,
                acc=accr(P_big[:, 1, :]))
            vdrain()
            vop(lambda e, s=sm: e.tensor_tensor(s[:, 0:1], s[:, 0:1],
                                                s[:, 1:2], op=OP.add), est=140)
            vdrain()
            vop(lambda e, s=sm: e.tensor_scalar(s[:, 0:1], s[:, 0:1], 1.0 / D,
                                                None, op0=OP.mult), est=140)
            vdrain()
            vop(lambda e, s=sm: e.tensor_tensor(s[:, 4:5], s[:, 0:1],
                                                s[:, 0:1], op=OP.mult), est=140)
            vop(lambda e, s=sm: e.tensor_tensor(s[:, 2:3], s[:, 2:3],
                                                s[:, 3:4], op=OP.add),
                wait=[("S", s_sq)], est=140)
            vdrain()
            vop(lambda e, s=sm: e.scalar_tensor_tensor(
                s[:, 5:6], s[:, 2:3], 1.0 / D, s[:, 4:5],
                op0=OP.mult, op1=OP.subtract), est=140)
            if sfin_fold:
                vdrain()
                vop(lambda e, s=sm: e.tensor_tensor(s[:, 5:6], s[:, 5:6],
                                                    sc_bc[:, 1:2], op=OP.mult),
                    est=140)
            v_var = vdrain(inc=("V", 1))
            sop(lambda e, s=sm: e.activation(s[:, 4:5], s[:, 5:6], ACT.Ln,
                                             bias=EPS),
                wait=[("V", v_var)], est=350)
            sdrain()
            s_r = sop(lambda e, s=sm: e.activation(s[:, 6:7], s[:, 4:5],
                                                   ACT.Exp, scale=-0.5),
                      inc=("S", 1), est=350)
            if sfin_fold:
                vop(lambda e, s=sm: e.tensor_tensor(s[:, 6:7], s[:, 6:7],
                                                    sc_bc[:, 0:1], op=OP.mult),
                    wait=[("S", s_r)], est=140)
                vdrain()
            v_b = vop(lambda e, s=sm: e.tensor_tensor(s[:, 0:1], s[:, 0:1],
                                                      s[:, 6:7], op=OP.mult),
                      wait=([] if sfin_fold else [("S", s_r)]),
                      inc=("V", 1), est=140)
            s_a = sop(lambda e, s=sm, o=out_ap: e.activation(
                o.rearrange("p (a c) -> p a c", a=2), P_big[:, 0:2, :],
                ACT.Identity, scale=s[:, 6:7], bias=s[:, 0:1]),
                wait=[("V", v_b)], inc=("S", 1), est=(200 + D) / 1.2,
                acc=accr(P_big[:, 0, :], P_big[:, 1, :]))
            return ("S", s_a)

        def ln_sbuf(x_ap, sm, out_ap, wait_pre=(), scale_extra=None,
                    have_sum=False):
            wp = list(wait_pre)
            if not have_sum:
                vop(lambda e, xa=x_ap, s=sm: e.reduce_sum(
                    s[:, 0:1], xa, axis=AX.X, negate=True),
                    wait=wp, est=(140 + D) / 0.96)
            s_sq = sop(lambda e, xa=x_ap, s=sm: e.activation(
                SQ_t[:], xa, ACT.Square, accum_out=s[:, 2:3]),
                wait=wp, inc=("S", 1), est=(200 + D) / 1.2)
            if not have_sum:
                vdrain()
            sgn = -1.0 if have_sum else 1.0
            vop(lambda e, s=sm, sg=sgn: e.tensor_scalar(
                s[:, 0:1], s[:, 0:1], sg / D, None, op0=OP.mult),
                wait=(wp if have_sum else []), est=140)
            vdrain()
            vop(lambda e, s=sm: e.tensor_tensor(s[:, 4:5], s[:, 0:1],
                                                s[:, 0:1], op=OP.mult), est=140)
            vdrain()
            vop(lambda e, s=sm: e.scalar_tensor_tensor(
                s[:, 5:6], s[:, 2:3], 1.0 / D, s[:, 4:5],
                op0=OP.mult, op1=OP.subtract),
                wait=[("S", s_sq)], est=140)
            v_var = vdrain(inc=("V", 1))
            sop(lambda e, s=sm: e.activation(s[:, 4:5], s[:, 5:6], ACT.Ln,
                                             bias=EPS),
                wait=[("V", v_var)], est=350)
            sdrain()
            s_r = sop(lambda e, s=sm: e.activation(s[:, 6:7], s[:, 4:5],
                                                   ACT.Exp, scale=-0.5),
                      inc=("S", 1), est=350)
            v_w = [("S", s_r)]
            if scale_extra is not None:
                vop(lambda e, s=sm, x2=scale_extra: e.tensor_tensor(
                    s[:, 6:7], s[:, 6:7], x2, op=OP.mult), wait=v_w, est=140)
                vdrain()
                v_w = []
            v_b = vop(lambda e, s=sm: e.tensor_tensor(s[:, 0:1], s[:, 0:1],
                                                      s[:, 6:7], op=OP.mult),
                      wait=v_w, inc=("V", 1), est=140)
            s_a = sop(lambda e, xa=x_ap, s=sm, o=out_ap: e.activation(
                o, xa, ACT.Identity, scale=s[:, 6:7], bias=s[:, 0:1]),
                wait=[("V", v_b)], inc=("S", 1), est=(200 + D) / 1.2)
            return ("S", s_a)

        def proj_fm(w, out_region, oc, rhs_fn, wl):
            t = None
            for kc in range(8):
                t = mm(out_region, Wf[w][:, kc, oc * 128:(oc + 1) * 128],
                       rhs_fn(kc), kc == 0, kc == 7,
                       wait=(wl if kc == 0 else []),
                       inc=(("T", 1) if kc == 7 else None), est=115)
            return t

        pro_fm = xfm[0]
        for gc in range(2):
            s_a = ln_full(gath[:, gc, :], SM[gc], ln1o2[:, gc, :], SQ[gc],
                          wait_pre=[("dS", pro_dS)])
            t_t = None
            for j in range(8):
                t_t = transpose(P_tr[:, j, :], ln1o2[:, gc, j * 128:(j + 1) * 128],
                                ident[:], wait=([s_a, g("ptr")] if j == 0 else []),
                                inc=(("T", 1) if j == 7 else None))
            ev = evac_copy(pro_fm[:, :, gc * 128:(gc + 1) * 128], P_tr[:],
                           wait=[("T", t_t)], est=680)
            rec("ptr", ev)
            rec(("profm", gc), ev)
        for oc in range(8):
            bank = oc % 2
            wl = [g(("pbig", bank)), g(("profm", 0)), g(("profm", 1))]
            t_mm = proj_fm("wk", P_big[:, bank, 0:256], oc,
                           lambda kc, f=pro_fm: f[:, kc, 0:256], wl)
            ev1 = evac_bias(K0f[:, oc, :], P_big[:, bank, 0:64],
                            bkf[:, oc:oc + 1], wait=[("T", t_mm)], est=300)
            ev2 = evac_bias(K127f[:, oc, :], P_big[:, bank, 64:128],
                            bkf[:, oc:oc + 1], wait=[("T", t_mm)], est=300)
            ev3 = evac_bias(gkf[:, oc, :], P_big[:, bank, 128:256],
                            bkf[:, oc:oc + 1], wait=[("T", t_mm)], est=300)
            rec(("pbig", bank), ev1, ev2, ev3)
        for oc in range(8):
            bank = oc % 2
            t_mm = proj_fm("wq", P_big[:, bank, 0:128], oc,
                           lambda kc, f=pro_fm: f[:, kc, 128:256],
                           [g(("pbig", bank))])
            ev = evac_bias(qgf[:, oc, :], P_big[:, bank, 0:128],
                           bqf[:, oc:oc + 1], wait=[("T", t_mm)], est=300)
            rec(("pbig", bank), ev)
        for gc in range(2):
            for half in range(2):
                bank = half
                wl = [g(("pbig", bank))]
                for kc in range(8):
                    mm(P_big[:, bank, :], pro_fm[:, kc, gc * 128:(gc + 1) * 128],
                       Wf["wv"][:, kc, half * 512:(half + 1) * 512],
                       kc == 0, False, wait=(wl if kc == 0 else []), est=230)
                t_mm = mm(P_big[:, bank, :], ones_b[:],
                          bvrow[:, half * 512:(half + 1) * 512], False, True,
                          inc=("T", 1), est=230)
                src = P_big[:, bank, :].rearrange("p (h v) -> p h v", v=64)
                if gc == 0:
                    ev1 = evac_copy(V0x[0:64, half * 8:(half + 1) * 8, 0:64],
                                    src[0:64], wait=[("T", t_mm)], est=300)
                    ev2 = evac_copy(V127x[64:128, half * 8:(half + 1) * 8, 0:64],
                                    src[64:128], wait=[("T", t_mm)], est=300)
                    rec(("pbig", bank), ev1, ev2)
                else:
                    ev = evac_copy(gvx[:, half * 8:(half + 1) * 8, 0:64], src[:],
                                   wait=[("T", t_mm)], est=600)
                    rec(("pbig", bank), ev)
        pro_done = [("T", b.v("T")), ("V", b.v("V")), ("S", b.v("S"))]
        rec(("pro",), *pro_done)
        rec(("pt", 0), ("V", b.v("V")))
        rec(("pt", 1), ("V", b.v("V")))
        rec(("xfree_pro",), ("T", b.v("T")), ("S", b.v("S")), ("V", b.v("V")))
        rec(("ln1o_free",), ("T", b.v("T")))
        rec(("paux_free",), *pro_done)
        rec(("xfm_free",), ("T", b.v("T")))
        for ch in range(2):
            rec(("sq_free", ch), ("S", b.v("S")), ("T", b.v("T")))
            rec(("lnat_free", ch), ("T", b.v("T")))
            rec(("xp_free", ch), ("T", b.v("T")), ("S", b.v("S")))

        def load_x(i):
            if i >= NT:
                return
            slot = i % 2
            w = []
            if i < 2:
                if slot == 0:
                    w.append(g(("xfree_pro",)))
            else:
                w.append(g(("xfree", slot)))
            dv = dma("sync", f"xl{slot}", xbuf[slot][:],
                     x_d[i * TILE:(i + 1) * TILE, :].rearrange(
                         "(c p) d -> p c d", p=128), wait=w)
            rec(("xload", i), (f"xl{slot}", dv))

        def stage_A_gen(i):
            par, slot = i % 2, i % 3
            wl = [g(("xload", i)), g(("sma_free",))]
            if i < 2:
                wl.append(g(("pro",)))
            vop(lambda e: e.reduce_sum(SMA[:, 0:2], xbuf[par][:, 0:2, :],
                                       axis=AX.X, negate=True),
                wait=wl, est=(140 + 2 * D) / 0.96)
            sop(lambda e: e.activation(SQ_t[:], xbuf[par][:, 0, :], ACT.Square,
                                       accum_out=SMA[:, 2:3]),
                wait=wl, est=(200 + D) / 1.2)
            sdrain()
            s_sq = sop(lambda e: e.activation(SQ_t[:], xbuf[par][:, 1, :],
                                              ACT.Square, accum_out=SMA[:, 3:4]),
                       inc=("S", 1), est=(200 + D) / 1.2)
            vdrain()
            vop(lambda e: e.tensor_scalar(SMA[:, 0:2], SMA[:, 0:2], 1.0 / D,
                                          None, op0=OP.mult), est=140)
            vdrain()
            vop(lambda e: e.tensor_tensor(SMA[:, 4:6], SMA[:, 0:2], SMA[:, 0:2],
                                          op=OP.mult), est=140)
            vdrain()
            v_var = vop(lambda e: e.scalar_tensor_tensor(
                SMA[:, 6:8], SMA[:, 2:4], 1.0 / D, SMA[:, 4:6],
                op0=OP.mult, op1=OP.subtract),
                wait=[("S", s_sq)], inc=("V", 1), est=140)
            sop(lambda e: e.activation(SMA[:, 4:6], SMA[:, 6:8], ACT.Ln,
                                       bias=EPS), wait=[("V", v_var)], est=350)
            sdrain()
            s_r = sop(lambda e: e.activation(SMA[:, 6:8], SMA[:, 4:6], ACT.Exp,
                                             scale=-0.5), inc=("S", 1), est=350)
            v_b = vop(lambda e: e.tensor_tensor(SMA[:, 0:2], SMA[:, 0:2],
                                                SMA[:, 6:8], op=OP.mult),
                      wait=[("S", s_r)], inc=("V", 1), est=140)
            s_ap = []
            for ch in range(2):
                s_ap.append(sop(
                    lambda e, cc=ch: e.activation(
                        ln1o2[:, cc, :], xbuf[par][:, cc, :], ACT.Identity,
                        scale=SMA[:, 6 + cc:7 + cc], bias=SMA[:, cc:cc + 1]),
                    wait=([("V", v_b), g(("ln1o_free",))] if ch == 0 else []),
                    inc=("S", 1), est=(200 + D) / 1.2))
            rec(("sma_free",), ("S", s_ap[1]), ("V", v_b))
            yield
            for ch in range(2):
                t_t = None
                for j in range(8):
                    t_t = transpose(P_tr[:, j, :],
                                    ln1o2[:, ch, j * 128:(j + 1) * 128],
                                    ident[:],
                                    wait=([("S", s_ap[ch]), g("ptr")]
                                          if j == 0 else []),
                                    inc=(("T", 1) if j == 7 else None))
                ev = evac_copy(xfm[par][:, :, ch * 128:(ch + 1) * 128], P_tr[:],
                               wait=[("T", t_t)], est=680)
                rec("ptr", ev)
                rec(("xfm_ready", par, ch), ev)
                yield
            rec(("ln1o_free",), ("T", b.v("T")))
            for (w, kind) in (("wq", "q"), ("wk", "k")):
                for oc in range(8):
                    bank = oc % 2
                    wl = [g(("pbig", bank)), g(("xfm_ready", par, 0)),
                          g(("xfm_ready", par, 1))]
                    t_mm = proj_fm(w, P_big[:, bank, 0:TILE], oc,
                                   lambda kc, f=xfm[par]: f[:, kc, :], wl)
                    if kind == "q":
                        ev = evac_bias(qf[par][:, oc, :], P_big[:, bank, 0:TILE],
                                       bqf[:, oc:oc + 1], wait=[("T", t_mm)],
                                       est=(200 + TILE) / 1.2)
                    else:
                        ev = evac_bias(Kr[slot][:, oc, 64:320],
                                       P_big[:, bank, 0:TILE],
                                       bkf[:, oc:oc + 1], wait=[("T", t_mm)],
                                       est=(200 + TILE) / 1.2)
                    rec(("pbig", bank), ev)
                    if oc % 2 == 1:
                        yield
                rec(("qready", i) if kind == "q" else ("kready", i),
                    ("S", b.v("S")), ("V", b.v("V")))
            src_c1 = (K127f[:, :, :] if i == 0
                      else Kr[(i - 1) % 3][:, :, 256:320])
            cw = [g(("kr_a_free", slot))]
            cw.append(g(("pro",)) if i == 0 else g(("kready", i - 1)))
            c1 = vop(lambda e, s=src_c1: e.tensor_copy(Kr[slot][:, :, 0:64], s),
                     wait=cw, inc=("V", 1), est=250)
            rec(("kr_c1", i), ("V", c1))
            for ch in range(2):
                for half in range(2):
                    bank = half
                    wl = [g(("pbig", bank)), g(("xfm_ready", par, ch))]
                    for kc in range(8):
                        mm(P_big[:, bank, :],
                           xfm[par][:, kc, ch * 128:(ch + 1) * 128],
                           Wf["wv"][:, kc, half * 512:(half + 1) * 512],
                           kc == 0, False, wait=(wl if kc == 0 else []), est=230)
                    t_mm = mm(P_big[:, bank, :], ones_b[:],
                              bvrow[:, half * 512:(half + 1) * 512], False, True,
                              inc=("T", 1), est=230)
                    src = P_big[:, bank, :].rearrange("p (h v) -> p h v", v=64)
                    ev = evac_copy(Vr[slot][:, ch, half * 8:(half + 1) * 8, 0:64],
                                   src[:], wait=[("T", t_mm)], est=680)
                    rec(("pbig", bank), ev)
                    yield
            rec(("xfm_free",), ("T", b.v("T")))
            rec(("vready", i), ("S", b.v("S")), ("V", b.v("V")))

        def vblk(b0):
            if b0 < 0:
                return V127x[64:128, :, :]
            if b0 >= NB:
                return V0x[0:64, :, :]
            tb, vp = (b0 // 4) % 3, (b0 % 2) * 64
            return Vr[tb][vp:vp + 64, (b0 % 4) // 2, :, :]

        def build_U(j, wl):
            s = j % 3
            vop(lambda e, src=vblk(2 * j - 1): e.tensor_copy(
                VEU[s][0:64, :, :], src),
                wait=list(wl) + [g(("veu_free", s))], est=330)
            v2 = vop(lambda e, src=vblk(2 * j): e.tensor_copy(
                VEU[s][64:128, :, :], src), inc=("V", 1), est=330)
            rec(("veu", j), ("V", v2))

        def waves_ch(i, ch):
            par, slot = i % 2, i % 3
            cg, off = 2 * i + ch, 128 * ch
            VEa, VEb = VEU[cg % 3], VEU[(cg + 1) % 3]
            NWV = H // 2
            n_extra = 4 if ch == 1 else 0
            se_rec, evn_rec = {}, {}
            for w in range(NWV + 2 + n_extra):
                if ch == 1 and NWV + 2 <= w:
                    for sp in (2 * (w - NWV - 2), 2 * (w - NWV - 2) + 1):
                        s_r = sp % 2
                        Pr = PS[s_r]
                        t_m = None
                        for p in range(2):
                            h = 2 * sp + p
                            pt = (h % 2) * 64
                            wl = [g(("psc_free", s_r))] if p == 0 else []
                            t_m = mm(Pr[:, p, 0:2, :].rearrange(
                                "p a q -> p (a q)"),
                                qgf[pt:pt + 64, h // 2, :],
                                Kr[slot][pt:pt + 64, h // 2, 64:320],
                                True, True, wait=wl,
                                inc=(("T", 1) if p == 1 else None),
                                est=200, acc=accw(Pr[:, p, 0:2, :]), skip=True)
                        v_red = vop(lambda e, P_=Pr, s_=sp: e.reduce_max(
                            RS8[:, 2 * s_:2 * s_ + 2],
                            P_[:, 0:2, 0:2, :].rearrange("p a b q -> p a (b q)"),
                            axis=AX.X),
                            wait=[("T", t_m)], inc=("V", 1),
                            est=(140 + 512) / 0.96,
                            acc=accr(Pr[:, 0:2, 0:2, :]))
                        rec(("psc_free", s_r), ("V", v_red))
                    yield
                    continue
                if w >= 2:
                    wb = w - 2
                    sb_ = wb % 2
                    t_pv = None
                    for p in range(2):
                        h = 2 * wb + p
                        wl = [("S", se_rec[wb][p])]
                        if p == 0:
                            wl.append(g(("paux_free",)))
                            if wb == 0:
                                wl += [g(("veu", cg)), g(("veu", cg + 1))]
                        mm(P_aux[:, 2 + p, 0:65], PT[:, sb_, p, 2, :],
                           gvx[:, h, :], True, True, wait=wl, est=120,
                           acc=accw(P_aux[:, 2 + p, 0:65]), skip=True)
                        mm(P_aux[:, p, 0:65], PT[:, sb_, p, 0, :],
                           VEa[:, h, :], True, False, est=120,
                           acc=accw(P_aux[:, p, 0:65]), skip=True)
                        t_pv = mm(P_aux[:, p, 0:65], PT[:, sb_, p, 1, :],
                                  VEb[:, h, :], False, True,
                                  inc=(("T", 1) if p == 1 else None), est=120,
                                  acc=accw(P_aux[:, p, 0:65]), skip=True)
                    wl = [("T", t_pv)]
                    if wb == 0:
                        wl.append(g(("ctxlc_free",)))
                    vop(lambda e, w_=wb: e.tensor_copy(
                        CTXlc[:, 2 * w_:2 * w_ + 2, 0, :],
                        P_aux[:, 0:2, 0:65]), wait=wl, est=280,
                        acc=accr(P_aux[:, 0:2, 0:65]))
                    v_ev = vop(lambda e, w_=wb: e.tensor_copy(
                        CTXlc[:, 2 * w_:2 * w_ + 2, 1, :],
                        P_aux[:, 2:4, 0:65]), inc=("V", 1), est=280,
                        acc=accr(P_aux[:, 2:4, 0:65]))
                    rec(("paux_free",), ("V", v_ev))
                if 1 <= w and w - 1 < NWV:
                    wm = w - 1
                    s = wm % 2
                    Pw = PS[s]
                    t_n = transpose(Pw[0:2, 0, 3, :], SMW[:, 2 * s:2 * s + 2],
                                    identf[:], wait=[g(("vmx", s))],
                                    inc=("T", 1))
                    rec(("smw_free", s), ("T", t_n))
                    evn_rec[wm] = vop(lambda e, s_=s, P_=Pw: e.tensor_copy(
                        nfm2[0:2, s_, :, :],
                        P_[0:2, 0, 3:4, :].broadcast_to([2, 3, 128])),
                        wait=[("T", t_n), g(("nfm_free", s))], inc=("V", 1),
                        est=450, acc=accr(Pw[0:2, 0, 3, :]))
                if w < NWV:
                    s = w % 2
                    Pw = PS[s]
                    t_qm = None
                    for p in range(2):
                        h = 2 * w + p
                        part, hc = (h % 2) * 64, h // 2
                        q_h = qf[par][part:part + 64, hc, off:off + 128]
                        wl = []
                        if p == 0:
                            wl.append(g(("psc_free", s)))
                            if w == 0:
                                wl += [g(("qready", i)), g(("pro",))]
                                if ch == 0:
                                    wl += [g(("kready", i)), g(("kr_c1", i))]
                                else:
                                    wl += [g(("kr_c2", i))]
                        mm(Pw[:, p, 0:2, :].rearrange("p a q -> p (a q)"), q_h,
                           Kr[slot][part:part + 64, hc, off:off + 256],
                           True, True, wait=wl, est=180,
                           acc=accw(Pw[:, p, 0:2, :]), skip=True)
                        t_qm = mm(Pw[:, p, 2, :], q_h,
                                  gkf[part:part + 64, hc, :], True, True,
                                  inc=(("T", 1) if p == 1 else None), est=120,
                                  acc=accw(Pw[:, p, 2, :]), skip=True)
                    v_mx = vop(lambda e, s_=s, P_=Pw: e.reduce_max(
                        SMW[:, 2 * s_:2 * s_ + 2],
                        P_[:, 0:2, 0:3, :].rearrange("p a b q -> p a (b q)"),
                        axis=AX.X, negate=True),
                        wait=[("T", t_qm), g(("smw_free", s))], inc=("V", 1),
                        est=(140 + 768) / 0.96, acc=accr(Pw[:, 0:2, 0:3, :]))
                    rec(("vmx", s), ("V", v_mx))
                if 1 <= w and w - 1 < NWV:
                    wm = w - 1
                    s = wm % 2
                    Pw = PS[s]
                    t_r1 = {}
                    for p in range(2):
                        h = 2 * wm + p
                        part, hc = (h % 2) * 64, h // 2
                        q_mv = qf[par][part:part + 64, hc, off:off + 128]
                        mm(Pw[:, p, 0:3, :].rearrange("p a q -> p (a q)"),
                           SEL[:, p, :],
                           nfm2[0:4, s, :, :].rearrange("a b q -> a (b q)"),
                           True, False,
                           wait=([("V", evn_rec[wm]), g(("vmx", s))]
                                 if p == 0 else []),
                           est=230, acc=accw(Pw[:, p, 0:3, :]), skip=True)
                        mm(Pw[:, p, 0, :],
                           Kr[slot][part:part + 64, hc, off:off + 128], q_mv,
                           False, False, est=120,
                           acc=accw(Pw[:, p, 0, :]), skip=True)
                        mm(Pw[:, p, 1, :],
                           Kr[slot][part:part + 64, hc, off + 128:off + 256],
                           q_mv, False, False, est=120,
                           acc=accw(Pw[:, p, 1, :]), skip=True)
                        t_r1[p] = mm(Pw[:, p, 2, :], gkf[part:part + 64, hc, :],
                                     q_mv, False, True, inc=("T", 1), est=120,
                                     acc=accw(Pw[:, p, 2, :]), skip=True)
                    rec(("nfm_free", s), ("T", t_r1[1]))
                    se_p = {}
                    for p in range(2):
                        se_p[p] = sop(lambda e, s_=s, P_=Pw, pp=p: e.activation(
                            PT[:, s_, pp, 0:3, :], P_[:, pp, 0:3, :],
                            ACT.Exp, scale=SCALE),
                            wait=[("T", t_r1[p])], inc=("S", 1), est=780,
                            acc=accr(Pw[:, p, 0:3, :]))
                    se_rec[wm] = se_p
                    rec(("psc_free", s), ("S", se_p[1]))
                yield
            vdrain()
            vop(lambda e: e.reciprocal(RLD[:, :, :],
                                       CTXlc[:, :, :, 64:65]), est=250)
            vdrain()
            vop(lambda e: e.tensor_tensor(
                CTXlc[:, :, 0, 0:64], CTXlc[:, :, 0, 0:64],
                RLD[:, :, 0:1].broadcast_to([128, 16, 64]), op=OP.mult),
                est=1100)
            vop(lambda e: e.tensor_tensor(
                CTXlc[:, :, 1, 0:64], CTXlc[:, :, 1, 0:64],
                RLD[:, :, 1:2].broadcast_to([128, 16, 64]), op=OP.mult),
                est=1100)
            v_dv = vdrain(inc=("V", 1))
            p_c = b.op("gpsimd", lambda e: e.tensor_add(
                CTXt_t[:], CTXlc[:, :, 0, 0:64], CTXlc[:, :, 1, 0:64]),
                wait=[("V", v_dv), g(("ctxT_done", 0)), g(("ctxT_done", 1))],
                inc=("P", 1), est=2100)
            rec(("ctxlc_free",), ("P", p_c))
            rec(("ctx_ready", ch), ("P", p_c))
            if ch == 0:
                rec(("kr_a_free", slot), ("T", b.v("T")))
            else:
                rec(("kr_b_free", slot), ("T", b.v("T")))
            rec(("veu_free", cg % 3), ("T", b.v("T")))

        def emit_c2_u(i):
            slot = i % 3
            src_c2 = (K0f[:, :, :] if i + 1 >= NT
                      else Kr[(i + 1) % 3][:, :, 64:128])
            cw = [g(("kr_b_free", slot))]
            cw.append(g(("pro",)) if i + 1 >= NT else g(("kready", i + 1)))
            c2 = vop(lambda e, s=src_c2: e.tensor_copy(
                Kr[slot][:, :, 320:384], s), wait=cw, inc=("V", 1), est=250)
            rec(("kr_c2", i), ("V", c2))
            build_U(2 * i + 2,
                    [g(("vready", i)),
                     g(("pro",)) if i + 1 >= NT else g(("vready", i + 1))])

        def rows_gen(i):
            slot = i % 3
            vdrain()
            vop(lambda e: e.reduce_max(RS[:, 1:2], RS8[:, 0:16], axis=AX.X),
                est=140)
            vdrain()
            vop(lambda e: e.tensor_tensor(mnb[:], RS[:, 1:2], mrun[:],
                                          op=OP.max),
                wait=[g(("rs_s",))], est=140)
            vdrain()
            vop(lambda e: e.tensor_tensor(RS[:, 4:5], mrun[:], mnb[:],
                                          op=OP.subtract), est=140)
            vop(lambda e: e.tensor_scalar(RS[:, 3:4], mnb[:], -1.0,
                                          None, op0=OP.mult),
                wait=[g(("rs_rd",))], est=140)
            v_al = vdrain(inc=("V", 1))
            vop(lambda e: e.tensor_copy(mrun[:], mnb[:]), est=140)
            s_al = sop(lambda e: e.activation(RS[:, 2:3], RS[:, 4:5], ACT.Exp,
                                              scale=SCALE),
                       wait=[("V", v_al)], inc=("S", 1), est=350)
            rec(("rs_s",), ("S", s_al))
            v_rs = vop(lambda e: e.tensor_scalar(racc[:], racc[:], RS[:, 2:3],
                                                 None, op0=OP.mult),
                       wait=[("S", s_al)], inc=("V", 1), est=1200)
            t_nr = transpose(PS[0][0:1, 0, 3, :], RS[:, 3:4], identf[:],
                             wait=[("V", v_al), g(("psc_free", 0))],
                             inc=("T", 1))
            rec(("rs_rd",), ("T", t_nr))
            ev_n = sop(lambda e: e.copy(
                nfm2[0:1, 0, 0:2, :],
                PS[0][0:1, 0, 3:4, :].broadcast_to([1, 2, 128])),
                wait=[("T", t_nr), g(("nfm_free", 0))], inc=("S", 1), est=400,
                acc=accr(PS[0][0:1, 0, 3, :]))
            yield
            se_rec = {}
            v_a = None
            for w in range(H // 2 + 1):
                if w < H // 2:
                    s = w % 2
                    Pw = PS[s]
                    t_r1 = None
                    for p in range(2):
                        h = 2 * w + p
                        pt = (h % 2) * 64
                        wl = [("S", ev_n)]
                        if p == 0:
                            wl.append(g(("psc_free", s)))
                        mm(Pw[:, p, 0:2, :].rearrange("p a q -> p (a q)"),
                           ones_b[:],
                           nfm2[0:1, 0, 0:2, :].rearrange("a b q -> a (b q)"),
                           True, False, wait=wl, est=180,
                           acc=accw(Pw[:, p, 0:2, :]), skip=True)
                        mm(Pw[:, p, 0, :],
                           Kr[slot][pt:pt + 64, h // 2, 64:192],
                           qgf[pt:pt + 64, h // 2, :], False, False,
                           est=120, acc=accw(Pw[:, p, 0, :]), skip=True)
                        t_r1 = mm(Pw[:, p, 1, :],
                                  Kr[slot][pt:pt + 64, h // 2, 192:320],
                                  qgf[pt:pt + 64, h // 2, :], False, True,
                                  inc=(("T", 1) if p == 1 else None),
                                  est=120, acc=accw(Pw[:, p, 1, :]), skip=True)
                    if w >= H // 2 - 2:
                        rec(("nfm_free", 0), ("T", t_r1))
                    se_p = {}
                    for p in range(2):
                        se_p[p] = sop(lambda e, s_=s, P_=Pw, pp=p: e.activation(
                            PT[:, s_, pp, 0:2, :], P_[:, pp, 0:2, :], ACT.Exp,
                            scale=SCALE), wait=[("T", t_r1)], inc=("S", 1),
                            est=380, acc=accr(Pw[:, p, 0:2, :]))
                    se_rec[w] = se_p
                    rec(("psc_free", s), ("S", se_p[1]))
                if w >= 1:
                    wb = w - 1
                    sb_ = wb % 2
                    t_pv = None
                    for p in range(2):
                        h = 2 * wb + p
                        wl = [("S", se_rec[wb][p])]
                        if p == 0:
                            wl.append(g(("paux_free",)))
                        mm(P_aux[:, p, 0:65], PT[:, sb_, p, 0, :],
                           Vr[slot][:, 0, h, :], True, False, wait=wl,
                           est=120, acc=accw(P_aux[:, p, 0:65]), skip=True)
                        t_pv = mm(P_aux[:, p, 0:65], PT[:, sb_, p, 1, :],
                                  Vr[slot][:, 1, h, :], False, True,
                                  inc=(("T", 1) if p == 1 else None), est=120,
                                  acc=accw(P_aux[:, p, 0:65]), skip=True)
                    v_a = vop(lambda e, w_=wb: e.tensor_tensor(
                        racc[:, 2 * w_:2 * w_ + 2, :],
                        racc[:, 2 * w_:2 * w_ + 2, :],
                        P_aux[:, 0:2, 0:65], op=OP.add),
                        wait=[("T", t_pv)] +
                             ([("V", v_rs)] if wb == 0 else []),
                        inc=("V", 1), est=400, acc=accr(P_aux[:, 0:2, 0:65]))
                    rec(("paux_free",), ("V", v_a))
                yield
            rec("racc_rd", ("V", v_a))

        def stage_C_gen(i, ch, epilogue=False, out=None):
            par = 0 if epilogue else i % 2
            ctx_flat = CTXt[ch][:].rearrange("p h v -> p (h v)")
            t_t = None
            for j in range(8):
                t_t = transpose(P_tr[:, j, :], ctx_flat[:, j * 128:(j + 1) * 128],
                                ident[:],
                                wait=([g(("ctx_ready", ch)), g("ptr")]
                                      if j == 0 else []),
                                inc=(("T", 1) if j == 7 else None))
            ev = evac_copy(ctxf[ch][:], P_tr[:], wait=[("T", t_t)], est=680)
            rec("ptr", ev)
            rec(("ctxT_done", ch), ("T", b.v("T")))
            yield
            for half in range(2):
                bank = half
                wl = [ev, g(("pbig", bank))]
                for kc in range(8):
                    mm(P_big[:, bank, :], ctxf[ch][:, kc, :],
                       Wf["wo"][:, kc, half * 512:(half + 1) * 512],
                       kc == 0, False, wait=(wl if kc == 0 else []), est=230)
                t_wo = mm(P_big[:, bank, :], ones_b[:],
                          borow[:, half * 512:(half + 1) * 512], False, True,
                          inc=(("T", 1) if half == 1 else None), est=230)
            yield
            ev_o = sop(lambda e, cc=ch: e.activation(
                ln2s[cc][:].rearrange("p (a c) -> p a c", a=2),
                P_big[:, 0:2, :], ACT.Identity, accum_out=SM2[ch][:, 0:1]),
                wait=[("T", t_wo), g(("ln2s_free", 0)), g(("ln2s_free", 1)),
                      g(("sm2", ch))],
                inc=("S", 1), est=(200 + D) / 1.2,
                acc=accr(P_big[:, 0, :], P_big[:, 1, :]))
            rec(("pbig", 0), ("S", ev_o))
            rec(("pbig", 1), ("S", ev_o))
            s_a = ln_sbuf(ln2s[ch][:], SM2[ch], lnat[ch][:],
                          wait_pre=[("S", ev_o),
                                    g(("sq_free", ch)), g(("lnat_free", ch))],
                          have_sum=True)
            rec(("sq_free", ch), ("S", b.v("S")))
            yield
            xin = xg if epilogue else xbuf[par][:, ch, :]
            vop(lambda e, cc=ch, xi=xin: e.tensor_add(xp[cc][:], xi, lnat[cc][:]),
                wait=[s_a, g(("xp_free", 0)), g(("xp_free", 1)),
                      g(("sm2", ch))] +
                     ([g(("xg_ld",))] if epilogue else [g(("xload", i))]),
                est=(140 + D) / 0.96)
            v_xp = vdrain(inc=("V", 1))
            s_2 = ln_sbuf(xp[ch][:], SM2[ch], ln2s[ch][:],
                          wait_pre=[("V", v_xp), g(("ln2s_free", ch))],
                          scale_extra=sc_bc[:, 2:3])
            rec(("pbig", 0), s_a, s_2, ("V", v_xp))
            rec(("pbig", 1), s_a, s_2, ("V", v_xp))
            rec(("sq_free", ch), ("S", b.v("S")))
            rec(("lnat_free", ch), ("V", v_xp))
            yield
            pb1 = P_big[:, 1, :].rearrange("p (a c) -> p a c", a=4)
            t_a = None
            for j in range(4):
                t_a = transpose(pb1[:, j, :],
                                ln2s[ch][:, j * 128:(j + 1) * 128],
                                identf[:],
                                wait=([s_2, g(("pbig", 1))] if j == 0 else []),
                                inc=(("T", 1) if j == 3 else None))
            ev_a = evac(lambda e, cc=ch, s=pb1: e.tensor_scalar(
                xq8[cc][:, 0:4, :], s[:], CLIP_HALF, -CLIP_HALF,
                op0=OP.min, op1=OP.max), None, wait=[("T", t_a)], est=700,
                acc=accr(pb1[:]))
            rec(("pbig", 1), ev_a)
            pb0 = P_big[:, 0, :].rearrange("p (a c) -> p a c", a=4)
            t_b = None
            for j in range(4):
                t_b = transpose(pb0[:, j, :],
                                ln2s[ch][:, (4 + j) * 128:(5 + j) * 128],
                                identf[:],
                                wait=([g(("pbig", 0))] if j == 0 else []),
                                inc=(("T", 1) if j == 3 else None))
            ev_b = evac(lambda e, cc=ch, s=pb0: e.tensor_scalar(
                xq8[cc][:, 4:8, :], s[:], CLIP_HALF, -CLIP_HALF,
                op0=OP.min, op1=OP.max), None, wait=[("T", t_b)], est=700,
                acc=accr(pb0[:]))
            rec(("pbig", 0), ev_b)
            rec(("ln2s_free", ch), ("T", b.v("T")))
            yield
            for half in range(2):
                bank = half
                wl = [ev_a, ev_b, g(("pbig", bank))]
                for kp in range(4):
                    b.op("tensor", lambda e, cc=ch, k_=kp, hh=half, bb=bank:
                         e.matmul(P_big[:, bb, :], xq8[cc][:, 2 * k_:2 * k_ + 2, :],
                                  Wf8[:, 2 * k_:2 * k_ + 2,
                                      hh * 512:(hh + 1) * 512],
                                  start=(k_ == 0), stop=False,
                                  perf_mode=mybir.MatmulPerfMode.DoubleRow),
                         wait=(wl if kp == 0 else []), est=300,
                         acc=accw(P_big[:, bank, :]))
                t_ff = mm(P_big[:, bank, :], ones_b[:],
                          bfrow[:, half * 512:(half + 1) * 512], False, True,
                          inc=(("T", 1) if half == 1 else None), est=230)
            yield
            ev_f = sop(lambda e, cc=ch: e.activation(
                ln2s[cc][:].rearrange("p (a c) -> p a c", a=2),
                P_big[:, 0:2, :], ACT.Identity, scale=sc_bc[:, 0:1],
                accum_out=SM2[ch][:, 0:1]),
                wait=[("T", t_ff), g(("ln2s_free", ch)), g(("sm2", ch))],
                inc=("S", 1), est=(200 + D) / 1.2,
                acc=accr(P_big[:, 0, :], P_big[:, 1, :]))
            rec(("pbig", 0), ("S", ev_f))
            rec(("pbig", 1), ("S", ev_f))
            s_f = ln_sbuf(ln2s[ch][:], SM2[ch], lnat[ch][:],
                          wait_pre=[("S", ev_f), g(("lnat_free", ch)),
                                    g(("sq_free", ch))], have_sum=True)
            rec(("sq_free", ch), ("S", b.v("S")))
            rec(("sm2", ch), ("S", b.v("S")), ("V", b.v("V")))
            if epilogue:
                vop(lambda e, cc=ch: e.tensor_add(xg, xp[cc][:], lnat[cc][:]),
                    wait=[s_f], est=(140 + D) / 0.96)
                v_o = vdrain(inc=("V", 1))
                rec(("lnat_free", ch), ("V", v_o))
                rec(("xp_free", ch), ("V", v_o))
                rec(("pbig", 0), s_f, ("V", b.v("V")))
                rec(("pbig", 1), s_f, ("V", b.v("V")))
                if out is not None:
                    out["v_o"] = ("V", v_o)
                return
            vop(lambda e, cc=ch, pp=par: e.tensor_add(
                xbuf[pp][:, cc, :], xp[cc][:], lnat[cc][:]),
                wait=[s_f], est=(140 + D) / 0.96)
            v_o = vdrain(inc=("V", 1))
            rec(("lnat_free", ch), ("V", v_o))
            rec(("xp_free", ch), ("V", v_o))
            rec(("pbig", 0), s_f, ("V", b.v("V")))
            rec(("pbig", 1), s_f, ("V", b.v("V")))
            cg = 2 * i + ch
            dv = dma("gpsimd", f"st{par}", out_d[cg * 128:(cg + 1) * 128, :],
                     xbuf[par][:, ch, :], wait=[("V", v_o)])
            if ch == 1:
                rec(("xfree", i % 2), (f"st{par}", dv))

        def drain(gen):
            for _ in gen:
                pass

        def zip2(ga, gb):
            live = [x for x in (ga, gb) if x is not None]
            while live:
                for g_ in list(live):
                    try:
                        next(g_)
                    except StopIteration:
                        live.remove(g_)

        load_x(0)
        load_x(1)
        drain(stage_A_gen(0))
        for i in range(NT):
            if i == 0:
                build_U(0, [g(("vready", 0)), g(("pro",))])
            build_U(2 * i + 1, [g(("vready", i))])
            zip2(waves_ch(i, 0),
                 stage_A_gen(i + 1) if i + 1 < NT else None)
            emit_c2_u(i)
            zip2(waves_ch(i, 1), stage_C_gen(i, 0))
            zip2(rows_gen(i), stage_C_gen(i, 1))
            load_x(i + 2)

        dma("sync", "dS", xg[0:64, :], x_d[gi0 * 64:(gi0 + 1) * 64, :],
            wait=[g(("xfree", NT % 2))])
        dxg = dma("sync", "dS", xg[64:128, :], x_d[gi1 * 64:(gi1 + 1) * 64, :])
        rec(("xg_ld",), ("dS", dxg))
        vdrain()
        vop(lambda e: e.reciprocal(RL16[:], racc[:, :, 64]),
            wait=[g("racc_rd"), g(("ctxT_done", 0))], est=200)
        vdrain()
        vop(lambda e: e.tensor_tensor(
            CTXt[0][:], racc[:, :, 0:64],
            RL16[:].rearrange("p (h o) -> p h o", o=1).broadcast_to([128, H, 64]),
            op=OP.mult), est=1200)
        v_d = vdrain(inc=("V", 1))
        rec(("ctx_ready", 0), ("V", v_d))
        hold = {}
        drain(stage_C_gen(None, 0, epilogue=True, out=hold))
        v_o = hold["v_o"]
        st_f0, st_f1 = b.v("st0"), b.v("st1")
        dma("gpsimd", "st0", out_d[gi0 * 64:(gi0 + 1) * 64, :], xg[0:64, :],
            wait=[v_o, ("st0", st_f0), ("st1", st_f1)])
        dma("gpsimd", "st1", out_d[gi1 * 64:(gi1 + 1) * 64, :], xg[64:128, :])
        b.op("gpsimd", lambda e: e.nop(),
             wait=[("st0", b.v("st0")), ("st1", b.v("st1"))])

        import os as _osc
        if _osc.environ.get("KCHECK", "1") == "1":
            bad = b.check_banks()
            if bad:
                for x in bad[:20]:
                    print("PSUM COLLISION:", x)
                raise RuntimeError(f"{len(bad)} psum bank collisions")

        with ExitStack() as sctx:
            semh = {name: sctx.enter_context(nc.semaphore(name))
                    for name in b.cnt}
            with nc.Block() as block:
                def mk(ename):
                    def run(eng):
                        for fn, waits, inc in b.ops[ename]:
                            for (s, vv) in waits:
                                eng.wait_ge(semh[s], vv)
                            ins = fn(eng)
                            if inc is not None:
                                ins.then_inc(semh[inc[0]], inc[1])
                    return run
                block.sync(mk("sync"))
                block.gpsimd(mk("gpsimd"))
                block.tensor(mk("tensor"))
                block.vector(mk("vector"))
                block.scalar(mk("scalar"))
    return nc


def kernel(L0_attn_x, L0_attn_Wq, L0_attn_bq, L0_attn_Wk, L0_attn_bk,
           L0_attn_Wv, L0_attn_bv, L0_attn_Wo, L0_attn_bo,
           L0_attn_global_indices, L0_ffn_weight_fp8, L0_ffn_bias,
           L0_ffn_input_scale, L0_ffn_weight_scale):
    x = np.ascontiguousarray(np.asarray(L0_attn_x), dtype=np.float32)
    Bn, NTOK, Dm = x.shape
    gi = np.asarray(L0_attn_global_indices).astype(np.int64)
    nc = build_nc(NTOK, int(gi[0]), int(gi[1]))
    base = {
        "wq": np.ascontiguousarray(L0_attn_Wq, dtype=np.float32),
        "wk": np.ascontiguousarray(L0_attn_Wk, dtype=np.float32),
        "wv": np.ascontiguousarray(L0_attn_Wv, dtype=np.float32),
        "wo": np.ascontiguousarray(L0_attn_Wo, dtype=np.float32),
        "wf": np.ascontiguousarray(L0_ffn_weight_fp8, dtype=np.float32),
        "bq": np.ascontiguousarray(L0_attn_bq, dtype=np.float32).reshape(1, Dm),
        "bk": np.ascontiguousarray(L0_attn_bk, dtype=np.float32).reshape(1, Dm),
        "bv": np.ascontiguousarray(L0_attn_bv, dtype=np.float32).reshape(1, Dm),
        "bo": np.ascontiguousarray(L0_attn_bo, dtype=np.float32).reshape(1, Dm),
        "bf": np.ascontiguousarray(L0_ffn_bias, dtype=np.float32).reshape(1, Dm),
        "sin": np.ascontiguousarray(L0_ffn_input_scale,
                                    dtype=np.float32).reshape(1, 1),
        "sw": np.ascontiguousarray(L0_ffn_weight_scale,
                                   dtype=np.float32).reshape(1, 1),
    }
    in_maps = [dict(base, x=np.ascontiguousarray(x[c])) for c in range(Bn)]
    res = run_bass_kernel_spmd(nc, in_maps, core_ids=list(range(Bn)))
    return np.stack([np.asarray(r["out"]) for r in res.results], axis=0)


# version 27 (final)
# speedup vs baseline: 1.0347x; 1.0251x over previous
import numpy as np
import ml_dtypes
from contextlib import ExitStack

import concourse.bass as bass
import concourse.mybir as mybir
from concourse.bass_utils import run_bass_kernel_spmd

F32 = mybir.dt.float32
BF16 = mybir.dt.bfloat16
FP8 = mybir.dt.float8e4
AX = mybir.AxisListType
OP = mybir.AluOpType
ACT = mybir.ActivationFunctionType

D = 1024
H = 16
BS = 64
TILE = 256
SCALE = 0.125
EPS = 1e-5
CLIP_HALF = 224.0
ENGINES = ("sync", "gpsimd", "tensor", "vector", "scalar")


class Bld:
    def __init__(self):
        self.ops = {e: [] for e in ENGINES}
        self.cnt = {}
        self.est = {e: 0.0 for e in ENGINES}
        self.G = {}
        self.accs = []
        self.sem_log = {}
        self.waits_of = {}

    def check_banks(self):
        order = {e: i for i, e in enumerate(ENGINES)}
        n = {e: len(self.ops[e]) for e in ENGINES}
        clocks = {e: [None] * n[e] for e in ENGINES}
        sat = {}
        def provider(s, v):
            log = self.sem_log.get(s, [])
            lo, hi = 0, len(log)
            while lo < hi:
                mid = (lo + hi) // 2
                if log[mid][0] >= v:
                    hi = mid
                else:
                    lo = mid + 1
            if lo == len(log):
                raise RuntimeError(f"wait {s}>={v} never satisfied")
            return log[lo][1], log[lo][2]
        ptr = {e: 0 for e in ENGINES}
        cur = {e: [0] * len(ENGINES) for e in ENGINES}
        total = sum(n.values())
        done = 0
        while done < total:
            progressed = False
            for e in ENGINES:
                while ptr[e] < n[e]:
                    i = ptr[e]
                    fn, ws, inc = self.ops[e][i]
                    ok = True
                    c = list(cur[e])
                    for (s, v) in ws:
                        pe, pi = provider(s, v)
                        if clocks[pe][pi] is None or (
                                pe == e and pi >= i):
                            ok = False
                            break
                        pc = clocks[pe][pi]
                        for k in range(len(ENGINES)):
                            c[k] = max(c[k], pc[k])
                        c[order[pe]] = max(c[order[pe]], pi + 1)
                    if not ok:
                        break
                    clocks[e][i] = tuple(c)
                    c[order[e]] = i + 1
                    cur[e] = c
                    ptr[e] += 1
                    done += 1
                    progressed = True
            if not progressed:
                stuck = {e: (ptr[e], self.ops[e][ptr[e]][1]
                             if ptr[e] < n[e] else None) for e in ENGINES}
                raise RuntimeError(f"deadlock in program graph: {stuck}")
        bad = []
        by_bank = {}
        for (e, i, bank, rw) in self.accs:
            by_bank.setdefault(bank, []).append((e, i, rw))
        def unordered(e1, i1, e2, i2):
            if e1 == e2:
                return False
            c1 = clocks[e1][i1]
            c2 = clocks[e2][i2]
            return not (i1 < c2[order[e1]] or i2 < c1[order[e2]])

        for bank, lst in by_bank.items():
            tw = [(e, i) for (e, i, rw) in lst if e == "tensor" and rw == "w"]
            vs = [(e, i) for (e, i, rw) in lst if e in ("vector", "scalar")]
            for (te, ti) in tw:
                for (ve, vi) in vs:
                    if unordered(te, ti, ve, vi):
                        bad.append(("T-vs-VS", bank, ti, ve, vi))
            vv = [(e, i) for (e, i, rw) in lst if e == "vector"]
            ss = [(e, i) for (e, i, rw) in lst if e == "scalar"]
            for (e1, i1) in vv:
                for (e2, i2) in ss:
                    if unordered(e1, i1, e2, i2):
                        bad.append(("V-vs-S", bank, i1, e2, i2))
        return bad

    def sem(self, name):
        self.cnt.setdefault(name, 0)
        return name

    def op(self, eng, fn, wait=(), inc=None, est=0.0, acc=()):
        ws = []
        for w in wait:
            if w is None:
                continue
            if isinstance(w, list):
                ws.extend(p for p in w if p is not None and p[1] > 0)
            elif w[0] is not None and w[1] > 0:
                ws.append(w)
        self.ops[eng].append((fn, tuple(ws), inc))
        self.est[eng] += est
        idx = len(self.ops[eng]) - 1
        for a in acc:
            self.accs.append((eng, idx, a[0], a[1]))
        if inc is not None:
            self.cnt[inc[0]] += inc[1]
            self.sem_log.setdefault(inc[0], []).append(
                (self.cnt[inc[0]], eng, idx))
            return self.cnt[inc[0]]
        return None

    def v(self, s):
        return self.cnt[s]

    def rec(self, key, *pairs):
        self.G[key] = [p for p in pairs if p is not None]

    def g(self, key):
        return self.G.get(key, [])


def build_nc(NTOK, gi0, gi1):
    NB = NTOK // BS
    NT = NTOK // TILE
    assert NTOK % TILE == 0 and 0 <= gi0 < NB and 0 <= gi1 < NB

    nc = bass.Bass("TRN2", target_bir_lowering=False, debug=False)
    b = Bld()
    for s in ("dS", "dG", "wl0", "wl1", "xl0", "xl1", "st0", "st1", "T", "V", "S", "P", "tA0", "tA1"):
        b.sem(s)

    x_d = nc.dram_tensor("x", [NTOK, D], F32, kind="ExternalInput").ap()
    w_d = {w: nc.dram_tensor(w, [D, D], F32, kind="ExternalInput").ap()
           for w in ("wq", "wk", "wv", "wo", "wf")}
    b_d = {n: nc.dram_tensor(n, [1, D], F32, kind="ExternalInput").ap()
           for n in ("bq", "bk", "bv", "bo", "bf")}
    sin_d = nc.dram_tensor("sin", [1, 1], F32, kind="ExternalInput").ap()
    sw_d = nc.dram_tensor("sw", [1, 1], F32, kind="ExternalInput").ap()
    out_d = nc.dram_tensor("out", [NTOK, D], F32, kind="ExternalOutput").ap()

    ident_dram = nc.inline_tensor(np.eye(128, dtype=ml_dtypes.bfloat16), "ident_c")
    identf_dram = nc.inline_tensor(np.eye(128, dtype=np.float32), "identf_c")
    consts_dram = nc.inline_tensor(
        np.tile(np.array([[-0.5, SCALE, 0.5, 1e-5]], np.float32), (128, 1)),
        "consts_c")
    _sel = np.zeros((4, 2, 128), dtype=ml_dtypes.bfloat16)
    _sel[0, 0, :] = 1
    _sel[1, 1, :] = 1
    _sel[2, :, 0:64] = 1
    _sel[3, :, 64:128] = 1
    sel_dram = nc.inline_tensor(_sel, "sel_c")
    _cbig = np.zeros((2, 2, 3, 128), dtype=ml_dtypes.bfloat16)
    _cbig[0, :, 0, 64:128] = -30000.0
    _cbig[1, :, 1, 0:64] = -30000.0
    cbig_dram = nc.inline_tensor(_cbig, "cbig_c")

    with ExitStack() as ctx:
        E = ctx.enter_context
        sb = lambda n, sh, dt=F32: E(nc.sbuf_tensor(n, sh, dt))

        Wf = {w: sb(f"W_{w}", [128, 8, D], BF16) for w in ("wq", "wk", "wv", "wo")}
        Wf8 = sb("W_f8", [128, 8, D], FP8)
        bqf = sb("bqf", [128, 8])
        bkf = sb("bkf", [128, 8])
        bq_row = sb("bq_row", [1, D])
        bk_row = sb("bk_row", [1, D])
        bvrow = sb("bvrow", [1, D], BF16)
        borow = sb("borow", [1, D], BF16)
        bfrow_f = sb("bfrow_f", [1, D])
        bfrow = sb("bfrow", [1, D], BF16)
        ones_b = sb("ones_b", [1, 128], BF16)
        ones_f = sb("ones_f", [1, 128])
        ident = sb("ident", [128, 128], BF16)
        identf = sb("identf", [128, 128])
        CONSTS = sb("CONSTS", [128, 4])
        nc.const_aps.register(F32, -0.5, CONSTS[:, 0:1])
        nc.const_aps.register(F32, SCALE, CONSTS[:, 1:2])
        nc.const_aps.register(F32, 0.5, CONSTS[:, 2:3])
        nc.const_aps.register(F32, EPS, CONSTS[:, 3:4])
        s11 = sb("s11", [1, 4])
        sc_bc = sb("sc_bc", [128, 3])

        Kr = [sb(f"Kr{j}", [128, 8, 384], BF16) for j in range(3)]
        Vr = [sb(f"Vr{j}", [128, 2, H, 65], BF16) for j in range(3)]
        VEU = [sb(f"VEU{j}", [128, H, 65], BF16) for j in range(3)]
        K0f = sb("K0f", [128, 8, 64], BF16)
        K127f = sb("K127f", [128, 8, 64], BF16)
        V0x = sb("V0x", [128, H, 65], BF16)
        V127x = sb("V127x", [128, H, 65], BF16)
        gkf = sb("gkf", [128, 8, 128], BF16)
        gvx = sb("gvx", [128, H, 65], BF16)
        qgf = sb("qgf", [128, 8, 128], BF16)
        racc = sb("racc", [128, H, 65])
        mrun = sb("mrun", [128, 1], BF16)
        mnb = sb("mnb", [128, 1], BF16)
        RS8 = sb("RS8", [128, 16])

        xbuf = [sb(f"xbuf{j}", [128, 2, D]) for j in range(2)]
        qf = [sb(f"qf{j}", [128, 8, TILE], BF16) for j in range(2)]
        xfm_t = sb("xfm", [128, 8, TILE], BF16)
        xfm = [xfm_t, xfm_t]
        ln1o2 = sb("ln1o2", [128, 2, D], BF16)
        SMA = sb("SMA", [128, 8])
        SMW = sb("SMW", [128, 4])
        nfm2 = sb("nfm2", [4, 2, 3, 128], BF16)
        SEL = sb("SEL", [4, 2, 128], BF16)
        RLD = sb("RLD", [128, 16, 2])
        CTXlc = sb("CTXlc", [128, 16, 2, 65], BF16)
        PT = sb("PT", [128, 2, 2, 3, 128], BF16)
        CTXt_t = sb("CTXt", [128, H, 64], BF16)
        CTXt = [CTXt_t, CTXt_t]
        ctxf_t = sb("ctxf", [128, 8, 128], BF16)
        ctxf = [ctxf_t, ctxf_t]
        xp_t = sb("xp", [128, D])
        xp = [xp_t, xp_t]
        lnat_t = sb("lnat", [128, D])
        lnat = [lnat_t, lnat_t]
        ln2s_t = sb("ln2s", [128, D])
        ln2s = [ln2s_t, ln2s_t]
        xq8_t = sb("xq8", [128, 8, 128], FP8)
        xq8 = [xq8_t, xq8_t]
        SQ_t = sb("SQ", [128, D])
        SQ = [SQ_t, SQ_t]
        SM = [sb(f"SM{j}", [128, 8]) for j in range(2)]
        SM2 = [sb(f"SM2{j}", [128, 8]) for j in range(2)]
        RS = sb("RS", [128, 6])
        RL = sb("RL", [128, 4])
        RL16 = sb("RL16", [128, 16])

        P_big = E(nc.psum_tensor("P_big", [128, 2, 512], F32))
        P_tr = E(nc.psum_tensor("P_tr", [128, 8, 128], BF16))
        P_aux = E(nc.psum_tensor("P_aux", [128, 4, 128], F32))
        PS = [E(nc.psum_tensor(f"PS{j}", [128, 2, 4, 128], F32))
              for j in range(2)]

        g, rec = b.g, b.rec
        _pbase = {"P_big": 0, "P_tr": 2, "P_aux": 3, "PS0": 4, "PS1": 6}

        def pb(ap):
            try:
                t = ap.tensor
            except Exception:
                return []
            if t.name not in _pbase:
                return []
            es = mybir.dt.size(ap.dtype)
            start = ap.offset * es
            span = 0
            for (st, cnt) in ap.ap[1:]:
                span += st * es * (cnt - 1)
            span += es
            b0 = _pbase[t.name] + start // 2048
            b1 = _pbase[t.name] + (start + span - 1) // 2048
            return list(range(b0, b1 + 1))

        def accr(*aps):
            out = []
            for a in aps:
                out.extend((bb_, "r") for bb_ in pb(a))
            return out

        def accw(*aps):
            out = []
            for a in aps:
                out.extend((bb_, "w") for bb_ in pb(a))
            return out

        def vop(fn, wait=(), inc=None, est=250.0, acc=()):
            return b.op("vector", fn, wait=wait, inc=inc, est=est, acc=acc)

        def sop(fn, wait=(), inc=None, est=250.0, acc=()):
            return b.op("scalar", fn, wait=wait, inc=inc, est=est, acc=acc)

        def top(fn, wait=(), inc=None, est=100.0, acc=()):
            return b.op("tensor", fn, wait=wait, inc=inc, est=est, acc=acc)

        def vdrain(inc=None):
            return vop(lambda e: e.drain(), inc=inc, est=30.0)

        def sdrain(inc=None):
            return sop(lambda e: e.drain(), inc=inc, est=30.0)

        def tmark():
            return top(lambda e: e.nop(), inc=("T", 1), est=5.0)

        def dma(q, sem, out_ap, in_ap, wait=()):
            return b.op(q, lambda e, o=out_ap, i=in_ap: e.dma_start(out=o, in_=i),
                        wait=wait, inc=(sem, 16))

        def evac(fnv, fns, wait=(), est=500.0, acc=()):
            if fnv is None:
                eng = "scalar"
            elif fns is None:
                eng = "vector"
            else:
                eng = "vector" if b.est["vector"] <= b.est["scalar"] else "scalar"
            if eng == "vector":
                vop(fnv, wait=wait, est=est, acc=acc)
                return ("V", vdrain(inc=("V", 1)))
            sop(fns, wait=wait, est=est, acc=acc)
            return ("S", sdrain(inc=("S", 1)))

        def evac_copy(out_ap, in_ap, wait=(), est=500.0):
            return evac(lambda e, o=out_ap, i=in_ap: e.tensor_copy(o, i),
                        lambda e, o=out_ap, i=in_ap: e.copy(o, i),
                        wait=wait, est=est, acc=accr(in_ap))

        def evac_bias(out_ap, in_ap, bias_ap, wait=(), est=500.0):
            return evac(lambda e, o=out_ap, i=in_ap, bb=bias_ap:
                        e.tensor_scalar(o, i, bb, None, op0=OP.add),
                        lambda e, o=out_ap, i=in_ap, bb=bias_ap:
                        e.activation(o, i, ACT.Identity, bias=bb),
                        wait=wait, est=est, acc=accr(in_ap))

        def transpose(out_ap, in_ap, idt, wait=(), inc=None):
            return top(lambda e, o=out_ap, i=in_ap, d=idt: e.transpose(o, i, d),
                       wait=wait, inc=inc, est=120, acc=accw(out_ap))

        def mm(out_ap, lhsT_ap, rhs_ap, start, stop, wait=(), inc=None, est=120,
               acc=None, skip=False):
            return top(lambda e, o=out_ap, l=lhsT_ap, r=rhs_ap, st=start,
                       sp=stop, sk=skip:
                       e.matmul(o, l, r, start=st, stop=sp,
                                skip_group_check=sk),
                       wait=wait, inc=inc, est=est,
                       acc=(accw(out_ap) if acc is None else acc))

        dma("sync", "dS", ident[:], ident_dram.ap())
        dma("sync", "dS", identf[:], identf_dram.ap())
        dma("sync", "dS", CONSTS[:], consts_dram.ap())
        dma("sync", "dS", s11[0:1, 0:1], sin_d)
        dma("sync", "dS", s11[0:1, 1:2], sw_d)
        dma("sync", "dS", bq_row[:], b_d["bq"])
        dma("sync", "dS", bk_row[:], b_d["bk"])
        dma("sync", "dS", bfrow_f[:], b_d["bf"])
        dma("sync", "dS", SEL[:], sel_dram.ap())
        dma("sync", "dS", nfm2[2:4, :, :, :], cbig_dram.ap())
        xg = xbuf[NT % 2][:, 0, :]
        gath = xbuf[0]
        for j, blk in enumerate((0, NB - 1, gi0, gi1)):
            dma("sync", "dS", gath[(j % 2) * 64:(j % 2) * 64 + 64, j // 2, :],
                x_d[blk * 64:(blk + 1) * 64, :])
        pro_dS = b.v("dS")
        dma("gpsimd", "dG", bvrow[:], b_d["bv"])
        dma("gpsimd", "dG", borow[:], b_d["bo"])
        pro_dG = b.v("dG")

        vop(lambda e: e.memset(ones_b[:], 1.0))

        vop(lambda e: e.memset(ones_f[:], 1.0))
        vop(lambda e: e.memset(PT[:], 0.0))
        vop(lambda e: e.memset(racc[:], 0.0))
        vop(lambda e: e.memset(mrun[:], -1e30))
        for j in range(3):
            vop(lambda e, jj=j: e.memset(Vr[jj][:, :, :, 64:65], 1.0))
        for t in (V0x, V127x, gvx):
            vop(lambda e, tt=t: e.memset(tt[:, :, 64:65], 1.0))
        vop(lambda e: e.tensor_scalar(s11[0:1, 2:3], s11[0:1, 0:1], s11[0:1, 1:2],
                                      4.0, op0=OP.mult, op1=OP.mult),
            wait=[("dS", pro_dS)])
        vdrain()
        vop(lambda e: e.reciprocal(s11[0:1, 3:4], s11[0:1, 2:3]))
        vop(lambda e: e.reciprocal(s11[0:1, 0:1], s11[0:1, 0:1]))
        vdrain()
        vop(lambda e: e.tensor_scalar(bfrow[:], bfrow_f[:], s11[0:1, 3:4], None,
                                      op0=OP.mult), wait=[("dG", pro_dG)])
        v_pro = vdrain(inc=("V", 1))
        mm(P_aux[:, 0, 0:1], ones_f[:], s11[0:1, 2:3], True, True,
           wait=[("V", v_pro)])
        t_bc = mm(P_aux[:, 1, 0:1], ones_f[:], s11[0:1, 0:1], True, True,
                  inc=("T", 1))
        vop(lambda e: e.tensor_copy(sc_bc[:, 0:1], P_aux[:, 0, 0:1]),
            wait=[("T", t_bc)], acc=accr(P_aux[:, 0, 0:1]))
        vop(lambda e: e.tensor_scalar(sc_bc[:, 2:3], P_aux[:, 1, 0:1], 0.5, None,
                                      op0=OP.mult), acc=accr(P_aux[:, 1, 0:1]))
        vdrain()
        vop(lambda e: e.tensor_tensor(sc_bc[:, 1:2], sc_bc[:, 0:1], sc_bc[:, 0:1],
                                      op=OP.mult))
        v_scbc = vdrain(inc=("V", 1))
        rec("p_aux", ("V", v_scbc))
        for row, dstf in ((bq_row, bqf), (bk_row, bkf)):
            t_bb = None
            for j in range(8):
                t_bb = transpose(P_aux[:, 2, j:j + 1],
                                 row[0:1, j * 128:(j + 1) * 128],
                                 ones_f[0:1, 0:1],
                                 wait=([("dS", pro_dS), ("V", v_scbc),
                                        g("p_aux")] if j == 0 else []),
                                 inc=(("T", 1) if j == 7 else None))
            ev_bb = evac_copy(dstf[:], P_aux[:, 2, 0:8], wait=[("T", t_bb)],
                              est=300)
            rec("p_aux", ev_bb)

        for w in ("wq", "wk", "wv", "wo", "wf"):
            for oc in range(8):
                buf = SQ_t if oc % 2 == 0 else lnat_t
                dv = dma("sync", f"wl{oc % 2}", buf[:],
                         w_d[w][oc * 128:(oc + 1) * 128, :],
                         wait=[g(("wbuf", oc % 2))])
                for half in range(2):
                    tv = None
                    for kc in range(4 * half, 4 * half + 4):
                        tv = transpose(
                            P_big[:, half, (kc % 4) * 128:(kc % 4) * 128 + 128],
                            buf[:, kc * 128:(kc + 1) * 128], identf[:],
                            wait=([(f"wl{oc % 2}", dv), g(("wps", half))]
                                  if kc % 4 == 0 else []),
                            inc=(("T", 1) if kc % 4 == 3 else None))
                    src = P_big[:, half, :].rearrange("p (a c) -> p a c", a=4)
                    if w == "wf":
                        ev = evac(None, lambda e, hh=half, occ=oc, s=src:
                                  e.activation(
                                      Wf8[:, 4 * hh:4 * hh + 4,
                                          occ * 128:(occ + 1) * 128],
                                      s, ACT.Identity, scale=0.5),
                                  wait=[("T", tv)], est=600, acc=accr(src))
                    else:
                        ev = evac(lambda e, ww=w, hh=half, occ=oc, s=src:
                                  e.tensor_copy(
                                      Wf[ww][:, 4 * hh:4 * hh + 4,
                                             occ * 128:(occ + 1) * 128], s),
                                  lambda e, ww=w, hh=half, occ=oc, s=src:
                                  e.activation(
                                      Wf[ww][:, 4 * hh:4 * hh + 4,
                                             occ * 128:(occ + 1) * 128],
                                      s, ACT.Identity),
                                  wait=[("T", tv)], est=600, acc=accr(src))
                    rec(("wps", half), ev)
                rec(("wbuf", oc % 2), ("T", b.v("T")))
        rec(("pbig", 0), g(("wps", 0))[0])
        rec(("pbig", 1), g(("wps", 1))[0])

        def ln_full(x_ap, sm, out_ap, sq_scr, wait_pre=(), scale_extra=None,
                    sfin_fold=False):
            wp = list(wait_pre)
            parts = x_ap if isinstance(x_ap, list) else None
            if parts is None:
                vop(lambda e, xa=x_ap, s=sm: e.reduce_sum(
                    s[:, 0:1], xa, axis=AX.X, negate=True),
                    wait=wp, est=(140 + D) / 0.96)
                vdrain()
                vop(lambda e, s=sm: e.tensor_scalar(
                    s[:, 0:1], s[:, 0:1], 1.0 / D, None, op0=OP.mult), est=140)
                v_mean = vdrain(inc=("V", 1))
                sop(lambda e, xa=x_ap, s=sm, q=sq_scr: e.activation(
                    q[:], xa, ACT.Square, accum_out=s[:, 1:2]),
                    wait=wp, est=(200 + D) / 1.2)
                s_sq = sdrain(inc=("S", 1))
            else:
                vop(lambda e, xa=parts[0][0], s=sm: e.reduce_sum(
                    s[:, 0:1], xa, axis=AX.X, negate=True),
                    wait=wp, est=(140 + D / 2) / 0.96, acc=accr(parts[0][0]))
                v_r0 = vdrain(inc=("V", 1))
                sop(lambda e, xa=parts[1][0], s=sm, q=sq_scr: e.activation(
                    q[:, D // 2:D], xa, ACT.Square, accum_out=s[:, 7:8]),
                    wait=wp, est=(200 + D / 2) / 1.2, acc=accr(parts[1][0]))
                s_sq2 = sdrain(inc=("S", 1))
                vop(lambda e, xa=parts[1][0], s=sm: e.reduce_sum(
                    s[:, 6:7], xa, axis=AX.X, negate=True),
                    wait=[("S", s_sq2)], est=(140 + D / 2) / 0.96,
                    acc=accr(parts[1][0]))
                vdrain()
                vop(lambda e, s=sm: e.tensor_tensor(
                    s[:, 0:1], s[:, 0:1], s[:, 6:7], op=OP.add), est=140)
                vdrain()
                vop(lambda e, s=sm: e.tensor_scalar(
                    s[:, 0:1], s[:, 0:1], 1.0 / D, None, op0=OP.mult), est=140)
                v_mean = vdrain(inc=("V", 1))
                sop(lambda e, xa=parts[0][0], s=sm, q=sq_scr: e.activation(
                    q[:, 0:D // 2], xa, ACT.Square, accum_out=s[:, 1:2]),
                    wait=[("V", v_r0)], est=(200 + D / 2) / 1.2,
                    acc=accr(parts[0][0]))
                s_sq1 = sdrain(inc=("S", 1))
                vop(lambda e, s=sm: e.tensor_tensor(
                    s[:, 1:2], s[:, 1:2], s[:, 7:8], op=OP.add),
                    wait=[("S", s_sq1)], est=140)
                vdrain()
                s_sq = sdrain(inc=("S", 1))
            vop(lambda e, s=sm: e.tensor_tensor(s[:, 4:5], s[:, 0:1], s[:, 0:1],
                                                op=OP.mult),
                wait=[("V", v_mean), ("S", s_sq)], est=140)
            vdrain()
            vdrain()
            vop(lambda e, s=sm: e.scalar_tensor_tensor(
                s[:, 5:6], s[:, 1:2], 1.0 / D, s[:, 4:5],
                op0=OP.mult, op1=OP.subtract), est=140)
            vdrain()
            if sfin_fold:
                vop(lambda e, s=sm: e.tensor_tensor(s[:, 5:6], s[:, 5:6],
                                                    sc_bc[:, 1:2], op=OP.mult),
                    est=140)
                vdrain()
            vop(lambda e, s=sm: e.tensor_scalar(s[:, 5:6], s[:, 5:6], EPS, None,
                                                op0=OP.add), est=140)
            v_var = vdrain(inc=("V", 1))
            sop(lambda e, s=sm: e.activation(s[:, 4:5], s[:, 5:6], ACT.Ln),
                wait=[("V", v_var)], est=350)
            sdrain()
            sop(lambda e, s=sm: e.activation(s[:, 2:3], s[:, 4:5], ACT.Exp,
                                             scale=-0.5), est=350)
            s_r = sdrain(inc=("S", 1))
            v_w = [("S", s_r)]
            if scale_extra is not None:
                vop(lambda e, s=sm, x2=scale_extra: e.tensor_tensor(
                    s[:, 2:3], s[:, 2:3], x2, op=OP.mult), wait=v_w, est=140)
                vdrain()
                v_w = []
            if sfin_fold:
                vop(lambda e, s=sm: e.tensor_tensor(s[:, 2:3], s[:, 2:3],
                                                    sc_bc[:, 0:1], op=OP.mult),
                    wait=v_w, est=140)
                vdrain()
                v_w = []
            vop(lambda e, s=sm: e.tensor_tensor(s[:, 3:4], s[:, 0:1], s[:, 2:3],
                                                op=OP.mult), wait=v_w, est=140)
            v_n = vdrain(inc=("V", 1))
            if parts is None:
                sop(lambda e, xa=x_ap, s=sm, o=out_ap: e.activation(
                    o, xa, ACT.Identity, scale=s[:, 2:3], bias=s[:, 3:4]),
                    wait=[("V", v_n)], est=(200 + D) / 1.2)
            else:
                sop(lambda e, xa=parts[0][0], s=sm, o=parts[0][1]: e.activation(
                    o, xa, ACT.Identity, scale=s[:, 2:3], bias=s[:, 3:4]),
                    wait=[("V", v_n)], est=(200 + D / 2) / 1.2,
                    acc=accr(parts[0][0]))
                sop(lambda e, xa=parts[1][0], s=sm, o=parts[1][1]: e.activation(
                    o, xa, ACT.Identity, scale=s[:, 2:3], bias=s[:, 3:4]),
                    est=(200 + D / 2) / 1.2, acc=accr(parts[1][0]))
            s_a = sdrain(inc=("S", 1))
            return ("S", s_a)

        def ln_psum(sm, out_ap, wait_pre=(), sfin_fold=False):
            wp = list(wait_pre)
            v1 = vop(lambda e, s=sm: e.reduce_sum(
                s[:, 0:2], P_big[:, 0:2, :], axis=AX.X, negate=True),
                wait=wp, inc=("V", 1), est=(140 + D) / 0.96,
                acc=accr(P_big[:, 0, :], P_big[:, 1, :]))
            sop(lambda e, s=sm: e.activation(
                SQ_t[:, 0:512], P_big[:, 0, :], ACT.Square,
                accum_out=s[:, 2:3]),
                wait=[("V", v1)], est=(200 + 512) / 1.2,
                acc=accr(P_big[:, 0, :]))
            sdrain()
            s_sq = sop(lambda e, s=sm: e.activation(
                SQ_t[:, 512:1024], P_big[:, 1, :], ACT.Square,
                accum_out=s[:, 3:4]),
                inc=("S", 1), est=(200 + 512) / 1.2,
                acc=accr(P_big[:, 1, :]))
            vdrain()
            vop(lambda e, s=sm: e.tensor_tensor(s[:, 0:1], s[:, 0:1],
                                                s[:, 1:2], op=OP.add), est=140)
            vdrain()
            vop(lambda e, s=sm: e.tensor_scalar(s[:, 0:1], s[:, 0:1], 1.0 / D,
                                                None, op0=OP.mult), est=140)
            vdrain()
            vop(lambda e, s=sm: e.tensor_tensor(s[:, 4:5], s[:, 0:1],
                                                s[:, 0:1], op=OP.mult), est=140)
            vop(lambda e, s=sm: e.tensor_tensor(s[:, 2:3], s[:, 2:3],
                                                s[:, 3:4], op=OP.add),
                wait=[("S", s_sq)], est=140)
            vdrain()
            vop(lambda e, s=sm: e.scalar_tensor_tensor(
                s[:, 5:6], s[:, 2:3], 1.0 / D, s[:, 4:5],
                op0=OP.mult, op1=OP.subtract), est=140)
            if sfin_fold:
                vdrain()
                vop(lambda e, s=sm: e.tensor_tensor(s[:, 5:6], s[:, 5:6],
                                                    sc_bc[:, 1:2], op=OP.mult),
                    est=140)
            v_var = vdrain(inc=("V", 1))
            sop(lambda e, s=sm: e.activation(s[:, 4:5], s[:, 5:6], ACT.Ln,
                                             bias=EPS),
                wait=[("V", v_var)], est=350)
            sdrain()
            s_r = sop(lambda e, s=sm: e.activation(s[:, 6:7], s[:, 4:5],
                                                   ACT.Exp, scale=-0.5),
                      inc=("S", 1), est=350)
            if sfin_fold:
                vop(lambda e, s=sm: e.tensor_tensor(s[:, 6:7], s[:, 6:7],
                                                    sc_bc[:, 0:1], op=OP.mult),
                    wait=[("S", s_r)], est=140)
                vdrain()
            v_b = vop(lambda e, s=sm: e.tensor_tensor(s[:, 0:1], s[:, 0:1],
                                                      s[:, 6:7], op=OP.mult),
                      wait=([] if sfin_fold else [("S", s_r)]),
                      inc=("V", 1), est=140)
            s_a = sop(lambda e, s=sm, o=out_ap: e.activation(
                o.rearrange("p (a c) -> p a c", a=2), P_big[:, 0:2, :],
                ACT.Identity, scale=s[:, 6:7], bias=s[:, 0:1]),
                wait=[("V", v_b)], inc=("S", 1), est=(200 + D) / 1.2,
                acc=accr(P_big[:, 0, :], P_big[:, 1, :]))
            return ("S", s_a)

        def ln_sbuf(x_ap, sm, out_ap, wait_pre=(), scale_extra=None,
                    have_sum=False):
            wp = list(wait_pre)
            if not have_sum:
                vop(lambda e, xa=x_ap, s=sm: e.reduce_sum(
                    s[:, 0:1], xa, axis=AX.X, negate=True),
                    wait=wp, est=(140 + D) / 0.96)
            s_sq = sop(lambda e, xa=x_ap, s=sm: e.activation(
                SQ_t[:], xa, ACT.Square, accum_out=s[:, 2:3]),
                wait=wp, inc=("S", 1), est=(200 + D) / 1.2)
            if not have_sum:
                vdrain()
            sgn = -1.0 if have_sum else 1.0
            vop(lambda e, s=sm, sg=sgn: e.tensor_scalar(
                s[:, 0:1], s[:, 0:1], sg / D, None, op0=OP.mult),
                wait=(wp if have_sum else []), est=140)
            vdrain()
            vop(lambda e, s=sm: e.tensor_tensor(s[:, 4:5], s[:, 0:1],
                                                s[:, 0:1], op=OP.mult), est=140)
            vdrain()
            vop(lambda e, s=sm: e.scalar_tensor_tensor(
                s[:, 5:6], s[:, 2:3], 1.0 / D, s[:, 4:5],
                op0=OP.mult, op1=OP.subtract),
                wait=[("S", s_sq)], est=140)
            v_var = vdrain(inc=("V", 1))
            sop(lambda e, s=sm: e.activation(s[:, 4:5], s[:, 5:6], ACT.Ln,
                                             bias=EPS),
                wait=[("V", v_var)], est=350)
            sdrain()
            s_r = sop(lambda e, s=sm: e.activation(s[:, 6:7], s[:, 4:5],
                                                   ACT.Exp, scale=-0.5),
                      inc=("S", 1), est=350)
            v_w = [("S", s_r)]
            if scale_extra is not None:
                vop(lambda e, s=sm, x2=scale_extra: e.tensor_tensor(
                    s[:, 6:7], s[:, 6:7], x2, op=OP.mult), wait=v_w, est=140)
                vdrain()
                v_w = []
            v_b = vop(lambda e, s=sm: e.tensor_tensor(s[:, 0:1], s[:, 0:1],
                                                      s[:, 6:7], op=OP.mult),
                      wait=v_w, inc=("V", 1), est=140)
            s_a = sop(lambda e, xa=x_ap, s=sm, o=out_ap: e.activation(
                o, xa, ACT.Identity, scale=s[:, 6:7], bias=s[:, 0:1]),
                wait=[("V", v_b)], inc=("S", 1), est=(200 + D) / 1.2)
            return ("S", s_a)

        def proj_fm(w, out_region, oc, rhs_fn, wl):
            t = None
            for kc in range(8):
                t = mm(out_region, Wf[w][:, kc, oc * 128:(oc + 1) * 128],
                       rhs_fn(kc), kc == 0, kc == 7,
                       wait=(wl if kc == 0 else []),
                       inc=(("T", 1) if kc == 7 else None), est=115)
            return t

        pro_fm = xfm[0]
        for gc in range(2):
            s_a = ln_full(gath[:, gc, :], SM[gc], ln1o2[:, gc, :], SQ[gc],
                          wait_pre=[("dS", pro_dS)])
            t_t = None
            for j in range(8):
                t_t = transpose(P_tr[:, j, :], ln1o2[:, gc, j * 128:(j + 1) * 128],
                                ident[:], wait=([s_a, g("ptr")] if j == 0 else []),
                                inc=(("T", 1) if j == 7 else None))
            ev = evac_copy(pro_fm[:, :, gc * 128:(gc + 1) * 128], P_tr[:],
                           wait=[("T", t_t)], est=680)
            rec("ptr", ev)
            rec(("profm", gc), ev)
        for oc in range(8):
            bank = oc % 2
            wl = [g(("pbig", bank)), g(("profm", 0)), g(("profm", 1))]
            t_mm = proj_fm("wk", P_big[:, bank, 0:256], oc,
                           lambda kc, f=pro_fm: f[:, kc, 0:256], wl)
            ev1 = evac_bias(K0f[:, oc, :], P_big[:, bank, 0:64],
                            bkf[:, oc:oc + 1], wait=[("T", t_mm)], est=300)
            ev2 = evac_bias(K127f[:, oc, :], P_big[:, bank, 64:128],
                            bkf[:, oc:oc + 1], wait=[("T", t_mm)], est=300)
            ev3 = evac_bias(gkf[:, oc, :], P_big[:, bank, 128:256],
                            bkf[:, oc:oc + 1], wait=[("T", t_mm)], est=300)
            rec(("pbig", bank), ev1, ev2, ev3)
        for oc in range(8):
            bank = oc % 2
            t_mm = proj_fm("wq", P_big[:, bank, 0:128], oc,
                           lambda kc, f=pro_fm: f[:, kc, 128:256],
                           [g(("pbig", bank))])
            ev = evac_bias(qgf[:, oc, :], P_big[:, bank, 0:128],
                           bqf[:, oc:oc + 1], wait=[("T", t_mm)], est=300)
            rec(("pbig", bank), ev)
        for gc in range(2):
            for half in range(2):
                bank = half
                wl = [g(("pbig", bank))]
                for kc in range(8):
                    mm(P_big[:, bank, :], pro_fm[:, kc, gc * 128:(gc + 1) * 128],
                       Wf["wv"][:, kc, half * 512:(half + 1) * 512],
                       kc == 0, False, wait=(wl if kc == 0 else []), est=230)
                t_mm = mm(P_big[:, bank, :], ones_b[:],
                          bvrow[:, half * 512:(half + 1) * 512], False, True,
                          inc=("T", 1), est=230)
                src = P_big[:, bank, :].rearrange("p (h v) -> p h v", v=64)
                if gc == 0:
                    ev1 = evac_copy(V0x[0:64, half * 8:(half + 1) * 8, 0:64],
                                    src[0:64], wait=[("T", t_mm)], est=300)
                    ev2 = evac_copy(V127x[64:128, half * 8:(half + 1) * 8, 0:64],
                                    src[64:128], wait=[("T", t_mm)], est=300)
                    rec(("pbig", bank), ev1, ev2)
                else:
                    ev = evac_copy(gvx[:, half * 8:(half + 1) * 8, 0:64], src[:],
                                   wait=[("T", t_mm)], est=600)
                    rec(("pbig", bank), ev)
        pro_done = [("T", b.v("T")), ("V", b.v("V")), ("S", b.v("S"))]
        rec(("pro",), *pro_done)
        rec(("pt", 0), ("V", b.v("V")))
        rec(("pt", 1), ("V", b.v("V")))
        rec(("xfree_pro",), ("T", b.v("T")), ("S", b.v("S")), ("V", b.v("V")))
        rec(("ln1o_free",), ("T", b.v("T")))
        rec(("paux_free",), *pro_done)
        rec(("xfm_free",), ("T", b.v("T")))
        for ch in range(2):
            rec(("sq_free", ch), ("S", b.v("S")), ("T", b.v("T")))
            rec(("lnat_free", ch), ("T", b.v("T")))
            rec(("xp_free", ch), ("T", b.v("T")), ("S", b.v("S")))

        def load_x(i):
            if i >= NT:
                return
            slot = i % 2
            w = []
            if i < 2:
                if slot == 0:
                    w.append(g(("xfree_pro",)))
            else:
                w.append(g(("xfree", slot)))
            dv = dma("sync", f"xl{slot}", xbuf[slot][:],
                     x_d[i * TILE:(i + 1) * TILE, :].rearrange(
                         "(c p) d -> p c d", p=128), wait=w)
            rec(("xload", i), (f"xl{slot}", dv))

        def stage_A_gen(i):
            par, slot = i % 2, i % 3
            wl = [g(("xload", i)), g(("sma_free",))]
            if i < 2:
                wl.append(g(("pro",)))
            vop(lambda e: e.reduce_sum(SMA[:, 0:2], xbuf[par][:, 0:2, :],
                                       axis=AX.X, negate=True),
                wait=wl, est=(140 + 2 * D) / 0.96)
            sop(lambda e: e.activation(SQ_t[:], xbuf[par][:, 0, :], ACT.Square,
                                       accum_out=SMA[:, 2:3]),
                wait=wl, est=(200 + D) / 1.2)
            sdrain()
            s_sq = sop(lambda e: e.activation(SQ_t[:], xbuf[par][:, 1, :],
                                              ACT.Square, accum_out=SMA[:, 3:4]),
                       inc=("S", 1), est=(200 + D) / 1.2)
            vdrain()
            vop(lambda e: e.tensor_scalar(SMA[:, 0:2], SMA[:, 0:2], 1.0 / D,
                                          None, op0=OP.mult), est=140)
            vdrain()
            vop(lambda e: e.tensor_tensor(SMA[:, 4:6], SMA[:, 0:2], SMA[:, 0:2],
                                          op=OP.mult), est=140)
            vdrain()
            v_var = vop(lambda e: e.scalar_tensor_tensor(
                SMA[:, 6:8], SMA[:, 2:4], 1.0 / D, SMA[:, 4:6],
                op0=OP.mult, op1=OP.subtract),
                wait=[("S", s_sq)], inc=("V", 1), est=140)
            sop(lambda e: e.activation(SMA[:, 4:6], SMA[:, 6:8], ACT.Ln,
                                       bias=EPS), wait=[("V", v_var)], est=350)
            sdrain()
            s_r = sop(lambda e: e.activation(SMA[:, 6:8], SMA[:, 4:6], ACT.Exp,
                                             scale=-0.5), inc=("S", 1), est=350)
            v_b = vop(lambda e: e.tensor_tensor(SMA[:, 0:2], SMA[:, 0:2],
                                                SMA[:, 6:8], op=OP.mult),
                      wait=[("S", s_r)], inc=("V", 1), est=140)
            s_ap = []
            for ch in range(2):
                s_ap.append(sop(
                    lambda e, cc=ch: e.activation(
                        ln1o2[:, cc, :], xbuf[par][:, cc, :], ACT.Identity,
                        scale=SMA[:, 6 + cc:7 + cc], bias=SMA[:, cc:cc + 1]),
                    wait=([("V", v_b), g(("ln1o_free",))] if ch == 0 else []),
                    inc=("S", 1), est=(200 + D) / 1.2))
            rec(("sma_free",), ("S", s_ap[1]), ("V", v_b))
            yield
            for ch in range(2):
                t_t = None
                for j in range(8):
                    t_t = transpose(P_tr[:, j, :],
                                    ln1o2[:, ch, j * 128:(j + 1) * 128],
                                    ident[:],
                                    wait=([("S", s_ap[ch]), g("ptr")]
                                          if j == 0 else []),
                                    inc=(("T", 1) if j == 7 else None))
                ev = evac_copy(xfm[par][:, :, ch * 128:(ch + 1) * 128], P_tr[:],
                               wait=[("T", t_t)], est=680)
                rec("ptr", ev)
                rec(("xfm_ready", par, ch), ev)
                yield
            rec(("ln1o_free",), ("T", b.v("T")))
            for (w, kind) in (("wq", "q"), ("wk", "k")):
                for oc in range(8):
                    bank = oc % 2
                    wl = [g(("pbig", bank)), g(("xfm_ready", par, 0)),
                          g(("xfm_ready", par, 1))]
                    t_mm = proj_fm(w, P_big[:, bank, 0:TILE], oc,
                                   lambda kc, f=xfm[par]: f[:, kc, :], wl)
                    if kind == "q":
                        ev = evac_bias(qf[par][:, oc, :], P_big[:, bank, 0:TILE],
                                       bqf[:, oc:oc + 1], wait=[("T", t_mm)],
                                       est=(200 + TILE) / 1.2)
                    else:
                        ev = evac_bias(Kr[slot][:, oc, 64:320],
                                       P_big[:, bank, 0:TILE],
                                       bkf[:, oc:oc + 1], wait=[("T", t_mm)],
                                       est=(200 + TILE) / 1.2)
                    rec(("pbig", bank), ev)
                    if oc % 2 == 1:
                        yield
                rec(("qready", i) if kind == "q" else ("kready", i),
                    ("S", b.v("S")), ("V", b.v("V")))
            src_c1 = (K127f[:, :, :] if i == 0
                      else Kr[(i - 1) % 3][:, :, 256:320])
            cw = [g(("kr_a_free", slot))]
            cw.append(g(("pro",)) if i == 0 else g(("kready", i - 1)))
            c1 = vop(lambda e, s=src_c1: e.tensor_copy(Kr[slot][:, :, 0:64], s),
                     wait=cw, inc=("V", 1), est=250)
            rec(("kr_c1", i), ("V", c1))
            for ch in range(2):
                for half in range(2):
                    bank = half
                    wl = [g(("pbig", bank)), g(("xfm_ready", par, ch))]
                    for kc in range(8):
                        mm(P_big[:, bank, :],
                           xfm[par][:, kc, ch * 128:(ch + 1) * 128],
                           Wf["wv"][:, kc, half * 512:(half + 1) * 512],
                           kc == 0, False, wait=(wl if kc == 0 else []), est=230)
                    t_mm = mm(P_big[:, bank, :], ones_b[:],
                              bvrow[:, half * 512:(half + 1) * 512], False, True,
                              inc=("T", 1), est=230)
                    src = P_big[:, bank, :].rearrange("p (h v) -> p h v", v=64)
                    ev = evac_copy(Vr[slot][:, ch, half * 8:(half + 1) * 8, 0:64],
                                   src[:], wait=[("T", t_mm)], est=680)
                    rec(("pbig", bank), ev)
                    yield
            rec(("xfm_free",), ("T", b.v("T")))
            rec(("vready", i), ("S", b.v("S")), ("V", b.v("V")))

        def vblk(b0):
            if b0 < 0:
                return V127x[64:128, :, :]
            if b0 >= NB:
                return V0x[0:64, :, :]
            tb, vp = (b0 // 4) % 3, (b0 % 2) * 64
            return Vr[tb][vp:vp + 64, (b0 % 4) // 2, :, :]

        def build_U(j, wl):
            s = j % 3
            vop(lambda e, src=vblk(2 * j - 1): e.tensor_copy(
                VEU[s][0:64, :, :], src),
                wait=list(wl) + [g(("veu_free", s))], est=330)
            v2 = vop(lambda e, src=vblk(2 * j): e.tensor_copy(
                VEU[s][64:128, :, :], src), inc=("V", 1), est=330)
            rec(("veu", j), ("V", v2))

        def waves_ch(i, ch):
            par, slot = i % 2, i % 3
            cg, off = 2 * i + ch, 128 * ch
            VEa, VEb = VEU[cg % 3], VEU[(cg + 1) % 3]
            NWV = H // 2
            n_extra = 4 if ch == 1 else 0
            se_rec, evn_rec = {}, {}
            for w in range(NWV + 2):
                if w >= 2:
                    wb = w - 2
                    sb_ = wb % 2
                    t_pv = None
                    for p in range(2):
                        h = 2 * wb + p
                        wl = [("S", se_rec[wb][p])]
                        if p == 0:
                            wl.append(g(("paux_free",)))
                            if wb == 0:
                                wl += [g(("veu", cg)), g(("veu", cg + 1))]
                        mm(P_aux[:, 2 + p, 0:65], PT[:, sb_, p, 2, :],
                           gvx[:, h, :], True, True, wait=wl, est=120,
                           acc=accw(P_aux[:, 2 + p, 0:65]), skip=True)
                        mm(P_aux[:, p, 0:65], PT[:, sb_, p, 0, :],
                           VEa[:, h, :], True, False, est=120,
                           acc=accw(P_aux[:, p, 0:65]), skip=True)
                        t_pv = mm(P_aux[:, p, 0:65], PT[:, sb_, p, 1, :],
                                  VEb[:, h, :], False, True,
                                  inc=(("T", 1) if p == 1 else None), est=120,
                                  acc=accw(P_aux[:, p, 0:65]), skip=True)
                    wl = [("T", t_pv)]
                    if wb == 0:
                        wl.append(g(("ctxlc_free",)))
                    vop(lambda e, w_=wb: e.tensor_copy(
                        CTXlc[:, 2 * w_:2 * w_ + 2, 0, :],
                        P_aux[:, 0:2, 0:65]), wait=wl, est=280,
                        acc=accr(P_aux[:, 0:2, 0:65]))
                    v_ev = vop(lambda e, w_=wb: e.tensor_copy(
                        CTXlc[:, 2 * w_:2 * w_ + 2, 1, :],
                        P_aux[:, 2:4, 0:65]), inc=("V", 1), est=280,
                        acc=accr(P_aux[:, 2:4, 0:65]))
                    rec(("paux_free",), ("V", v_ev))
                if 1 <= w and w - 1 < NWV:
                    wm = w - 1
                    s = wm % 2
                    Pw = PS[s]
                    t_n = transpose(Pw[0:2, 0, 3, :], SMW[:, 2 * s:2 * s + 2],
                                    identf[:], wait=[g(("vmx", s))],
                                    inc=("T", 1))
                    rec(("smw_free", s), ("T", t_n))
                    evn_rec[wm] = vop(lambda e, s_=s, P_=Pw: e.tensor_copy(
                        nfm2[0:2, s_, :, :],
                        P_[0:2, 0, 3:4, :].broadcast_to([2, 3, 128])),
                        wait=[("T", t_n), g(("nfm_free", s))], inc=("V", 1),
                        est=450, acc=accr(Pw[0:2, 0, 3, :]))
                if w < NWV:
                    s = w % 2
                    Pw = PS[s]
                    t_qm = None
                    for p in range(2):
                        h = 2 * w + p
                        part, hc = (h % 2) * 64, h // 2
                        q_h = qf[par][part:part + 64, hc, off:off + 128]
                        wl = []
                        if p == 0:
                            wl.append(g(("psc_free", s)))
                            if w == 0:
                                wl += [g(("qready", i)), g(("pro",))]
                                if ch == 0:
                                    wl += [g(("kready", i)), g(("kr_c1", i))]
                                else:
                                    wl += [g(("kr_c2", i))]
                        mm(Pw[:, p, 0:2, :].rearrange("p a q -> p (a q)"), q_h,
                           Kr[slot][part:part + 64, hc, off:off + 256],
                           True, True, wait=wl, est=180,
                           acc=accw(Pw[:, p, 0:2, :]), skip=True)
                        t_qm = mm(Pw[:, p, 2, :], q_h,
                                  gkf[part:part + 64, hc, :], True, True,
                                  inc=(("T", 1) if p == 1 else None), est=120,
                                  acc=accw(Pw[:, p, 2, :]), skip=True)
                    v_mx = vop(lambda e, s_=s, P_=Pw: e.reduce_max(
                        SMW[:, 2 * s_:2 * s_ + 2],
                        P_[:, 0:2, 0:3, :].rearrange("p a b q -> p a (b q)"),
                        axis=AX.X, negate=True),
                        wait=[("T", t_qm), g(("smw_free", s))], inc=("V", 1),
                        est=(140 + 768) / 0.96, acc=accr(Pw[:, 0:2, 0:3, :]))
                    rec(("vmx", s), ("V", v_mx))
                if 1 <= w and w - 1 < NWV:
                    wm = w - 1
                    s = wm % 2
                    Pw = PS[s]
                    t_r1 = {}
                    for p in range(2):
                        h = 2 * wm + p
                        part, hc = (h % 2) * 64, h // 2
                        q_mv = qf[par][part:part + 64, hc, off:off + 128]
                        mm(Pw[:, p, 0:3, :].rearrange("p a q -> p (a q)"),
                           SEL[:, p, :],
                           nfm2[0:4, s, :, :].rearrange("a b q -> a (b q)"),
                           True, False,
                           wait=([("V", evn_rec[wm]), g(("vmx", s))]
                                 if p == 0 else []),
                           est=230, acc=accw(Pw[:, p, 0:3, :]), skip=True)
                        mm(Pw[:, p, 0, :],
                           Kr[slot][part:part + 64, hc, off:off + 128], q_mv,
                           False, False, est=120,
                           acc=accw(Pw[:, p, 0, :]), skip=True)
                        mm(Pw[:, p, 1, :],
                           Kr[slot][part:part + 64, hc, off + 128:off + 256],
                           q_mv, False, False, est=120,
                           acc=accw(Pw[:, p, 1, :]), skip=True)
                        t_r1[p] = mm(Pw[:, p, 2, :], gkf[part:part + 64, hc, :],
                                     q_mv, False, True, inc=("T", 1), est=120,
                                     acc=accw(Pw[:, p, 2, :]), skip=True)
                    rec(("nfm_free", s), ("T", t_r1[1]))
                    se_p = {}
                    for p in range(2):
                        se_p[p] = sop(lambda e, s_=s, P_=Pw, pp=p: e.activation(
                            PT[:, s_, pp, 0:3, :], P_[:, pp, 0:3, :],
                            ACT.Exp, scale=SCALE),
                            wait=[("T", t_r1[p])], inc=("S", 1), est=780,
                            acc=accr(Pw[:, p, 0:3, :]))
                    se_rec[wm] = se_p
                    rec(("psc_free", s), ("S", se_p[1]))
                yield
            vdrain()
            vop(lambda e: e.reciprocal(RLD[:, :, :],
                                       CTXlc[:, :, :, 64:65]), est=250)
            vdrain()
            vop(lambda e: e.tensor_tensor(
                CTXlc[:, :, 0, 0:64], CTXlc[:, :, 0, 0:64],
                RLD[:, :, 0:1].broadcast_to([128, 16, 64]), op=OP.mult),
                est=1100)
            vop(lambda e: e.tensor_tensor(
                CTXlc[:, :, 1, 0:64], CTXlc[:, :, 1, 0:64],
                RLD[:, :, 1:2].broadcast_to([128, 16, 64]), op=OP.mult),
                est=1100)
            v_dv = vdrain(inc=("V", 1))
            p_c = b.op("gpsimd", lambda e: e.tensor_add(
                CTXt_t[:], CTXlc[:, :, 0, 0:64], CTXlc[:, :, 1, 0:64]),
                wait=[("V", v_dv), g(("ctxT_done", 0)), g(("ctxT_done", 1))],
                inc=("P", 1), est=2100)
            rec(("ctxlc_free",), ("P", p_c))
            rec(("ctx_ready", ch), ("P", p_c))
            rec(("ctx_tile", i, ch), ("P", p_c))
            if ch == 0:
                rec(("kr_a_free", slot), ("T", b.v("T")))
            else:
                rec(("kr_b_free", slot), ("T", b.v("T")))
            rec(("veu_free", cg % 3), ("T", b.v("T")))
            if ch == 1:
                for sp in range(8):
                    s_r = sp % 2
                    Pr = PS[s_r]
                    t_m = None
                    for p in range(2):
                        h = 2 * sp + p
                        pt = (h % 2) * 64
                        wl = [g(("psc_free", s_r))] if p == 0 else []
                        t_m = mm(Pr[:, p, 0:2, :].rearrange(
                            "p a q -> p (a q)"),
                            qgf[pt:pt + 64, h // 2, :],
                            Kr[slot][pt:pt + 64, h // 2, 64:320],
                            True, True, wait=wl,
                            inc=(("T", 1) if p == 1 else None),
                            est=200, acc=accw(Pr[:, p, 0:2, :]), skip=True)
                    v_red = vop(lambda e, P_=Pr, s_=sp: e.reduce_max(
                        RS8[:, 2 * s_:2 * s_ + 2],
                        P_[:, 0:2, 0:2, :].rearrange("p a b q -> p a (b q)"),
                        axis=AX.X),
                        wait=[("T", t_m)], inc=("V", 1),
                        est=(140 + 512) / 0.96,
                        acc=accr(Pr[:, 0:2, 0:2, :]))
                    rec(("psc_free", s_r), ("V", v_red))
                    if sp % 2 == 1:
                        yield
                yield from rows_part(i)

        def emit_c2_u(i):
            slot = i % 3
            src_c2 = (K0f[:, :, :] if i + 1 >= NT
                      else Kr[(i + 1) % 3][:, :, 64:128])
            cw = [g(("kr_b_free", slot))]
            cw.append(g(("pro",)) if i + 1 >= NT else g(("kready", i + 1)))
            c2 = vop(lambda e, s=src_c2: e.tensor_copy(
                Kr[slot][:, :, 320:384], s), wait=cw, inc=("V", 1), est=250)
            rec(("kr_c2", i), ("V", c2))
            build_U(2 * i + 2,
                    [g(("vready", i)),
                     g(("pro",)) if i + 1 >= NT else g(("vready", i + 1))])

        def rows_part(i):
            slot = i % 3
            vdrain()
            vop(lambda e: e.reduce_max(RS[:, 1:2], RS8[:, 0:16], axis=AX.X),
                est=140)
            vdrain()
            vop(lambda e: e.tensor_tensor(mnb[:], RS[:, 1:2], mrun[:],
                                          op=OP.max),
                wait=[g(("rs_s",))], est=140)
            vdrain()
            vop(lambda e: e.tensor_tensor(RS[:, 4:5], mrun[:], mnb[:],
                                          op=OP.subtract), est=140)
            vop(lambda e: e.tensor_scalar(RS[:, 3:4], mnb[:], -1.0,
                                          None, op0=OP.mult),
                wait=[g(("rs_rd",))], est=140)
            v_al = vdrain(inc=("V", 1))
            vop(lambda e: e.tensor_copy(mrun[:], mnb[:]), est=140)
            s_al = sop(lambda e: e.activation(RS[:, 2:3], RS[:, 4:5], ACT.Exp,
                                              scale=SCALE),
                       wait=[("V", v_al)], inc=("S", 1), est=350)
            rec(("rs_s",), ("S", s_al))
            v_rs = vop(lambda e: e.tensor_scalar(racc[:], racc[:], RS[:, 2:3],
                                                 None, op0=OP.mult),
                       wait=[("S", s_al)], inc=("V", 1), est=1200)
            t_nr = transpose(PS[0][0:1, 0, 3, :], RS[:, 3:4], identf[:],
                             wait=[("V", v_al), g(("psc_free", 0))],
                             inc=("T", 1))
            rec(("rs_rd",), ("T", t_nr))
            ev_n = sop(lambda e: e.copy(
                nfm2[0:1, 0, 0:2, :],
                PS[0][0:1, 0, 3:4, :].broadcast_to([1, 2, 128])),
                wait=[("T", t_nr), g(("nfm_free", 0))], inc=("S", 1), est=400,
                acc=accr(PS[0][0:1, 0, 3, :]))
            yield
            se_rec = {}
            v_a = None
            for w in range(H // 2 + 1):
                if w < H // 2:
                    s = w % 2
                    Pw = PS[s]
                    t_r1 = None
                    for p in range(2):
                        h = 2 * w + p
                        pt = (h % 2) * 64
                        wl = [("S", ev_n)]
                        if p == 0:
                            wl.append(g(("psc_free", s)))
                        mm(Pw[:, p, 0:2, :].rearrange("p a q -> p (a q)"),
                           ones_b[:],
                           nfm2[0:1, 0, 0:2, :].rearrange("a b q -> a (b q)"),
                           True, False, wait=wl, est=180,
                           acc=accw(Pw[:, p, 0:2, :]), skip=True)
                        mm(Pw[:, p, 0, :],
                           Kr[slot][pt:pt + 64, h // 2, 64:192],
                           qgf[pt:pt + 64, h // 2, :], False, False,
                           est=120, acc=accw(Pw[:, p, 0, :]), skip=True)
                        t_r1 = mm(Pw[:, p, 1, :],
                                  Kr[slot][pt:pt + 64, h // 2, 192:320],
                                  qgf[pt:pt + 64, h // 2, :], False, True,
                                  inc=(("T", 1) if p == 1 else None),
                                  est=120, acc=accw(Pw[:, p, 1, :]), skip=True)
                    if w >= H // 2 - 2:
                        rec(("nfm_free", 0), ("T", t_r1))
                    s_e = sop(lambda e, s_=s, P_=Pw: e.activation(
                        PT[:, s_, 0:2, 0:2, :], P_[:, 0:2, 0:2, :], ACT.Exp,
                        scale=SCALE), wait=[("T", t_r1)], inc=("S", 1),
                        est=600, acc=accr(Pw[:, 0:2, 0:2, :]))
                    se_rec[w] = s_e
                    rec(("psc_free", s), ("S", s_e))
                if w >= 1:
                    wb = w - 1
                    sb_ = wb % 2
                    t_pv = None
                    for p in range(2):
                        h = 2 * wb + p
                        wl = []
                        if p == 0:
                            wl += [("S", se_rec[wb]), g(("paux_free",))]
                        mm(P_aux[:, p, 0:65], PT[:, sb_, p, 0, :],
                           Vr[slot][:, 0, h, :], True, False, wait=wl,
                           est=120, acc=accw(P_aux[:, p, 0:65]), skip=True)
                        t_pv = mm(P_aux[:, p, 0:65], PT[:, sb_, p, 1, :],
                                  Vr[slot][:, 1, h, :], False, True,
                                  inc=(("T", 1) if p == 1 else None), est=120,
                                  acc=accw(P_aux[:, p, 0:65]), skip=True)
                    v_a = vop(lambda e, w_=wb: e.tensor_tensor(
                        racc[:, 2 * w_:2 * w_ + 2, :],
                        racc[:, 2 * w_:2 * w_ + 2, :],
                        P_aux[:, 0:2, 0:65], op=OP.add),
                        wait=[("T", t_pv)] +
                             ([("V", v_rs)] if wb == 0 else []),
                        inc=("V", 1), est=400, acc=accr(P_aux[:, 0:2, 0:65]))
                    rec(("paux_free",), ("V", v_a))
                yield
            rec("racc_rd", ("V", v_a))

        def stage_C_gen(i, ch, epilogue=False, out=None):
            par = 0 if epilogue else i % 2
            ctx_flat = CTXt[ch][:].rearrange("p h v -> p (h v)")
            t_t = None
            for j in range(8):
                t_t = transpose(P_tr[:, j, :], ctx_flat[:, j * 128:(j + 1) * 128],
                                ident[:],
                                wait=([g(("ctx_ready", ch)), g("ptr")]
                                      if j == 0 else []),
                                inc=(("T", 1) if j == 7 else None))
            ev = evac_copy(ctxf[ch][:], P_tr[:], wait=[("T", t_t)], est=680)
            rec("ptr", ev)
            rec(("ctxT_done", ch), ("T", b.v("T")))
            yield
            for half in range(2):
                bank = half
                wl = [ev, g(("pbig", bank))]
                for kc in range(8):
                    mm(P_big[:, bank, :], ctxf[ch][:, kc, :],
                       Wf["wo"][:, kc, half * 512:(half + 1) * 512],
                       kc == 0, False, wait=(wl if kc == 0 else []), est=230)
                t_wo = mm(P_big[:, bank, :], ones_b[:],
                          borow[:, half * 512:(half + 1) * 512], False, True,
                          inc=(("T", 1) if half == 1 else None), est=230)
            yield
            ev_o = sop(lambda e, cc=ch: e.activation(
                ln2s[cc][:].rearrange("p (a c) -> p a c", a=2),
                P_big[:, 0:2, :], ACT.Identity, accum_out=SM2[ch][:, 0:1]),
                wait=[("T", t_wo), g(("ln2s_free", 0)), g(("ln2s_free", 1)),
                      g(("sm2", ch))],
                inc=("S", 1), est=(200 + D) / 1.2,
                acc=accr(P_big[:, 0, :], P_big[:, 1, :]))
            rec(("pbig", 0), ("S", ev_o))
            rec(("pbig", 1), ("S", ev_o))
            s_a = ln_sbuf(ln2s[ch][:], SM2[ch], lnat[ch][:],
                          wait_pre=[("S", ev_o),
                                    g(("sq_free", ch)), g(("lnat_free", ch))],
                          have_sum=True)
            rec(("sq_free", ch), ("S", b.v("S")))
            yield
            xin = xg if epilogue else xbuf[par][:, ch, :]
            vop(lambda e, cc=ch, xi=xin: e.tensor_add(xp[cc][:], xi, lnat[cc][:]),
                wait=[s_a, g(("xp_free", 0)), g(("xp_free", 1)),
                      g(("sm2", ch))] +
                     ([g(("xg_ld",))] if epilogue else [g(("xload", i))]),
                est=(140 + D) / 0.96)
            v_xp = vdrain(inc=("V", 1))
            s_2 = ln_sbuf(xp[ch][:], SM2[ch], ln2s[ch][:],
                          wait_pre=[("V", v_xp), g(("ln2s_free", ch))],
                          scale_extra=sc_bc[:, 2:3])
            rec(("pbig", 0), s_a, s_2, ("V", v_xp))
            rec(("pbig", 1), s_a, s_2, ("V", v_xp))
            rec(("sq_free", ch), ("S", b.v("S")))
            rec(("lnat_free", ch), ("V", v_xp))
            yield
            pb1 = P_big[:, 1, :].rearrange("p (a c) -> p a c", a=4)
            t_a = None
            for j in range(4):
                t_a = transpose(pb1[:, j, :],
                                ln2s[ch][:, j * 128:(j + 1) * 128],
                                identf[:],
                                wait=([s_2, g(("pbig", 1))] if j == 0 else []),
                                inc=(("T", 1) if j == 3 else None))
            ev_a = evac(lambda e, cc=ch, s=pb1: e.tensor_scalar(
                xq8[cc][:, 0:4, :], s[:], CLIP_HALF, -CLIP_HALF,
                op0=OP.min, op1=OP.max), None, wait=[("T", t_a)], est=700,
                acc=accr(pb1[:]))
            rec(("pbig", 1), ev_a)
            pb0 = P_big[:, 0, :].rearrange("p (a c) -> p a c", a=4)
            t_b = None
            for j in range(4):
                t_b = transpose(pb0[:, j, :],
                                ln2s[ch][:, (4 + j) * 128:(5 + j) * 128],
                                identf[:],
                                wait=([g(("pbig", 0))] if j == 0 else []),
                                inc=(("T", 1) if j == 3 else None))
            ev_b = evac(lambda e, cc=ch, s=pb0: e.tensor_scalar(
                xq8[cc][:, 4:8, :], s[:], CLIP_HALF, -CLIP_HALF,
                op0=OP.min, op1=OP.max), None, wait=[("T", t_b)], est=700,
                acc=accr(pb0[:]))
            rec(("pbig", 0), ev_b)
            rec(("ln2s_free", ch), ("T", b.v("T")))
            yield
            for half in range(2):
                bank = half
                wl = [ev_a, ev_b, g(("pbig", bank))]
                for kp in range(4):
                    b.op("tensor", lambda e, cc=ch, k_=kp, hh=half, bb=bank:
                         e.matmul(P_big[:, bb, :], xq8[cc][:, 2 * k_:2 * k_ + 2, :],
                                  Wf8[:, 2 * k_:2 * k_ + 2,
                                      hh * 512:(hh + 1) * 512],
                                  start=(k_ == 0), stop=False,
                                  perf_mode=mybir.MatmulPerfMode.DoubleRow),
                         wait=(wl if kp == 0 else []), est=300,
                         acc=accw(P_big[:, bank, :]))
                t_ff = mm(P_big[:, bank, :], ones_b[:],
                          bfrow[:, half * 512:(half + 1) * 512], False, True,
                          inc=(("T", 1) if half == 1 else None), est=230)
            yield
            ev_f = sop(lambda e, cc=ch: e.activation(
                ln2s[cc][:].rearrange("p (a c) -> p a c", a=2),
                P_big[:, 0:2, :], ACT.Identity, scale=sc_bc[:, 0:1],
                accum_out=SM2[ch][:, 0:1]),
                wait=[("T", t_ff), g(("ln2s_free", ch)), g(("sm2", ch))],
                inc=("S", 1), est=(200 + D) / 1.2,
                acc=accr(P_big[:, 0, :], P_big[:, 1, :]))
            rec(("pbig", 0), ("S", ev_f))
            rec(("pbig", 1), ("S", ev_f))
            s_f = ln_sbuf(ln2s[ch][:], SM2[ch], lnat[ch][:],
                          wait_pre=[("S", ev_f), g(("lnat_free", ch)),
                                    g(("sq_free", ch))], have_sum=True)
            rec(("sq_free", ch), ("S", b.v("S")))
            rec(("sm2", ch), ("S", b.v("S")), ("V", b.v("V")))
            if epilogue:
                vop(lambda e, cc=ch: e.tensor_add(xg, xp[cc][:], lnat[cc][:]),
                    wait=[s_f], est=(140 + D) / 0.96)
                v_o = vdrain(inc=("V", 1))
                rec(("lnat_free", ch), ("V", v_o))
                rec(("xp_free", ch), ("V", v_o))
                rec(("pbig", 0), s_f, ("V", b.v("V")))
                rec(("pbig", 1), s_f, ("V", b.v("V")))
                if out is not None:
                    out["v_o"] = ("V", v_o)
                return
            vop(lambda e, cc=ch, pp=par: e.tensor_add(
                xbuf[pp][:, cc, :], xp[cc][:], lnat[cc][:]),
                wait=[s_f], est=(140 + D) / 0.96)
            v_o = vdrain(inc=("V", 1))
            rec(("lnat_free", ch), ("V", v_o))
            rec(("xp_free", ch), ("V", v_o))
            rec(("pbig", 0), s_f, ("V", b.v("V")))
            rec(("pbig", 1), s_f, ("V", b.v("V")))
            cg = 2 * i + ch
            dv = dma("gpsimd", f"st{par}", out_d[cg * 128:(cg + 1) * 128, :],
                     xbuf[par][:, ch, :], wait=[("V", v_o)])
            if ch == 1:
                rec(("xfree", i % 2), (f"st{par}", dv))

        def drain(gen):
            for _ in gen:
                pass

        def zip2(ga, gb):
            live = [x for x in (ga, gb) if x is not None]
            while live:
                for g_ in list(live):
                    try:
                        next(g_)
                    except StopIteration:
                        live.remove(g_)

        load_x(0)
        load_x(1)
        drain(stage_A_gen(0))
        for i in range(NT):
            if i == 0:
                build_U(0, [g(("vready", 0)), g(("pro",))])
            build_U(2 * i + 1, [g(("vready", i))])
            zip2(waves_ch(i, 0),
                 stage_A_gen(i + 1) if i + 1 < NT else None)
            emit_c2_u(i)
            def chain_c(ii):
                yield from stage_C_gen(ii, 0)
                while ("ctx_tile", ii, 1) not in b.G:
                    yield
                yield from stage_C_gen(ii, 1)
            zip2(waves_ch(i, 1), chain_c(i))
            load_x(i + 2)

        dma("sync", "dS", xg[0:64, :], x_d[gi0 * 64:(gi0 + 1) * 64, :],
            wait=[g(("xfree", NT % 2))])
        dxg = dma("sync", "dS", xg[64:128, :], x_d[gi1 * 64:(gi1 + 1) * 64, :])
        rec(("xg_ld",), ("dS", dxg))
        vdrain()
        vop(lambda e: e.reciprocal(RL16[:], racc[:, :, 64]),
            wait=[g("racc_rd"), g(("ctxT_done", 0)), g(("ctxT_done", 1)),
                  g(("ctxlc_free",))], est=200)
        vdrain()
        vop(lambda e: e.tensor_tensor(
            CTXt[0][:], racc[:, :, 0:64],
            RL16[:].rearrange("p (h o) -> p h o", o=1).broadcast_to([128, H, 64]),
            op=OP.mult), est=1200)
        v_d = vdrain(inc=("V", 1))
        rec(("ctx_ready", 0), ("V", v_d))
        hold = {}
        drain(stage_C_gen(None, 0, epilogue=True, out=hold))
        v_o = hold["v_o"]
        st_f0, st_f1 = b.v("st0"), b.v("st1")
        dma("gpsimd", "st0", out_d[gi0 * 64:(gi0 + 1) * 64, :], xg[0:64, :],
            wait=[v_o, ("st0", st_f0), ("st1", st_f1)])
        dma("gpsimd", "st1", out_d[gi1 * 64:(gi1 + 1) * 64, :], xg[64:128, :])
        b.op("gpsimd", lambda e: e.nop(),
             wait=[("st0", b.v("st0")), ("st1", b.v("st1"))])

        import os as _osc
        if _osc.environ.get("KCHECK", "1") == "1":
            bad = b.check_banks()
            if bad:
                for x in bad[:20]:
                    print("PSUM COLLISION:", x)
                raise RuntimeError(f"{len(bad)} psum bank collisions")

        with ExitStack() as sctx:
            semh = {name: sctx.enter_context(nc.semaphore(name))
                    for name in b.cnt}
            with nc.Block() as block:
                def mk(ename):
                    def run(eng):
                        for fn, waits, inc in b.ops[ename]:
                            for (s, vv) in waits:
                                eng.wait_ge(semh[s], vv)
                            ins = fn(eng)
                            if inc is not None:
                                ins.then_inc(semh[inc[0]], inc[1])
                    return run
                block.sync(mk("sync"))
                block.gpsimd(mk("gpsimd"))
                block.tensor(mk("tensor"))
                block.vector(mk("vector"))
                block.scalar(mk("scalar"))
    return nc


def kernel(L0_attn_x, L0_attn_Wq, L0_attn_bq, L0_attn_Wk, L0_attn_bk,
           L0_attn_Wv, L0_attn_bv, L0_attn_Wo, L0_attn_bo,
           L0_attn_global_indices, L0_ffn_weight_fp8, L0_ffn_bias,
           L0_ffn_input_scale, L0_ffn_weight_scale):
    x = np.ascontiguousarray(np.asarray(L0_attn_x), dtype=np.float32)
    Bn, NTOK, Dm = x.shape
    gi = np.asarray(L0_attn_global_indices).astype(np.int64)
    nc = build_nc(NTOK, int(gi[0]), int(gi[1]))
    base = {
        "wq": np.ascontiguousarray(L0_attn_Wq, dtype=np.float32),
        "wk": np.ascontiguousarray(L0_attn_Wk, dtype=np.float32),
        "wv": np.ascontiguousarray(L0_attn_Wv, dtype=np.float32),
        "wo": np.ascontiguousarray(L0_attn_Wo, dtype=np.float32),
        "wf": np.ascontiguousarray(L0_ffn_weight_fp8, dtype=np.float32),
        "bq": np.ascontiguousarray(L0_attn_bq, dtype=np.float32).reshape(1, Dm),
        "bk": np.ascontiguousarray(L0_attn_bk, dtype=np.float32).reshape(1, Dm),
        "bv": np.ascontiguousarray(L0_attn_bv, dtype=np.float32).reshape(1, Dm),
        "bo": np.ascontiguousarray(L0_attn_bo, dtype=np.float32).reshape(1, Dm),
        "bf": np.ascontiguousarray(L0_ffn_bias, dtype=np.float32).reshape(1, Dm),
        "sin": np.ascontiguousarray(L0_ffn_input_scale,
                                    dtype=np.float32).reshape(1, 1),
        "sw": np.ascontiguousarray(L0_ffn_weight_scale,
                                   dtype=np.float32).reshape(1, 1),
    }
    in_maps = [dict(base, x=np.ascontiguousarray(x[c])) for c in range(Bn)]
    res = run_bass_kernel_spmd(nc, in_maps, core_ids=list(range(Bn)))
    return np.stack([np.asarray(r["out"]) for r in res.results], axis=0)
